# version 10
# baseline (speedup 1.0000x reference)
import numpy as np
from contextlib import ExitStack

import concourse.bass as bass
import concourse.mybir as mybir
from concourse.bass_utils import run_bass_kernel_spmd

F32 = mybir.dt.float32
F16 = mybir.dt.float16
U8 = mybir.dt.uint8
AF = mybir.ActivationFunctionType
OP = mybir.AluOpType
AX = mybir.AxisListType

S, D, H, DH, CL = 2048, 1024, 16, 64, 512
NPAIR, NSQ, SQW, NCH = 8, 4, 512, 16
EPS = 1e-5
INV_D = 1.0 / D
SCALE = 0.125


class _Op:
    __slots__ = ("eng", "fn", "deps", "idx", "dma", "need", "val", "desc")


class Prog:
    ENGS = ("sync", "tensor", "vector", "scalar", "gpsimd")

    def __init__(self):
        self.ops = []
        self.ndma = {}

    def op(self, eng, fn, deps=(), dma=False, n=1, desc=""):
        o = _Op()
        o.eng, o.fn, o.dma, o.desc = eng, fn, dma, desc
        o.idx = len(self.ops)
        o.need, o.val = False, None
        o.deps = []
        for d in deps:
            if d is None:
                continue
            if not isinstance(d, tuple):
                d = (d, False)
            assert d[0].idx < o.idx, f"forward dep at {desc}"
            o.deps.append(d)
        if dma:
            self.ndma[o.idx] = n
        self.ops.append(o)
        return o

    def emit(self, blk, sems):
        for o in self.ops:
            if o.dma:
                o.need = True
            for d, hard in o.deps:
                if d.eng != o.eng or d.dma or o.dma or hard:
                    d.need = True
        cnt = {e: 0 for e in self.ENGS}
        for o in self.ops:
            if o.need:
                cnt[o.eng] += 16 * self.ndma[o.idx] if o.dma else 1
                o.val = cnt[o.eng]
        by_eng = {e: [o for o in self.ops if o.eng == e] for e in self.ENGS}

        def emit_engine(ename, h):
            guar = {e: 0 for e in self.ENGS}
            for o in by_eng[ename]:
                for d, hard in o.deps:
                    if not d.need:
                        continue
                    if d.eng == ename and not (d.dma or o.dma or hard):
                        continue
                    if d.val > guar[d.eng]:
                        h.wait_ge(sems[d.eng].sem, d.val)
                        guar[d.eng] = d.val
                r = o.fn(h)
                if o.need:
                    ct = sems[o.eng]
                    if o.dma:
                        instrs = r if isinstance(r, (list, tuple)) else [r]
                        assert len(instrs) == self.ndma[o.idx], \
                            f"{o.desc}: {len(instrs)} != {self.ndma[o.idx]}"
                        for ins in instrs:
                            ct.inc(ins, 16)
                    else:
                        ins = r[-1] if isinstance(r, (list, tuple)) else r
                        ct.inc(ins, 1)
                    assert ct.n == o.val, f"{o.desc}: sem val mismatch"

        for ename in self.ENGS:
            getattr(blk, ename)(lambda h, _e=ename: emit_engine(_e, h))




def _dma_gather(q, out_ap, ncols, dram_ap, **kw):
    c = dram_ap.shape[0] // 128
    return q.dma_start(
        out=out_ap.rearrange("p (c n) -> p c n", c=c),
        in_=dram_ap.rearrange("(c p) n -> p c n", p=128), **kw)

class Ct:
    __slots__ = ("sem", "n")

    def __init__(self, sem):
        self.sem = sem
        self.n = 0

    def inc(self, instr, k):
        instr.then_inc(self.sem, k)
        self.n += k


def build_program(tap=None):
    nc = bass.Bass("TRN2", target_bir_lowering=False)

    d_in = {}
    for name, shape in [
            ("x", [S, D]), ("wdqT", [D, CL]), ("wdkvT", [D, CL]),
            ("wuqT", [CL, D]), ("wukT", [CL, D]), ("wuvT", [CL, D]),
            ("woutT", [D, D]), ("wrT", [D, D]), ("wkT", [D, D]),
            ("wvT", [D, D]), ("gamma", [D, 1]), ("beta", [D, 1]),
            ("mur", [D, 1]), ("muk", [D, 1]), ("tc", [128, S]),
            ("tsn", [128, S]), ("id32", [128, 128]), ("perm32", [128, 128]),
            ("e1", [128, 128]), ("onesr", [1, 128]), ("epsv", [128, 1])]:
        d_in[name] = nc.dram_tensor(name, shape, F32, kind="ExternalInput").ap()
    y_d = nc.dram_tensor("y", [S, D], F32, kind="ExternalOutput").ap()

    scr = {}
    for name, shape in [
            ("cqh_s", [CL, S]), ("cql_s", [CL, S]),
            ("ckvh_s", [CL, S]), ("ckvl_s", [CL, S]),
            ("wuqh_s", [CL, D]), ("wuql_s", [CL, D]),
            ("wouth_s", [D, D]), ("woutl_s", [D, D]),
            ("wrh_s", [D, D]), ("wrl_s", [D, D]),
            ("wkh_s", [D, D]), ("wkl_s", [D, D]),
            ("wvh_s", [D, D]), ("wvl_s", [D, D]),
            ("anh_s", [D, S]), ("anl_s", [D, S])]:
        scr[name] = nc.dram_tensor(name, shape, F16).ap()

    with ExitStack() as ST:
        sems = {e: Ct(ST.enter_context(nc.semaphore(f"sm_{e}")))
                for e in Prog.ENGS}
        PS = [ST.enter_context(nc.psum_tensor(f"p{i}", [128, 512], F32))
              for i in range(8)]

        TOT = 196352
        backing = ST.enter_context(nc.sbuf_tensor("backing", [128, TOT], U8))
        base_addr = nc.lookup_mloc(backing).addr

        def place(name, shape, dtype, off):
            assert off % 64 == 0, name
            nbytes = int(np.prod(shape[1:])) * mybir.dt.size(dtype)
            assert off + nbytes <= TOT, f"{name} overflow"
            h = nc.alloc_sbuf_tensor_at(name, list(shape), dtype,
                                        offset=base_addr + off)
            return h.ap()

        A0 = 19200
        B0 = A0 + 65536
        C0 = B0 + 66560
        assert C0 + 45056 <= TOT

        tc_t = place("tc_t", [128, S], F32, 0)
        ts_t = place("ts_t", [128, S], F32, 8192)
        id32_t = place("id32_t", [128, 128], F32, 16384)
        perm_t = place("perm_t", [128, 128], F32, 16896)
        e1_t = place("e1_t", [128, 128], F32, 17408)
        onesr_t = place("onesr_t", [1, 128], F32, 17920)
        gam_t = place("gam_t", [128, 8], F32, 18432)
        bet_t = place("bet_t", [128, 8], F32, 18496)
        beth_t = place("beth_t", [128, 8], F16, 18560)
        betl_t = place("betl_t", [128, 8], F16, 18624)
        mur_t = place("mur_t", [128, 8], F32, 18688)
        muk_t = place("muk_t", [128, 8], F32, 18752)
        bq_t = place("bq_t", [128, 4], F32, 18816)
        bkv_t = place("bkv_t", [128, 4], F32, 18880)
        hstat = place("hstat", [128, 16], F32, 18944)
        stash_t = place("stash_t", [128, 8], F32, 19008)
        yst_t = place("yst_t", [128, 8], F32, 19072)
        eps_t = place("eps_t", [128, 1], F32, 19136)

        hh_fm = place("hh_fm", [128, 8 * S], F16, A0)
        hl_fm = place("hl_fm", [128, 8 * S], F16, A0 + 32768)
        kh_t = place("kh_t", [128, 8 * S], F16, A0)
        kl_t = place("kl_t", [128, 8 * S], F16, A0 + 32768)
        x1_t = place("x1_t", [128, 8 * S], F32, A0)

        wbuf = place("wbuf", [128, 2 * D], F32, B0)
        wbh = place("wbh", [128, 2 * D], F16, B0 + 8192)
        wbl = place("wbl", [128, 2 * D], F16, B0 + 12288)
        xtm = place("xtm", [128, 2 * D], F32, B0 + 16384)
        hjunk = place("hjunk", [128, D], F32, B0 + 24576)
        hf32 = place("hf32", [128, D], F32, B0 + 28672)
        hfm32 = place("hfm32", [128, D], F32, B0 + 32768)
        wd32 = place("wd32", [128, 2 * CL], F32, B0 + 36864)
        wdh = place("wdh", [128, 8 * CL], F16, B0 + 40960)
        wdl = place("wdl", [128, 8 * CL], F16, B0 + 49152)
        wdoh = place("wdoh", [128, 8 * CL], F16, B0 + 16384)
        wdol = place("wdol", [128, 8 * CL], F16, B0 + 24576)
        cbuf = place("cbuf", [128, 512], F32, B0 + 57344)
        cbh = place("cbh", [128, 2 * 512], F16, B0 + 59392)
        cbl = place("cbl", [128, 2 * 512], F16, B0 + 61440)
        vh_t = place("vh_t", [128, 16640], F16, B0)
        vl_t = place("vl_t", [128, 16640], F16, B0 + 33280)
        r1_t = place("r1_t", [128, 8 * 512], F32, B0)
        aw_t = place("aw_t", [128, 512], F32, B0 + 16384)
        sqm_t = place("sqm_t", [128, 512], F32, B0 + 18432)
        xtm4 = place("xtm4", [128, 2 * 512], F32, B0 + 20480)
        rowA = place("rowA", [1, 512], F32, B0 + 24576)
        rowB = place("rowB", [1, 512], F32, B0 + 26624)
        rowC = place("rowC", [1, 512], F32, B0 + 28672)
        rowD = place("rowD", [1, 512], F32, B0 + 30720)
        xmh_t = place("xmh_t", [128, 8 * 1024], F16, B0)
        xml_t = place("xml_t", [128, 8 * 1024], F16, B0 + 16384)
        kah_t = place("kah_t", [128, 8 * 1024], F16, B0 + 32768)
        kal_t = place("kal_t", [128, 8 * 1024], F16, B0 + 49152)

        ckv_th = place("ckv_th", [128, 4 * 512], F16, C0)
        ckv_tl = place("ckv_tl", [128, 4 * 512], F16, C0 + 4096)
        wukh = place("wukh", [128, 4 * D], F16, C0 + 8192)
        wukl = place("wukl", [128, 4 * D], F16, C0 + 16384)
        cvsh = place("cvsh", [128, 4 * 128], F16, C0)
        cvsl = place("cvsl", [128, 4 * 128], F16, C0 + 1024)
        wuvh = place("wuvh", [128, 4 * D], F16, C0 + 8192)
        wuvl = place("wuvl", [128, 4 * D], F16, C0 + 16384)
        cqh_t = place("cqh_t", [128, 4 * 512], F16, C0)
        cql_t = place("cql_t", [128, 4 * 512], F16, C0 + 4096)
        wuqh_t = place("wuqh_t", [128, 2 * 512], F16, C0 + 8192)
        wuql_t = place("wuql_t", [128, 2 * 512], F16, C0 + 10240)
        expf = place("expf", [128, 2 * 512], F32, C0 + 12288)
        exph = place("exph", [128, 6 * 512], F16, C0 + 16384)
        expl = place("expl", [128, 6 * 512], F16, C0 + 22528)
        anf_t = place("anf_t", [128, 2 * 512], F32, C0 + 28672)
        an16h = place("an16h", [128, 2 * 512], F16, C0 + 32768)
        an16l = place("an16l", [128, 2 * 512], F16, C0 + 34816)
        qf_t = place("qf_t", [128, 512], F32, C0 + 36864)
        qt1_t = place("qt1_t", [128, 512], F32, C0 + 38912)
        zsc_t = place("zsc_t", [128, 1024], F32, C0 + 12288)
        wu32 = place("wu32", [128, 1024], F32, C0 + 40960)
        qhi_t = place("qhi_t", [128, 512], F16, C0 + 40960)
        qlo_t = place("qlo_t", [128, 512], F16, C0 + 41984)
        wo_th = place("wo_th", [128, 2 * 1024], F16, C0)
        wo_tl = place("wo_tl", [128, 2 * 1024], F16, C0 + 4096)
        anrh = place("anrh", [128, 8 * 512], F16, C0 + 8192)
        anrl = place("anrl", [128, 8 * 512], F16, C0 + 16384)
        wf_h = [place("wf_h0", [128, 2048], F16, C0),
                place("wf_h1", [128, 2048], F16, C0 + 4096)]
        wf_l = [place("wf_l0", [128, 2048], F16, C0 + 8192),
                place("wf_l1", [128, 2048], F16, C0 + 12288)]
        dbuf = place("dbuf", [128, 1024], F32, C0 + 16384)
        xfbuf = place("xfbuf", [128, 1024], F32, C0 + 20480)
        relu_t = place("relu_t", [128, 512], F32, C0 + 24576)
        k2_t = place("k2_t", [128, 512], F32, C0 + 26624)
        rs_t = place("rs_t", [128, 512], F32, C0 + 28672)
        ff_t = place("ff_t", [128, 512], F32, C0 + 30720)
        ytm = place("ytm", [128, 2 * D], F32, C0 + 32768)
        yjunk = place("yjunk", [128, D], F32, C0 + 16384)

        blk = ST.enter_context(nc.Block())
        P = Prog()
        op = P.op

        def f_consts(q):
            r = []
            for nm, t in (("tc", tc_t), ("tsn", ts_t), ("id32", id32_t),
                          ("perm32", perm_t), ("e1", e1_t)):
                r.append(q.dma_start(out=t[:], in_=d_in[nm][:, :]))
            r.append(q.dma_start(out=onesr_t[0:1, :], in_=d_in["onesr"][:, :]))
            r.append(q.dma_start(out=eps_t[:, :], in_=d_in["epsv"][:, :]))
            for nm, t in (("gamma", gam_t), ("beta", bet_t), ("mur", mur_t),
                          ("muk", muk_t)):
                r.append(q.dma_start(
                    out=t[:, :],
                    in_=d_in[nm].rearrange("(c p) o -> p (c o)", p=128),
                    allow_slow_non_contiguous=True))
            return r
        c_ld = op("sync", f_consts, dma=True, n=11, desc="consts")

        def f_bsplit(v):
            v.tensor_copy(beth_t[:], bet_t[:])
            return v.scalar_tensor_tensor(betl_t[:], bet_t[:], 1.0, beth_t[:],
                                          OP.mult, OP.subtract)
        bsplit = op("vector", f_bsplit, deps=[c_ld], desc="beta split")

        wprep = [("wuqT", "wuqh_s", "wuql_s", 4), ("woutT", "wouth_s", "woutl_s", 8),
                 ("wrT", "wrh_s", "wrl_s", 8), ("wkT", "wkh_s", "wkl_s", 8),
                 ("wvT", "wvh_s", "wvl_s", 8)]
        pp_store = [None, None]
        i_pp = 0
        wprep_last = None
        for wname, hs, ls, nch in wprep:
            for c in range(nch):
                half = i_pp % 2
                sl = slice(half * D, half * D + D)

                def f_ld(q, _w=wname, _c=c, _sl=sl):
                    return q.dma_start(out=wbuf[:, _sl],
                                       in_=d_in[_w][_c * 128:(_c + 1) * 128, :])
                ldo = op("sync", f_ld, deps=[pp_store[half]], dma=True,
                         desc=f"wp {wname}{c} ld")

                def f_sp(v, _sl=sl):
                    v.tensor_copy(wbh[:, _sl], wbuf[:, _sl])
                    return v.scalar_tensor_tensor(wbl[:, _sl], wbuf[:, _sl],
                                                  1.0, wbh[:, _sl],
                                                  OP.mult, OP.subtract)
                spo = op("vector", f_sp, deps=[ldo, pp_store[half]],
                         desc=f"wp {wname}{c} split")

                def f_st(g, _h=hs, _l=ls, _c=c, _sl=sl):
                    return [g.dma_start(out=scr[_h][_c * 128:(_c + 1) * 128, :],
                                        in_=wbh[:, _sl]),
                            g.dma_start(out=scr[_l][_c * 128:(_c + 1) * 128, :],
                                        in_=wbl[:, _sl])]
                sto = op("gpsimd", f_st, deps=[spo], dma=True, n=2,
                         desc=f"wp {wname}{c} st")
                pp_store[half] = sto
                wprep_last = sto
                i_pp += 1

        h_ev2 = []
        hn_ops = []
        tp_prev = None
        ev1_prev = None
        ev1_prev2 = None
        st3_prev = None
        for i in range(16):
            half = i % 2
            xsl = slice(half * D, half * D + D)

            def f_xld(q, _i=i, _sl=xsl):
                return q.dma_start(out=xtm[:, _sl],
                                   in_=d_in["x"][_i * 128:(_i + 1) * 128, :])
            xld = op("sync", f_xld,
                     deps=[hn_ops[i - 2] if i >= 2 else None], dma=True,
                     desc=f"x ld {i}")

            def f_st1a(v, _sl=xsl):
                return v.reduce_sum(hstat[:, 0:1], xtm[:, _sl], axis=AX.X)
            st1a = op("vector", f_st1a, deps=[xld], desc=f"ln1 sum {i}")

            def f_st1b(s):
                return s.activation(hstat[:, 1:2], hstat[:, 0:1], AF.Identity,
                                    scale=INV_D)
            st1 = op("scalar", f_st1b, deps=[st1a],
                     desc=f"ln1 mu {i}")

            def f_st2a(v, _sl=xsl):
                return v.scalar_tensor_tensor(
                    hjunk[:], xtm[:, _sl], hstat[:, 1:2], xtm[:, _sl],
                    OP.subtract, OP.mult, accum_out=hstat[:, 2:3])
            st2a = op("vector", f_st2a, deps=[st1, st3_prev],
                      desc=f"ln1 ssq {i}")

            def f_st3(s):
                s.activation(hstat[:, 3:4], hstat[:, 2:3], AF.Identity,
                             scale=INV_D, bias=eps_t[:, 0:1])
                s.activation(hstat[:, 4:5], hstat[:, 3:4], AF.Ln)
                return s.activation(hstat[:, 5:6], hstat[:, 4:5], AF.Exp,
                                    scale=-0.5)
            st3 = op("scalar", f_st3,
                     deps=[st2a, hn_ops[i - 1] if i >= 1 else None],
                     desc=f"ln1 rstd {i}")
            st3_prev = st3

            def f_hn(v, _sl=xsl):
                return v.tensor_scalar(hf32[:], xtm[:, _sl], hstat[:, 1:2],
                                       hstat[:, 5:6], OP.subtract, OP.mult)
            hn = op("vector", f_hn, deps=[st3, tp_prev], desc=f"ln1 h {i}")
            hn_ops.append(hn)

            bk0 = (i % 2) * 2

            def f_tp(t, _b=bk0):
                r = []
                for c in range(8):
                    r.append(t.transpose(
                        PS[_b + c // 4][:, (c % 4) * 128:(c % 4 + 1) * 128],
                        hf32[:, c * 128:(c + 1) * 128], id32_t[:]))
                return r
            tp = op("tensor", f_tp, deps=[hn, c_ld, ev1_prev2],
                    desc=f"h tp {i}")
            tp_prev = tp

            def f_ev1(v, _b=bk0):
                v.tensor_copy(hfm32[:, 0:512], PS[_b][:, :])
                return v.tensor_copy(hfm32[:, 512:1024], PS[_b + 1][:, :])
            ev1 = op("vector", f_ev1, deps=[tp], desc=f"h evac {i}")
            ev1_prev2 = ev1_prev
            ev1_prev = ev1

            def f_ev2(v, _i=i):
                r = []
                for c in range(8):
                    dsl = slice(c * S + _i * 128, c * S + (_i + 1) * 128)
                    ssl = slice(c * 128, (c + 1) * 128)
                    v.tensor_scalar(hjunk[:, ssl], hfm32[:, ssl],
                                    gam_t[:, c:c + 1], bet_t[:, c:c + 1],
                                    OP.mult, OP.add)
                    v.tensor_copy(hh_fm[:, dsl], hjunk[:, ssl])
                    r.append(v.scalar_tensor_tensor(
                        hl_fm[:, dsl], hjunk[:, ssl], 1.0, hh_fm[:, dsl],
                        OP.mult, OP.subtract))
                return r
            ev2 = op("vector", f_ev2, deps=[c_ld], desc=f"h split {i}")
            h_ev2.append(ev2)

        tapped = []

        def do_tap(deps, srcs):
            outs = []
            for i, (ap_, shape, dt_) in enumerate(srcs):
                outs.append(nc.dram_tensor(f"dbg{i}", list(shape), dt_,
                                           kind="ExternalOutput").ap())

            def f_tap(q, _o=outs, _s=srcs):
                return [q.dma_start(out=o[:, :], in_=s[0][:, :] if s[0].ndim == 2
                                    else s[0][:])
                        for o, s in zip(_o, _s)]
            t_ = op("sync", f_tap, deps=deps, dma=True, n=len(srcs),
                    desc="tap")
            op("sync", lambda q: None, deps=[t_], desc="tapfin")
            tapped.append(True)

        if tap == "h":
            do_tap([h_ev2[15]], [(hh_fm, [128, 8 * S], F16),
                                 (hl_fm, [128, 8 * S], F16)])
            P.emit(blk, sems)
            return nc

        c_stores = []
        bias_ev = {}
        wd_splits = {}
        for wi, (wname, hs_c, ls_c) in enumerate(
                [("wdqT", "cqh_s", "cql_s"), ("wdkvT", "ckvh_s", "ckvl_s")]):
            wsp_list = []
            for c in range(8):
                half = c % 2
                sl32 = slice(half * CL, half * CL + CL)
                slc = slice(c * CL, (c + 1) * CL)

                def f_wld(q, _w=wname, _c=c, _sl=sl32):
                    return q.dma_start(out=wd32[:, _sl],
                                       in_=d_in[_w][_c * 128:(_c + 1) * 128, :])
                dep_reuse = wsp_list[c - 2] if c >= 2 else \
                    (wd_splits.get(wi - 1, [None] * 8)[6 + half] if wi > 0 else None)
                extra = []
                if wi == 0 and c < 2:
                    extra = [hn_ops[15], h_ev2[15]]
                wld = op("sync", f_wld, deps=[dep_reuse] + extra, dma=True,
                         desc=f"{wname} ld {c}")

                def f_wsp(v, _sl=sl32, _c=slc):
                    v.tensor_copy(wdh[:, _c], wd32[:, _sl])
                    return v.scalar_tensor_tensor(wdl[:, _c], wd32[:, _sl],
                                                  1.0, wdh[:, _c],
                                                  OP.mult, OP.subtract)
                dep_sp_extra = []
                if wi == 0 and c < 2:
                    dep_sp_extra = [hn_ops[15], h_ev2[15]]
                wsp = op("vector", f_wsp,
                         deps=[wld, c_ld] + dep_sp_extra, desc=f"{wname} sp {c}")
                wsp_list.append(wsp)
            wd_splits[wi] = wsp_list

            for co in range(4):
                for nt in range(4):
                    ti = co * 4 + nt
                    bank = ti % 2

                    def f_cmm(t, _co=co, _nt=nt, _b=bank):
                        r = []
                        for kc in range(8):
                            for p in range(3):
                                lhs = [wdh, wdh, wdl][p][
                                    :, kc * CL + _co * 128:
                                    kc * CL + _co * 128 + 128]
                                rhs = [hh_fm, hl_fm, hh_fm][p][
                                    :, kc * S + _nt * 512:
                                    kc * S + _nt * 512 + 512]
                                r.append(t.matmul(
                                    PS[_b][:, :], lhs, rhs,
                                    start=(kc == 0 and p == 0),
                                    stop=(kc == 7 and p == 2)))
                        return r
                    prev_ev = c_stores[-2][0] if len(c_stores) >= 2 else None
                    dep_p1 = [h_ev2[15]] if ti == 0 and wi == 0 else []
                    dep_ps = [ev1_prev] if ti < 2 and wi == 0 else []
                    cmm = op("tensor", f_cmm,
                             deps=[wsp_list[-1], prev_ev] + dep_p1 + dep_ps,
                             desc=f"c mm {wi}.{co}.{nt}")

                    half2 = ti % 2

                    def f_cev(v, _b=bank, _h2=half2):
                        sl = slice(_h2 * 512, _h2 * 512 + 512)
                        v.tensor_copy(cbuf[:], PS[_b][:, :])
                        v.tensor_copy(cbh[:, sl], cbuf[:])
                        return v.scalar_tensor_tensor(cbl[:, sl], cbuf[:], 1.0,
                                                      cbh[:, sl], OP.mult,
                                                      OP.subtract)
                    prev_st = c_stores[-2][1] if len(c_stores) >= 2 else None
                    cev = op("vector", f_cev,
                             deps=[cmm, prev_st],
                             desc=f"c ev {wi}.{co}.{nt}")

                    def f_cst(g, _h=hs_c, _l=ls_c, _co=co, _nt=nt, _h2=half2):
                        sl = slice(_h2 * 512, _h2 * 512 + 512)
                        return [
                            g.dma_start(out=scr[_h][_co * 128:(_co + 1) * 128,
                                                    _nt * 512:(_nt + 1) * 512],
                                        in_=cbh[:, sl]),
                            g.dma_start(out=scr[_l][_co * 128:(_co + 1) * 128,
                                                    _nt * 512:(_nt + 1) * 512],
                                        in_=cbl[:, sl])]
                    cst = op("gpsimd", f_cst, deps=[cev], dma=True, n=2,
                             desc=f"c st {wi}.{co}.{nt}")
                    c_stores.append((cev, cst))
        p2_last_cmm_dep = c_stores[-1][0]
        cq_all_stored = c_stores[15][1]
        ckv_all_stored = c_stores[-1][1]

        if tap == "c":
            do_tap([cq_all_stored, ckv_all_stored],
                   [(scr["cqh_s"], [CL, S], F16), (scr["cql_s"], [CL, S], F16),
                    (scr["ckvh_s"], [CL, S], F16),
                    (scr["ckvl_s"], [CL, S], F16),
                    (hh_fm, [128, 8 * S], F16), (hl_fm, [128, 8 * S], F16),
                    (wdh, [128, 8 * CL], F16), (wdl, [128, 8 * CL], F16)])
            P.emit(blk, sems)
            return nc

        wuk_sp = []
        for c in range(4):
            slc = slice(c * D, (c + 1) * D)

            def f_ld(q, _c=c):
                return q.dma_start(out=wu32[:, :],
                                   in_=d_in["wukT"][_c * 128:(_c + 1) * 128, :])
            ldo = op("sync", f_ld,
                     deps=[wuk_sp[c - 1] if c >= 1 else None,
                           p2_last_cmm_dep if c == 0 else None],
                     dma=True, desc=f"wuk ld {c}")

            def f_sp(v, _c=slc):
                v.tensor_copy(wukh[:, _c], wu32[:, :])
                return v.scalar_tensor_tensor(wukl[:, _c], wu32[:, :], 1.0,
                                              wukh[:, _c], OP.mult,
                                              OP.subtract)
            spo = op("vector", f_sp, deps=[ldo], desc=f"wuk sp {c}")
            wuk_sp.append(spo)

        k_written = []
        prev_kev = None
        prev_t2 = None
        prev_t2s = [None, None]
        prev_kevs = [None, None]
        for nt in range(4):
            def f_ckvld(q, _nt=nt):
                return [q.dma_start(
                            out=ckv_th[:, :].rearrange("p (c n) -> p c n", c=4),
                            in_=scr["ckvh_s"][:, _nt * 512:(_nt + 1) * 512]
                            .rearrange("(c p) n -> p c n", p=128)),
                        q.dma_start(
                            out=ckv_tl[:, :].rearrange("p (c n) -> p c n", c=4),
                            in_=scr["ckvl_s"][:, _nt * 512:(_nt + 1) * 512]
                            .rearrange("(c p) n -> p c n", p=128))]
            prev_mm_dep = k_written[-1] if k_written else None
            ckld = op("sync", f_ckvld,
                      deps=[ckv_all_stored, prev_mm_dep], dma=True, n=2,
                      desc=f"ckv ld {nt}")
            for pr in range(8):
                kbank = [2, 4][pr % 2]

                def f_kmm(t, _pr=pr, _kb=kbank):
                    r = []
                    for kc in range(4):
                        for p in range(3):
                            lhs = [wukh, wukh, wukl][p][
                                :, kc * D + _pr * 128: kc * D + _pr * 128 + 128]
                            rhs = [ckv_th, ckv_tl, ckv_th][p][
                                :, kc * 512:(kc + 1) * 512]
                            r.append(t.matmul(PS[_kb][:, :], lhs, rhs,
                                              start=(kc == 0 and p == 0),
                                              stop=(kc == 3 and p == 2)))
                    return r
                kmm = op("tensor", f_kmm,
                         deps=[ckld, wuk_sp[-1], prev_kevs[pr % 2],
                               p2_last_cmm_dep
                               if nt == 0 and pr == 0 else None],
                         desc=f"k mm {nt}.{pr}")

                def f_kev(s, _kb=kbank):
                    return s.copy(qf_t[:], PS[_kb][:, :])
                kev = op("scalar", f_kev, deps=[kmm, prev_t2],
                         desc=f"k ev {nt}.{pr}")
                prev_kev = kev
                prev_kevs[pr % 2] = kev

                pbank = [3, 5][pr % 2]

                def f_kpm(t, _pb=pbank):
                    return t.matmul(PS[_pb][:, :], perm_t[:], qf_t[:],
                                    start=True, stop=True)
                kpm = op("tensor", f_kpm, deps=[kev, prev_t2s[pr % 2]],
                         desc=f"k pm {nt}.{pr}")

                def f_krope(v, _nt=nt, _pb=pbank):
                    tsl = slice(_nt * 512, (_nt + 1) * 512)
                    v.tensor_mul(qt1_t[:], qf_t[:], tc_t[:, tsl])
                    v.tensor_mul(qf_t[:], PS[_pb][:, :], ts_t[:, tsl])
                    return v.tensor_add(qt1_t[:], qt1_t[:], qf_t[:])
                krope = op("vector", f_krope, deps=[kpm, kev],
                           desc=f"k rope {nt}.{pr}")
                prev_t2 = krope
                prev_t2s[pr % 2] = krope

                def f_ksp(v, _nt=nt, _pr=pr):
                    dsl = slice(_pr * S + _nt * 512, _pr * S + (_nt + 1) * 512)
                    v.tensor_copy(kh_t[:, dsl], qt1_t[:])
                    return v.scalar_tensor_tensor(kl_t[:, dsl], qt1_t[:], 1.0,
                                                  kh_t[:, dsl], OP.mult,
                                                  OP.subtract)
                ksp = op("vector", f_ksp, deps=[], desc=f"k sp {nt}.{pr}")
                k_written.append(ksp)
        k_done = k_written[-1]

        wuv_sp = []
        for c in range(4):
            slc = slice(c * D, (c + 1) * D)

            def f_ld(q, _c=c):
                return q.dma_start(out=wu32[:, :],
                                   in_=d_in["wuvT"][_c * 128:(_c + 1) * 128, :])
            ldo = op("sync", f_ld,
                     deps=[wuk_sp[-1] if c == 0 else wuv_sp[c - 1],
                           k_done if c == 0 else None],
                     dma=True, desc=f"wuv ld {c}")

            def f_sp(v, _c=slc):
                v.tensor_copy(wuvh[:, _c], wu32[:, :])
                return v.scalar_tensor_tensor(wuvl[:, _c], wu32[:, :], 1.0,
                                              wuvh[:, _c], OP.mult,
                                              OP.subtract)
            spo = op("vector", f_sp, deps=[ldo, k_done], desc=f"wuv sp {c}")
            wuv_sp.append(spo)

        def f_vones(g):
            g.memset(vh_t[:, 0:16640].rearrange("p (k h c) -> p k h c",
                                                k=16, h=16)[:, :, :, 64:65], 1.0)
            return g.memset(vl_t[:, 0:16640].rearrange("p (k h c) -> p k h c",
                                                       k=16, h=16)
                            [:, :, :, 64:65], 0.0)
        vones = op("gpsimd", f_vones,
                   deps=[ckv_all_stored, wprep_last, h_ev2[15]],
                   desc="v ones")

        v_written = []
        prev_vev = [None, None]
        prev_vsp = [None, None]
        for sk in range(16):
            def f_cvld(q, _sk=sk):
                return [q.dma_start(
                            out=cvsh[:, :].rearrange("p (c n) -> p c n", c=4),
                            in_=scr["ckvh_s"][:, _sk * 128:(_sk + 1) * 128]
                            .rearrange("(c p) n -> p c n", p=128)),
                        q.dma_start(
                            out=cvsl[:, :].rearrange("p (c n) -> p c n", c=4),
                            in_=scr["ckvl_s"][:, _sk * 128:(_sk + 1) * 128]
                            .rearrange("(c p) n -> p c n", p=128))]
            prev_dep = prev_vev[1] if sk > 0 else k_written[-1]
            cvld = op("sync", f_cvld, deps=[ckv_all_stored, prev_dep],
                      dma=True, n=2, desc=f"cv ld {sk}")
            for ntv in range(2):
                bank = 4 + ntv

                def f_vmm(t, _ntv=ntv, _b=bank):
                    r = []
                    for kc in range(4):
                        for p in range(3):
                            lhs = [cvsh, cvsh, cvsl][p][:, kc * 128:(kc + 1) * 128]
                            rhs = [wuvh, wuvl, wuvh][p][
                                :, kc * D + _ntv * 512: kc * D + _ntv * 512 + 512]
                            r.append(t.matmul(PS[_b][:, :], lhs, rhs,
                                              start=(kc == 0 and p == 0),
                                              stop=(kc == 3 and p == 2)))
                    return r
                vmm = op("tensor", f_vmm,
                         deps=[cvld, wuv_sp[-1], prev_vev[ntv]],
                         desc=f"v mm {sk}.{ntv}")

                def f_vev(s, _b=bank, _ntv=ntv):
                    dst = anf_t[:, 0:512] if _ntv == 0 else anf_t[:, 512:1024]
                    return s.copy(dst[:], PS[_b][:, :])
                vev = op("scalar", f_vev,
                         deps=[vmm, k_written[-1] if sk == 0 else None,
                               prev_vsp[ntv]],
                         desc=f"v ev {sk}.{ntv}")
                prev_vev[ntv] = vev

                def f_vsp(v, _sk=sk, _ntv=ntv):
                    src = anf_t[:, 0:512] if _ntv == 0 else anf_t[:, 512:1024]
                    r = []
                    for hh in range(8):
                        hd = _ntv * 8 + hh
                        dsl = slice(_sk * 1040 + hd * 65,
                                    _sk * 1040 + hd * 65 + 64)
                        ssl = slice(hh * 64, (hh + 1) * 64)
                        v.tensor_copy(vh_t[:, dsl], src[:, ssl])
                    for hh in range(8):
                        hd = _ntv * 8 + hh
                        dsl = slice(_sk * 1040 + hd * 65,
                                    _sk * 1040 + hd * 65 + 64)
                        ssl = slice(hh * 64, (hh + 1) * 64)
                        r.append(v.scalar_tensor_tensor(
                            vl_t[:, dsl], src[:, ssl], 1.0, vh_t[:, dsl],
                            OP.mult, OP.subtract))
                    return r
                vsp = op("vector", f_vsp, deps=[vev, vones],
                         desc=f"v sp {sk}.{ntv}")
                prev_vsp[ntv] = vsp
                v_written.append(vsp)
        v_done = v_written[-1]

        if tap == "kv":
            do_tap([k_done, v_done],
                   [(kh_t, [128, 8 * S], F16), (kl_t, [128, 8 * S], F16),
                    (vh_t, [128, 16640], F16), (vl_t, [128, 16640], F16)])
            P.emit(blk, sems)
            return nc

        an_spills = []
        blk_tail = {"qev": None, "zread": None, "bcast": None, "anorm": None,
                    "sc": {}, "ex": {}, "sp": {}, "av": {}}
        prev_block = {}
        for sq in range(NSQ):
            def f_cqld(q, _sq=sq):
                return [q.dma_start(
                            out=cqh_t[:, :].rearrange("p (c n) -> p c n", c=4),
                            in_=scr["cqh_s"][:, _sq * 512:(_sq + 1) * 512]
                            .rearrange("(c p) n -> p c n", p=128)),
                        q.dma_start(
                            out=cql_t[:, :].rearrange("p (c n) -> p c n", c=4),
                            in_=scr["cql_s"][:, _sq * 512:(_sq + 1) * 512]
                            .rearrange("(c p) n -> p c n", p=128))]
            dep0 = [cq_all_stored]
            if sq == 0:
                dep0 += [v_written[-1], k_written[-1]]
            else:
                dep0 += [prev_block["last_sc"]]
            cqld = op("sync", f_cqld, deps=dep0, dma=True, n=2,
                      desc=f"cq ld {sq}")

            for pr in range(NPAIR):
                b = sq * NPAIR + pr
                ring = b % 2

                def f_wuqld(g, _pr=pr, _r=ring):
                    sl = slice(_r * 512, _r * 512 + 512)
                    return [g.dma_start(
                                out=wuqh_t[:, sl].rearrange("p (c n) -> p c n", c=4),
                                in_=scr["wuqh_s"][:, _pr * 128:(_pr + 1) * 128]
                                .rearrange("(c p) n -> p c n", p=128)),
                            g.dma_start(
                                out=wuql_t[:, sl].rearrange("p (c n) -> p c n", c=4),
                                in_=scr["wuql_s"][:, _pr * 128:(_pr + 1) * 128]
                                .rearrange("(c p) n -> p c n", p=128))]
                wuqld = op("sync", f_wuqld,
                           deps=[wprep_last if b == 0 else None,
                                 prev_block.get("qmm2")],
                           dma=True, n=2, desc=f"wuq ld {b}")

                def f_qmm(t, _r=ring):
                    r = []
                    for kc in range(4):
                        for p in range(3):
                            lhs = [wuqh_t, wuqh_t, wuql_t][p][
                                :, _r * 512 + kc * 128: _r * 512 + kc * 128 + 128]
                            rhs = [cqh_t, cql_t, cqh_t][p][
                                :, kc * 512:(kc + 1) * 512]
                            r.append(t.matmul(PS[0][:, :], lhs, rhs,
                                              start=(kc == 0 and p == 0),
                                              stop=(kc == 3 and p == 2)))
                    return r
                qmm = op("tensor", f_qmm,
                         deps=[cqld, wuqld, prev_block.get("qev"),
                               prev_block.get(("ex", 14)),
                               v_done if b == 0 else None],
                         desc=f"q mm {b}")
                prev_block["qmm2"] = qmm

                def f_qev(s):
                    return s.copy(qf_t[:], PS[0][:, :])
                qev = op("scalar", f_qev,
                         deps=[qmm, prev_block.get("rope")],
                         desc=f"q ev {b}")
                prev_block["qev"] = qev

                def f_qpm(t):
                    return t.matmul(PS[1][:, :], perm_t[:], qf_t[:],
                                    start=True, stop=True)
                qpm = op("tensor", f_qpm,
                         deps=[qev, prev_block.get("rope"),
                               prev_block.get(("ex", 14)),
                               wuv_sp[-1] if b == 0 else None],
                         desc=f"q pm {b}")

                def f_qrope(v, _sq=sq):
                    tsl = slice(_sq * 512, (_sq + 1) * 512)
                    v.tensor_mul(qt1_t[:], qf_t[:], tc_t[:, tsl])
                    v.tensor_mul(qf_t[:], PS[1][:, :], ts_t[:, tsl])
                    v.tensor_add(qt1_t[:], qt1_t[:], qf_t[:])
                    v.tensor_copy(qhi_t[:], qt1_t[:])
                    return v.scalar_tensor_tensor(qlo_t[:], qt1_t[:], 1.0,
                                                  qhi_t[:], OP.mult,
                                                  OP.subtract)
                qrope = op("vector", f_qrope,
                           deps=[qpm, qev],
                           desc=f"q rope {b}")
                prev_block["rope"] = qrope

                if tap == "q0" and b == 0:
                    do_tap([qrope, qev, wuqld],
                           [(qt1_t, [128, 512], F32),
                            (qf_t, [128, 512], F32),
                            (cqh_t, [128, 4 * 512], F16),
                            (cql_t, [128, 4 * 512], F16),
                            (wuqh_t, [128, 2 * 512], F16),
                            (wuql_t, [128, 2 * 512], F16),
                            (scr["wuqh_s"], [CL, D], F16),
                            (scr["wuql_s"], [CL, D], F16)])
                    P.emit(blk, sems)
                    return nc

                sc_ops = {}
                ex_ops = {}
                eh_ops = {}
                sp_ops = {}
                av_ops = {}

                SCBK = [(2, 3), (4, 5), (0, 1)]

                def mk_sc(j):
                    def f_sc(t, _pr=pr, _j=j):
                        bk = SCBK[_j % 3]
                        r = []
                        for hh in range(2):
                            rr = slice(hh * 64, hh * 64 + 64)
                            tp = (hh * 64, 0)
                            ksl = slice(_pr * S + _j * 128,
                                        _pr * S + _j * 128 + 128)
                            for p in range(3):
                                lhs = [kh_t, kh_t, kl_t][p][rr, ksl]
                                rhs = [qhi_t, qlo_t, qhi_t][p][rr, :]
                                r.append(t.matmul(PS[bk[hh]][:, :], lhs, rhs,
                                                  start=(p == 0), stop=(p == 2),
                                                  tile_position=tp))
                        return r
                    return f_sc

                def mk_ex2(j):
                    def f_ex(s, _j=j):
                        bk0, bk1 = SCBK[_j % 3]
                        rg = _j % 2
                        s.activation(expf[:, rg * 512:rg * 512 + 512],
                                     PS[bk0][:, :], AF.Exp, scale=SCALE)
                        return s.activation(anf_t[:, rg * 512:rg * 512 + 512],
                                            PS[bk1][:, :], AF.Exp, scale=SCALE)
                    return f_ex

                def mk_eh(j):
                    def f_eh(g, _j=j):
                        rg = _j % 2
                        r3 = _j % 3
                        r = []
                        for hh, srct in ((0, expf), (1, anf_t)):
                            ssl = slice(rg * 512, rg * 512 + 512)
                            dsl = slice((hh * 3 + r3) * 512,
                                        (hh * 3 + r3) * 512 + 512)
                            r.append(g.dma_start(out=exph[:, dsl],
                                                 in_=srct[:, ssl]))
                        return r
                    return f_eh

                def mk_sp(j):
                    def f_sp(v, _j=j):
                        rg = _j % 2
                        r3 = _j % 3
                        r = []
                        for hh, srct in ((0, expf), (1, anf_t)):
                            ssl = slice(rg * 512, rg * 512 + 512)
                            dsl = slice((hh * 3 + r3) * 512,
                                        (hh * 3 + r3) * 512 + 512)
                            r.append(v.scalar_tensor_tensor(
                                expl[:, dsl], srct[:, ssl], 1.0, exph[:, dsl],
                                OP.mult, OP.subtract))
                        return r
                    return f_sp

                def mk_av(j):
                    def f_av(t, _pr=pr, _j=j):
                        r3 = _j % 3
                        r = []
                        for hh in range(2):
                            hd = _pr * 2 + hh
                            vsl = slice(_j * 1040 + hd * 65,
                                        _j * 1040 + hd * 65 + 65)
                            esl = slice((hh * 3 + r3) * 512,
                                        (hh * 3 + r3) * 512 + 512)
                            for p in range(3):
                                lhs = [vh_t, vh_t, vl_t][p][:, vsl]
                                rhs = [exph, expl, exph][p][:, esl]
                                r.append(t.matmul(
                                    PS[6 + hh][0:65, :], lhs, rhs,
                                    start=(_j == 0 and p == 0),
                                    stop=(_j == 15 and p == 2)))
                        return r
                    return f_av

                def mk_sc_deps(j):
                    deps_sc = [qrope]
                    if j >= 3:
                        deps_sc.append(ex_ops[j - 3])
                    else:
                        deps_sc.append(prev_block.get(("ex", [15, 13, 14][j])))
                    if b == 0 and j == 0:
                        deps_sc += [prev_kev, prev_t2]
                    if b == 0 and j == 1:
                        deps_sc += [prev_vev[0], prev_vev[1]]
                    return deps_sc

                def mk_av_deps(j):
                    deps_av = [sp_ops[j], eh_ops[j]]
                    if j == 0:
                        deps_av.append(prev_block.get("ansc"))
                        deps_av.append(prev_block.get("anorm"))
                        if b == 0:
                            deps_av.append(v_done)
                    return deps_av

                def emit_exsp(j):
                    deps_ex = [sc_ops[j]]
                    if j >= 2:
                        deps_ex.append(sp_ops[j - 2])
                        deps_ex.append(eh_ops[j - 2])
                    else:
                        deps_ex.append(prev_block.get(("sp", 14 + j)))
                        deps_ex.append(prev_block.get(("eh", 14 + j)))
                    if j < 2:
                        deps_ex.append(prev_block.get("anorm"))
                    ex_ops[j] = op("scalar", mk_ex2(j), deps=deps_ex,
                                   desc=f"ex {b}.{j}")
                    deps_eh = [ex_ops[j]]
                    if j >= 3:
                        deps_eh.append(av_ops[j - 3])
                    else:
                        deps_eh.append(prev_block.get(("av", 13 + j)))
                    eh_ops[j] = op("gpsimd", mk_eh(j), deps=deps_eh,
                                   dma=True, n=2, desc=f"eh {b}.{j}")
                    deps_sp = [eh_ops[j]]
                    if j >= 3:
                        deps_sp.append(av_ops[j - 3])
                    else:
                        deps_sp.append(prev_block.get(("av", 13 + j)))
                    sp_ops[j] = op("vector", mk_sp(j), deps=deps_sp,
                                   desc=f"sp {b}.{j}")

                for j in range(NCH):
                    sc_ops[j] = op("tensor", mk_sc(j), deps=mk_sc_deps(j),
                                   desc=f"sc {b}.{j}")
                    emit_exsp(j)
                    if j >= 2:
                        jj = j - 2
                        av_ops[jj] = op("tensor", mk_av(jj),
                                        deps=mk_av_deps(jj),
                                        desc=f"av {b}.{jj}")
                for jj in (14, 15):
                    av_ops[jj] = op("tensor", mk_av(jj), deps=mk_av_deps(jj),
                                    desc=f"av {b}.{jj}")

                def f_zcp(s):
                    s.copy(zsc_t[64:65, 0:512], PS[6][64:65, :])
                    return s.copy(zsc_t[64:65, 512:1024], PS[7][64:65, :])
                zcp = op("scalar", f_zcp,
                         deps=[av_ops[15], qrope], desc=f"z cp {b}")

                def f_zrc(v):
                    return v.reciprocal(zsc_t[0:1, 0:1024], zsc_t[64:65, 0:1024])
                zrc = op("vector", f_zrc, deps=[zcp], desc=f"z rc {b}")

                def f_ansc(s):
                    s.copy(anf_t[0:64, 0:512], PS[6][0:64, :])
                    return s.copy(anf_t[0:64, 512:1024], PS[7][0:64, :])
                ansc = op("scalar", f_ansc,
                          deps=[av_ops[15], sp_ops[15], zrc], desc=f"an sc {b}")
                prev_block["ansc"] = ansc

                def f_zbc(t):
                    t.matmul(PS[6][:, :], onesr_t[0:1, :], zsc_t[0:1, 0:512],
                             start=True, stop=True)
                    return t.matmul(PS[7][:, :], onesr_t[0:1, :],
                                    zsc_t[0:1, 512:1024], start=True, stop=True)
                zbc = op("tensor", f_zbc,
                         deps=[zrc, ansc, av_ops[15]], desc=f"z bc {b}")
                prev_block["zbc"] = zbc


                def f_anorm(v):
                    r = []
                    for hh in range(2):
                        asl = slice(hh * 512, hh * 512 + 512)
                        v.tensor_mul(anf_t[0:64, asl], anf_t[0:64, asl],
                                     PS[6 + hh][0:64, :])
                        v.tensor_copy(an16h[0:64, asl], anf_t[0:64, asl])
                        r.append(v.scalar_tensor_tensor(
                            an16l[0:64, asl], anf_t[0:64, asl], 1.0,
                            an16h[0:64, asl], OP.mult, OP.subtract))
                    return r
                anorm = op("vector", f_anorm,
                           deps=[ansc, zbc,
                                 an_spills[-1] if an_spills else None],
                           desc=f"an norm {b}")
                prev_block["anorm"] = anorm

                def f_anst(g, _sq=sq, _pr=pr):
                    r = []
                    for hh in range(2):
                        asl = slice(hh * 512, hh * 512 + 512)
                        rr = slice(_pr * 128 + hh * 64, _pr * 128 + hh * 64 + 64)
                        cc = slice(_sq * 512, (_sq + 1) * 512)
                        r.append(g.dma_start(out=scr["anh_s"][rr, cc],
                                             in_=an16h[0:64, asl]))
                        r.append(g.dma_start(out=scr["anl_s"][rr, cc],
                                             in_=an16l[0:64, asl]))
                    return r
                anst = op("gpsimd", f_anst, deps=[anorm], dma=True, n=4,
                          desc=f"an st {b}")
                an_spills.append(anst)
                prev_block["anst"] = anst

                prev_block["last_sc"] = sc_ops[15]
                for j in range(NCH):
                    prev_block[("ex", j)] = ex_ops[j]
                    prev_block[("sp", j)] = sp_ops[j]
                    prev_block[("eh", j)] = eh_ops[j]
                    prev_block[("av", j)] = av_ops[j]
        attn_done = an_spills[-1]
        attn_done_compute = prev_block["anorm"]

        if tap == "an":
            do_tap([attn_done], [(scr["anh_s"], [D, S], F16),
                                 (scr["anl_s"], [D, S], F16)])
            P.emit(blk, sems)
            return nc

        x1_written = []
        prev_sq_norm = None
        prev_wout_mms = None
        for sq in range(NSQ):
            def f_anld(q, _sq=sq):
                cc = slice(_sq * 512, (_sq + 1) * 512)
                return [q.dma_start(out=anrh[:, :].rearrange("p (c n) -> p c n", c=8),
                                    in_=scr["anh_s"][:, cc]
                                    .rearrange("(c p) n -> p c n", p=128)),
                        q.dma_start(out=anrl[:, :].rearrange("p (c n) -> p c n", c=8),
                                    in_=scr["anl_s"][:, cc]
                                    .rearrange("(c p) n -> p c n", p=128))]
            anld = op("sync", f_anld,
                      deps=[an_spills[(sq + 1) * NPAIR - 1],
                            an_spills[-1] if sq == 0 else None,
                            prev_sq_norm],
                      dma=True, n=2, desc=f"an ld {sq}")

            wout_mms = []
            aw_ops = []
            r1_ops = []
            tp_ops = []
            for m in range(8):
                ring = m % 2

                def f_wold(g, _m=m, _r=ring):
                    sl = slice(_r * 1024, _r * 1024 + 1024)
                    return [g.dma_start(
                                out=wo_th[:, sl].rearrange("p (c n) -> p c n", c=8),
                                in_=scr["wouth_s"][:, _m * 128:(_m + 1) * 128]
                                .rearrange("(c p) n -> p c n", p=128)),
                            g.dma_start(
                                out=wo_tl[:, sl].rearrange("p (c n) -> p c n", c=8),
                                in_=scr["woutl_s"][:, _m * 128:(_m + 1) * 128]
                                .rearrange("(c p) n -> p c n", p=128))]
                if m >= 2:
                    dep_wo = [wout_mms[m - 2]]
                elif sq == 0:
                    dep_wo = [attn_done, attn_done_compute]
                else:
                    dep_wo = [prev_wout_mms[6 + m]]
                wold = op("sync", f_wold, deps=dep_wo,
                          dma=True, n=2, desc=f"wo ld {sq}.{m}")

                def f_womm(t, _m=m, _r=ring):
                    r = []
                    for kc in range(8):
                        for p in range(3):
                            lhs = [wo_th, wo_th, wo_tl][p][
                                :, _r * 1024 + kc * 128:
                                _r * 1024 + kc * 128 + 128]
                            rhs = [anrh, anrl, anrh][p][
                                :, kc * 512:(kc + 1) * 512]
                            r.append(t.matmul(PS[_m][:, :], lhs, rhs,
                                              start=(kc == 0 and p == 0),
                                              stop=(kc == 7 and p == 2)))
                    return r
                dep_ps = [attn_done_compute] if sq == 0 else [prev_sq_norm]
                womm = op("tensor", f_womm,
                          deps=[anld, wold] + dep_ps, desc=f"wo mm {sq}.{m}")
                wout_mms.append(womm)

                def f_aw(s, _m=m):
                    return s.copy(aw_t[:], PS[_m][:, :])
                awo = op("scalar", f_aw,
                         deps=[womm, r1_ops[m - 1] if m >= 1 else None],
                         desc=f"aw {sq}.{m}")
                aw_ops.append(awo)

                def f_xld(q, _sq=sq, _m=m, _r=ring):
                    sl = slice(_r * 512, _r * 512 + 512)
                    r = []
                    for i in range(4):
                        r.append(q.dma_start(
                            out=xtm4[:, _r * 512 + i * 128:
                                     _r * 512 + (i + 1) * 128],
                            in_=d_in["x"][_sq * 512 + i * 128:
                                          _sq * 512 + (i + 1) * 128,
                                          _m * 128:(_m + 1) * 128]))
                    return r
                xld4 = op("sync", f_xld,
                          deps=[anld, tp_ops[m - 2] if m >= 2 else
                                (prev_sq_norm if sq > 0 else None)],
                          dma=True, n=4, desc=f"x4 ld {sq}.{m}")

                def f_xtp(t, _m=m, _r=ring):
                    r = []
                    for i in range(4):
                        r.append(t.transpose(
                            PS[_m][:, i * 128:(i + 1) * 128],
                            xtm4[:, _r * 512 + i * 128:_r * 512 + (i + 1) * 128],
                            id32_t[:]))
                    return r
                xtp = op("tensor", f_xtp, deps=[xld4, awo],
                         desc=f"x tp {sq}.{m}")
                tp_ops.append(xtp)

                def f_r1(v, _m=m):
                    return v.tensor_add(r1_t[:, _m * 512:(_m + 1) * 512],
                                        aw_t[:], PS[_m][:, :])
                dep_r1 = [xtp, awo]
                if sq == 0 and m == 0:
                    dep_r1.append(attn_done_compute)
                r1o = op("vector", f_r1, deps=dep_r1, desc=f"r1 {sq}.{m}")
                r1_ops.append(r1o)

            st_mms = []
            sq_ops = []
            for m in range(8):
                def f_sqm(v, _m=m):
                    return v.tensor_mul(sqm_t[:], r1_t[:, _m * 512:(_m + 1) * 512],
                                        r1_t[:, _m * 512:(_m + 1) * 512])
                sqo = op("vector", f_sqm,
                         deps=[r1_ops[m],
                               st_mms[m - 1] if m >= 1 else None],
                         desc=f"sqm {sq}.{m}")
                sq_ops.append(sqo)

                def f_stm(t, _m=m):
                    a = t.matmul(PS[0][:, :], e1_t[:],
                                 r1_t[:, _m * 512:(_m + 1) * 512],
                                 start=(_m == 0), stop=(_m == 7))
                    bmm_ = t.matmul(PS[1][:, :], e1_t[:], sqm_t[:],
                                    start=(_m == 0), stop=(_m == 7))
                    return bmm_
                stm = op("tensor", f_stm,
                         deps=[sqo, r1_ops[m],
                               r1_ops[0] if m == 0 else None,
                               r1_ops[1] if m == 0 else None],
                         desc=f"st mm {sq}.{m}")
                st_mms.append(stm)

            def f_rows1(s):
                s.activation(rowA[0:1, :], PS[0][0:1, :], AF.Identity,
                             scale=INV_D)
                return s.activation(rowB[0:1, :], PS[1][0:1, :], AF.Identity,
                                    scale=INV_D)
            rows1 = op("scalar", f_rows1, deps=[st_mms[-1]],
                       desc=f"rows1 {sq}")

            def f_rows2(v):
                v.tensor_mul(rowC[0:1, :], rowA[0:1, :], rowA[0:1, :])
                v.tensor_sub(rowD[0:1, :], rowB[0:1, :], rowC[0:1, :])
                return v.tensor_scalar(rowD[0:1, :], rowD[0:1, :], EPS, None,
                                       OP.add)
            rows2 = op("vector", f_rows2, deps=[rows1], desc=f"rows2 {sq}")

            def f_rows3(s):
                s.activation(rowC[0:1, :], rowD[0:1, :], AF.Ln)
                return s.activation(rowC[0:1, :], rowC[0:1, :], AF.Exp,
                                    scale=-0.5)
            rows3 = op("scalar", f_rows3, deps=[rows2], desc=f"rows3 {sq}")

            def f_bc2(t):
                t.matmul(PS[2][:, :], onesr_t[0:1, :], rowA[0:1, :],
                         start=True, stop=True)
                return t.matmul(PS[3][:, :], onesr_t[0:1, :], rowC[0:1, :],
                                start=True, stop=True)
            bc2 = op("tensor", f_bc2,
                     deps=[rows3, rows2, r1_ops[2], r1_ops[3]],
                     desc=f"bc2 {sq}")

            norm_ops = []
            for m in range(8):
                def f_norm(v, _sq=sq, _m=m):
                    v.tensor_sub(aw_t[:], r1_t[:, _m * 512:(_m + 1) * 512],
                                 PS[2][:, :])
                    return v.tensor_mul(
                        x1_t[:, _m * S + _sq * 512: _m * S + _sq * 512 + 512],
                        aw_t[:], PS[3][:, :])
                depn = [bc2, aw_ops[-1]]
                if sq == 0 and m == 0:
                    depn.append(k_done)
                    depn.append(prev_block["last_sc"])
                nrm = op("vector", f_norm, deps=depn, desc=f"x1 {sq}.{m}")
                norm_ops.append(nrm)
                x1_written.append(nrm)
            prev_sq_norm = norm_ops[-1]
            prev_wout_mms = wout_mms
        x1_done = x1_written[-1]

        if tap == "x1":
            do_tap([x1_done], [(x1_t, [128, 8 * S], F32)])
            P.emit(blk, sems)
            return nc

        def f_stash(v):
            r = []
            for c in range(8):
                r.append(v.tensor_copy(stash_t[:, c:c + 1],
                                       x1_t[:, c * S + 1023:c * S + 1024]))
            return r
        stash = op("vector", f_stash, deps=[x1_done], desc="stash")

        x2_written = []
        prev_half = {}
        for hf in range(2):
            hw = slice(hf * 1024, hf * 1024 + 1024)
            mix_ops = {}
            for tag, mu_t, dsth, dstl in (("k", muk_t, xmh_t, xml_t),):
                pass
            for phase, (mu_t,) in enumerate([(muk_t,), (mur_t,)]):
                tag = "xk" if phase == 0 else "xr"
                mx_list = []
                for c in range(8):
                    def f_mix(v, _c=c, _hf=hf, _mu=mu_t, _ph=phase):
                        xsl = slice(_c * S + _hf * 1024,
                                    _c * S + _hf * 1024 + 1024)
                        xprev = slice(_c * S + _hf * 1024 - 1,
                                      _c * S + _hf * 1024 + 1023)
                        r = []
                        if _hf == 0:
                            v.tensor_copy(dbuf[:, 0:1],
                                          x1_t[:, _c * S:_c * S + 1])
                            v.tensor_sub(dbuf[:, 1:1024],
                                         x1_t[:, _c * S + 1:_c * S + 1024],
                                         x1_t[:, _c * S:_c * S + 1023])
                            v.tensor_scalar(xfbuf[:, 0:1], dbuf[:, 0:1],
                                            _mu[:, _c:_c + 1], None, OP.mult)
                            v.scalar_tensor_tensor(
                                xfbuf[:, 1:1024], dbuf[:, 1:1024],
                                _mu[:, _c:_c + 1],
                                x1_t[:, _c * S:_c * S + 1023],
                                OP.mult, OP.add)
                        else:
                            v.scalar_tensor_tensor(
                                dbuf[:, 0:1],
                                x1_t[:, _c * S + 1024:_c * S + 1025],
                                1.0, stash_t[:, _c:_c + 1],
                                OP.mult, OP.subtract)
                            v.tensor_sub(dbuf[:, 1:1024],
                                         x1_t[:, _c * S + 1025:_c * S + 2048],
                                         x1_t[:, _c * S + 1024:_c * S + 2047])
                            v.scalar_tensor_tensor(
                                xfbuf[:, 0:1], dbuf[:, 0:1],
                                _mu[:, _c:_c + 1], stash_t[:, _c:_c + 1],
                                OP.mult, OP.add)
                            v.scalar_tensor_tensor(
                                xfbuf[:, 1:1024], dbuf[:, 1:1024],
                                _mu[:, _c:_c + 1],
                                x1_t[:, _c * S + 1024:_c * S + 2047],
                                OP.mult, OP.add)
                        dsl = slice(_c * 1024, (_c + 1) * 1024)
                        v.tensor_copy(xmh_t[:, dsl], xfbuf[:])
                        r.append(v.scalar_tensor_tensor(
                            xml_t[:, dsl], xfbuf[:], 1.0, xmh_t[:, dsl],
                            OP.mult, OP.subtract))
                        return r
                    depm = [stash, c_ld]
                    if phase == 0 and c == 0:
                        if hf == 0:
                            depm.append(x1_done)
                        else:
                            depm.append(prev_half["x2"])
                        depm.append(prev_half.get("last_rmm"))
                    if phase == 1 and c == 0:
                        depm.append(kk_mms[-1])
                    mx = op("vector", f_mix, deps=depm,
                            desc=f"{tag} mix {hf}.{c}")
                    mx_list.append(mx)
                if phase == 0:
                    xk_mix = mx_list
                else:
                    xr_mix = mx_list

                if phase == 0:
                    kk_mms = []
                    ka_ops = []
                    kar_ops = []
                    for m in range(8):
                        ring = m % 2

                        def f_wld(g, _m=m, _r=ring):
                            return [g.dma_start(
                                        out=wf_h[_r][:, 0:1024].rearrange("p (c n) -> p c n", c=8),
                                        in_=scr["wkh_s"][:, _m * 128:(_m + 1) * 128]
                                        .rearrange("(c p) n -> p c n", p=128)),
                                    g.dma_start(
                                        out=wf_l[_r][:, 0:1024].rearrange("p (c n) -> p c n", c=8),
                                        in_=scr["wkl_s"][:, _m * 128:(_m + 1) * 128]
                                        .rearrange("(c p) n -> p c n", p=128))]
                        dep_w = []
                        if hf == 0 and m < 2:
                            dep_w = [attn_done, wprep_last,
                                     prev_sq_norm]
                        wldk = op("sync", f_wld,
                                  deps=dep_w + [kk_mms[-3] if m >= 2 else
                                                prev_half.get("last_rmm")],
                                  dma=True, n=2, desc=f"wk ld {hf}.{m}")

                        for ntv in range(2):
                            bank = ntv

                            def f_kkmm(t, _m=m, _r=ring, _n=ntv, _b=bank):
                                r = []
                                for kc in range(8):
                                    for p in range(3):
                                        lhs = [wf_h[_r], wf_h[_r], wf_l[_r]][p][
                                            :, kc * 128:kc * 128 + 128]
                                        rhs = [xmh_t, xml_t, xmh_t][p][
                                            :, kc * 1024 + _n * 512:
                                            kc * 1024 + _n * 512 + 512]
                                        r.append(t.matmul(
                                            PS[_b][:, :], lhs, rhs,
                                            start=(kc == 0 and p == 0),
                                            stop=(kc == 7 and p == 2)))
                                return r
                            dep_kk = [wldk, xk_mix[-1]]
                            if m == 0 and hf == 0:
                                dep_kk.append(x1_done)
                            if m >= 1:
                                dep_kk.append(kar_ops[-2])
                            kkm = op("tensor", f_kkmm, deps=dep_kk,
                                     desc=f"kk mm {hf}.{m}.{ntv}")
                            kk_mms.append(kkm)

                            def f_karelu(s, _b=bank):
                                return s.activation(relu_t[:], PS[_b][:, :],
                                                    AF.Relu)
                            kar = op("scalar", f_karelu,
                                     deps=[kkm,
                                           ka_ops[-1] if ka_ops else None],
                                     desc=f"ka relu {hf}.{m}.{ntv}")
                            kar_ops.append(kar)

                            def f_kasp(v, _m=m, _n=ntv):
                                dsl = slice(_m * 1024 + _n * 512,
                                            _m * 1024 + _n * 512 + 512)
                                v.tensor_mul(k2_t[:], relu_t[:], relu_t[:])
                                v.tensor_copy(kah_t[:, dsl], k2_t[:])
                                return v.scalar_tensor_tensor(
                                    kal_t[:, dsl], k2_t[:], 1.0,
                                    kah_t[:, dsl], OP.mult, OP.subtract)
                            dep_ka = [kar]
                            if hf == 0 and m == 0:
                                dep_ka.append(attn_done_compute)
                            kasp = op("vector", f_kasp, deps=dep_ka,
                                      desc=f"ka sp {hf}.{m}.{ntv}")
                            ka_ops.append(kasp)
                    kk_last = kk_mms[-1]
                    if tap == "ffn1" and hf == 0:
                        do_tap([ka_ops[-1], xk_mix[-1]],
                               [(xmh_t, [128, 8 * 1024], F16),
                                (xml_t, [128, 8 * 1024], F16),
                                (kah_t, [128, 8 * 1024], F16),
                                (kal_t, [128, 8 * 1024], F16)])
                        P.emit(blk, sems)
                        return nc

            rv_ops = []
            rmms = []
            last_rmm = None
            for m in range(8):
                ring = m % 2

                def f_wldrv(g, _m=m, _r=ring):
                    return [g.dma_start(
                                out=wf_h[_r][:, 0:1024].rearrange("p (c n) -> p c n", c=8),
                                in_=scr["wrh_s"][:, _m * 128:(_m + 1) * 128]
                                .rearrange("(c p) n -> p c n", p=128)),
                            g.dma_start(
                                out=wf_l[_r][:, 0:1024].rearrange("p (c n) -> p c n", c=8),
                                in_=scr["wrl_s"][:, _m * 128:(_m + 1) * 128]
                                .rearrange("(c p) n -> p c n", p=128)),
                            g.dma_start(
                                out=wf_h[_r][:, 1024:2048].rearrange("p (c n) -> p c n", c=8),
                                in_=scr["wvh_s"][:, _m * 128:(_m + 1) * 128]
                                .rearrange("(c p) n -> p c n", p=128)),
                            g.dma_start(
                                out=wf_l[_r][:, 1024:2048].rearrange("p (c n) -> p c n", c=8),
                                in_=scr["wvl_s"][:, _m * 128:(_m + 1) * 128]
                                .rearrange("(c p) n -> p c n", p=128))]
                dep_w = [rmms[-3]] if m >= 2 else [kk_last]
                wldrv = op("sync", f_wldrv, deps=dep_w, dma=True, n=4,
                           desc=f"wrv ld {hf}.{m}")

                for ntv in range(2):
                    rbank = 2
                    vbank = 3

                    def f_rmm(t, _m=m, _r=ring, _n=ntv, _rb=rbank, _vb=vbank):
                        r = []
                        for kc in range(8):
                            for p in range(3):
                                lhs = [wf_h[_r], wf_h[_r], wf_l[_r]][p][
                                    :, kc * 128:kc * 128 + 128]
                                rhs = [xmh_t, xml_t, xmh_t][p][
                                    :, kc * 1024 + _n * 512:
                                    kc * 1024 + _n * 512 + 512]
                                r.append(t.matmul(PS[_rb][:, :], lhs, rhs,
                                                  start=(kc == 0 and p == 0),
                                                  stop=(kc == 7 and p == 2)))
                        for kc in range(8):
                            for p in range(3):
                                lhs = [wf_h[_r], wf_h[_r], wf_l[_r]][p][
                                    :, 1024 + kc * 128:1024 + kc * 128 + 128]
                                rhs = [kah_t, kal_t, kah_t][p][
                                    :, kc * 1024 + _n * 512:
                                    kc * 1024 + _n * 512 + 512]
                                r.append(t.matmul(PS[_vb][:, :], lhs, rhs,
                                                  start=(kc == 0 and p == 0),
                                                  stop=(kc == 7 and p == 2)))
                        return r
                    dep_r = [wldrv, xr_mix[-1], ka_ops[-1]]
                    if rv_ops:
                        dep_r.append(rv_ops[-1])
                    elif hf == 0:
                        dep_r.append(x1_done)
                    rmm = op("tensor", f_rmm, deps=dep_r,
                             desc=f"rv mm {hf}.{m}.{ntv}")
                    rmms.append(rmm)
                    last_rmm = rmm

                    def f_sig(s, _rb=rbank):
                        return s.activation(rs_t[:], PS[_rb][:, :], AF.Sigmoid)
                    sig = op("scalar", f_sig,
                             deps=[rmm, rv_ops[-1] if rv_ops else None],
                             desc=f"sig {hf}.{m}.{ntv}")

                    def f_x2(v, _m=m, _hf=hf, _n=ntv, _vb=vbank):
                        xsl = slice(_m * S + _hf * 1024 + _n * 512,
                                    _m * S + _hf * 1024 + _n * 512 + 512)
                        v.tensor_mul(ff_t[:], rs_t[:], PS[_vb][:, :])
                        return v.tensor_add(x1_t[:, xsl], ff_t[:],
                                            x1_t[:, xsl])
                    x2o = op("vector", f_x2, deps=[sig, rmm, stash],
                             desc=f"x2 {hf}.{m}.{ntv}")
                    rv_ops.append(x2o)
                    x2_written.append(x2o)
            prev_half["x2"] = rv_ops[-1]
            prev_half["last_rmm"] = last_rmm
        x2_done = x2_written[-1]

        if tap == "x2":
            do_tap([x2_done], [(x1_t, [128, 8 * S], F32)])
            P.emit(blk, sems)
            return nc

        y_stores = []
        prev_yev = None
        prev_ynorm = None
        for i in range(16):
            ring = i % 2

            def f_ytp(t, _i=i):
                r = []
                for c in range(8):
                    r.append(t.transpose(
                        PS[c // 4][:, (c % 4) * 128:(c % 4 + 1) * 128],
                        x1_t[:, c * S + _i * 128:c * S + (_i + 1) * 128],
                        id32_t[:]))
                return r
            dep_tp = [x2_done, prev_yev]
            if i == 0:
                dep_tp += [prev_half["last_rmm"]]
            ytp = op("tensor", f_ytp, deps=dep_tp, desc=f"y tp {i}")

            def f_yev(v, _r=ring):
                sl = slice(_r * D, _r * D + D)
                v.tensor_copy(ytm[:, sl][:, 0:512], PS[0][:, :])
                return v.tensor_copy(ytm[:, sl][:, 512:1024], PS[1][:, :])
            dep_ev = [ytp]
            if i >= 2:
                dep_ev.append(y_stores[i - 2])
            yev = op("vector", f_yev, deps=dep_ev, desc=f"y ev {i}")
            prev_yev = yev

            def f_yst1a(v, _r=ring):
                sl = slice(_r * D, _r * D + D)
                return v.reduce_sum(yst_t[:, 0:1], ytm[:, sl], axis=AX.X)
            yst1a = op("vector", f_yst1a, deps=[], desc=f"y sum {i}")

            def f_yst1b(s):
                return s.activation(yst_t[:, 1:2], yst_t[:, 0:1], AF.Identity,
                                    scale=INV_D)
            yst1 = op("scalar", f_yst1b, deps=[yst1a],
                      desc=f"y mu {i}")

            def f_yst2a(v, _r=ring):
                sl = slice(_r * D, _r * D + D)
                return v.scalar_tensor_tensor(
                    yjunk[:], ytm[:, sl], yst_t[:, 1:2], ytm[:, sl],
                    OP.subtract, OP.mult, accum_out=yst_t[:, 2:3])
            yst2a = op("vector", f_yst2a, deps=[yst1, prev_ynorm],
                       desc=f"y ssq {i}")

            def f_yst3(s):
                s.activation(yst_t[:, 3:4], yst_t[:, 2:3], AF.Identity,
                             scale=INV_D, bias=eps_t[:, 0:1])
                s.activation(yst_t[:, 4:5], yst_t[:, 3:4], AF.Ln)
                return s.activation(yst_t[:, 5:6], yst_t[:, 4:5], AF.Exp,
                                    scale=-0.5)
            yst3 = op("scalar", f_yst3,
                      deps=[yst2a, prev_ynorm], desc=f"y st3 {i}")

            def f_ynorm(v, _r=ring):
                sl = slice(_r * D, _r * D + D)
                return v.tensor_scalar(ytm[:, sl], ytm[:, sl], yst_t[:, 1:2],
                                       yst_t[:, 5:6], OP.subtract, OP.mult)
            ynorm = op("vector", f_ynorm, deps=[yst3], desc=f"y norm {i}")
            prev_ynorm = ynorm

            def f_yst(q, _i=i, _r=ring):
                sl = slice(_r * D, _r * D + D)
                return q.dma_start(out=y_d[_i * 128:(_i + 1) * 128, :],
                                   in_=ytm[:, sl])
            ysto = op("sync", f_yst, deps=[ynorm], dma=True, desc=f"y st {i}")
            y_stores.append(ysto)

        def f_fin(q):
            return None
        op("sync", f_fin, deps=[y_stores[-1], y_stores[-2]], desc="fin")

        P.emit(blk, sems)
    return nc


def _host_constants():
    ROPE_BASE = 10000.0
    DR = 64
    inv_freq = 1.0 / ROPE_BASE ** (np.arange(0, DR, 2, dtype=np.float32) / DR)
    t = np.arange(S, dtype=np.float32)
    freqs = np.outer(t, inv_freq)
    cos = np.concatenate([np.cos(freqs), np.cos(freqs)], -1)[:, ::2]
    sin = np.concatenate([np.sin(freqs), np.sin(freqs)], -1)[:, ::2]
    p = np.arange(128)
    tc = cos[:, p % 32].T.astype(np.float32)
    sgn = np.where((p % 64) < 32, -1.0, 1.0).astype(np.float32)
    ts = (sin[:, p % 32] * sgn[None, :]).T.astype(np.float32)
    id32 = np.eye(128, dtype=np.float32)
    perm = np.zeros((128, 128), dtype=np.float32)
    for a in range(128):
        perm[a ^ 32, a] = 1.0
    e1 = np.zeros((128, 128), dtype=np.float32)
    e1[:, 0] = 1.0
    onesr = np.ones((1, 128), dtype=np.float32)
    return np.ascontiguousarray(tc), np.ascontiguousarray(ts), id32, perm, \
        e1, onesr


_DEINT = None


def _deint_cols():
    global _DEINT
    if _DEINT is None:
        idx = []
        for h in range(H):
            idx += [h * 64 + 2 * i for i in range(32)]
            idx += [h * 64 + 2 * i + 1 for i in range(32)]
        _DEINT = np.array(idx)
    return _DEINT


def kernel(L0_attn_x, L0_attn_w_down_q, L0_attn_w_down_kv, L0_attn_w_up_q,
           L0_attn_w_up_k, L0_attn_w_up_v, L0_attn_w_out, L0_attn_ln_scale,
           L0_attn_ln_bias, L0_ffn_W_r, L0_ffn_W_k, L0_ffn_W_v, L0_ffn_mu_r,
           L0_ffn_mu_k):
    x = np.ascontiguousarray(np.asarray(L0_attn_x, dtype=np.float32))
    tc, ts, id32, perm, e1, onesr = _host_constants()
    di = _deint_cols()
    c = np.ascontiguousarray
    f32 = np.float32
    wdqT = c(np.asarray(L0_attn_w_down_q, f32).T)
    wdkvT = c(np.asarray(L0_attn_w_down_kv, f32).T)
    wuqT = c(np.asarray(L0_attn_w_up_q, f32).T[:, di])
    wukT = c(np.asarray(L0_attn_w_up_k, f32).T[:, di])
    wuvT = c(np.asarray(L0_attn_w_up_v, f32).T)
    woutT = c(np.asarray(L0_attn_w_out, f32).T)
    wrT = c(np.asarray(L0_ffn_W_r, f32).T)
    wkT = c(np.asarray(L0_ffn_W_k, f32).T)
    wvT = c(np.asarray(L0_ffn_W_v, f32).T)
    gam = c(np.asarray(L0_attn_ln_scale, f32).reshape(D, 1))
    bet = c(np.asarray(L0_attn_ln_bias, f32).reshape(D, 1))
    mur = c(np.asarray(L0_ffn_mu_r, f32).reshape(D, 1))
    muk = c(np.asarray(L0_ffn_mu_k, f32).reshape(D, 1))

    nc = build_program()
    shared = dict(wdqT=wdqT, wdkvT=wdkvT, wuqT=wuqT, wukT=wukT, wuvT=wuvT,
                  woutT=woutT, wrT=wrT, wkT=wkT, wvT=wvT, gamma=gam, beta=bet,
                  mur=mur, muk=muk, tc=tc, tsn=ts, id32=id32, perm32=perm,
                  e1=e1, onesr=onesr,
                  epsv=np.full((128, 1), EPS, np.float32))
    in_maps = [dict(shared, x=c(x[b])) for b in range(8)]
    res = run_bass_kernel_spmd(nc, in_maps, core_ids=list(range(8)))
    out = np.stack([np.asarray(res.results[b]["y"], dtype=np.float32)
                    for b in range(8)], axis=0)
    return out


# version 11
# speedup vs baseline: 1.2177x; 1.2177x over previous
import numpy as np
from contextlib import ExitStack

import concourse.bass as bass
import concourse.mybir as mybir
from concourse.bass_utils import run_bass_kernel_spmd

F32 = mybir.dt.float32
F16 = mybir.dt.float16
U8 = mybir.dt.uint8
AF = mybir.ActivationFunctionType
OP = mybir.AluOpType
AX = mybir.AxisListType

S, D, H, DH, CL = 2048, 1024, 16, 64, 512
NPAIR, NSQ, SQW, NCH = 8, 4, 512, 16
EPS = 1e-5
INV_D = 1.0 / D
SCALE = 0.125


class _Op:
    __slots__ = ("eng", "fn", "deps", "idx", "dma", "need", "val", "desc")


class Prog:
    ENGS = ("sync", "tensor", "vector", "scalar", "gpsimd")

    def __init__(self):
        self.ops = []
        self.ndma = {}

    def op(self, eng, fn, deps=(), dma=False, n=1, desc=""):
        o = _Op()
        o.eng, o.fn, o.dma, o.desc = eng, fn, dma, desc
        o.idx = len(self.ops)
        o.need, o.val = False, None
        o.deps = []
        for d in deps:
            if d is None:
                continue
            if not isinstance(d, tuple):
                d = (d, False)
            assert d[0].idx < o.idx, f"forward dep at {desc}"
            o.deps.append(d)
        if dma:
            self.ndma[o.idx] = n
        self.ops.append(o)
        return o

    def emit(self, blk, sems):
        for o in self.ops:
            if o.dma:
                o.need = True
            for d, hard in o.deps:
                if d.eng != o.eng or d.dma or o.dma or hard:
                    d.need = True
        cnt = {e: 0 for e in self.ENGS}
        for o in self.ops:
            if o.need:
                cnt[o.eng] += 16 * self.ndma[o.idx] if o.dma else 1
                o.val = cnt[o.eng]
        by_eng = {e: [o for o in self.ops if o.eng == e] for e in self.ENGS}

        def emit_engine(ename, h):
            guar = {e: 0 for e in self.ENGS}
            for o in by_eng[ename]:
                for d, hard in o.deps:
                    if not d.need:
                        continue
                    if d.eng == ename and not (d.dma or o.dma or hard):
                        continue
                    if d.val > guar[d.eng]:
                        h.wait_ge(sems[d.eng].sem, d.val)
                        guar[d.eng] = d.val
                r = o.fn(h)
                if o.need:
                    ct = sems[o.eng]
                    if o.dma:
                        instrs = r if isinstance(r, (list, tuple)) else [r]
                        assert len(instrs) == self.ndma[o.idx], \
                            f"{o.desc}: {len(instrs)} != {self.ndma[o.idx]}"
                        for ins in instrs:
                            ct.inc(ins, 16)
                    else:
                        ins = r[-1] if isinstance(r, (list, tuple)) else r
                        ct.inc(ins, 1)
                    assert ct.n == o.val, f"{o.desc}: sem val mismatch"

        for ename in self.ENGS:
            getattr(blk, ename)(lambda h, _e=ename: emit_engine(_e, h))




def _dma_gather(q, out_ap, ncols, dram_ap, **kw):
    c = dram_ap.shape[0] // 128
    return q.dma_start(
        out=out_ap.rearrange("p (c n) -> p c n", c=c),
        in_=dram_ap.rearrange("(c p) n -> p c n", p=128), **kw)

class Ct:
    __slots__ = ("sem", "n")

    def __init__(self, sem):
        self.sem = sem
        self.n = 0

    def inc(self, instr, k):
        instr.then_inc(self.sem, k)
        self.n += k


def build_program(tap=None):
    nc = bass.Bass("TRN2", target_bir_lowering=False)

    d_in = {}
    for name, shape in [
            ("x", [S, D]), ("wdqT", [D, CL]), ("wdkvT", [D, CL]),
            ("wuqT", [CL, D]), ("wukT", [CL, D]), ("wuvT", [CL, D]),
            ("woutT", [D, D]), ("wrT", [D, D]), ("wkT", [D, D]),
            ("wvT", [D, D]), ("gamma", [D, 1]), ("beta", [D, 1]),
            ("mur", [D, 1]), ("muk", [D, 1]), ("tc", [128, S]),
            ("tsn", [128, S]), ("id32", [128, 128]), ("perm32", [128, 128]),
            ("e1", [128, 128]), ("onesr", [1, 128]), ("epsv", [128, 1])]:
        d_in[name] = nc.dram_tensor(name, shape, F32, kind="ExternalInput").ap()
    y_d = nc.dram_tensor("y", [S, D], F32, kind="ExternalOutput").ap()

    scr = {}
    for name, shape in [
            ("cqh_s", [CL, S]), ("cql_s", [CL, S]),
            ("ckvh_s", [CL, S]), ("ckvl_s", [CL, S]),
            ("wuqh_s", [CL, D]), ("wuql_s", [CL, D]),
            ("wouth_s", [D, D]), ("woutl_s", [D, D]),
            ("wrh_s", [D, D]), ("wrl_s", [D, D]),
            ("wkh_s", [D, D]), ("wkl_s", [D, D]),
            ("wvh_s", [D, D]), ("wvl_s", [D, D]),
            ("anh_s", [D, S]), ("anl_s", [D, S])]:
        scr[name] = nc.dram_tensor(name, shape, F16).ap()

    with ExitStack() as ST:
        sems = {e: Ct(ST.enter_context(nc.semaphore(f"sm_{e}")))
                for e in Prog.ENGS}
        PS = [ST.enter_context(nc.psum_tensor(f"p{i}", [128, 512], F32))
              for i in range(8)]

        TOT = 196352
        backing = ST.enter_context(nc.sbuf_tensor("backing", [128, TOT], U8))
        base_addr = nc.lookup_mloc(backing).addr

        def place(name, shape, dtype, off):
            assert off % 64 == 0, name
            nbytes = int(np.prod(shape[1:])) * mybir.dt.size(dtype)
            assert off + nbytes <= TOT, f"{name} overflow"
            h = nc.alloc_sbuf_tensor_at(name, list(shape), dtype,
                                        offset=base_addr + off)
            return h.ap()

        A0 = 19200
        B0 = A0 + 65536
        C0 = B0 + 66560
        assert C0 + 45056 <= TOT

        tc_t = place("tc_t", [128, S], F32, 0)
        ts_t = place("ts_t", [128, S], F32, 8192)
        id32_t = place("id32_t", [128, 128], F32, 16384)
        perm_t = place("perm_t", [128, 128], F32, 16896)
        e1_t = place("e1_t", [128, 128], F32, 17408)
        onesr_t = place("onesr_t", [1, 128], F32, 17920)
        gam_t = place("gam_t", [128, 8], F32, 18432)
        bet_t = place("bet_t", [128, 8], F32, 18496)
        beth_t = place("beth_t", [128, 8], F16, 18560)
        betl_t = place("betl_t", [128, 8], F16, 18624)
        mur_t = place("mur_t", [128, 8], F32, 18688)
        muk_t = place("muk_t", [128, 8], F32, 18752)
        bq_t = place("bq_t", [128, 4], F32, 18816)
        bkv_t = place("bkv_t", [128, 4], F32, 18880)
        hstat = place("hstat", [128, 16], F32, 18944)
        stash_t = place("stash_t", [128, 8], F32, 19008)
        yst_t = place("yst_t", [128, 8], F32, 19072)
        eps_t = place("eps_t", [128, 1], F32, 19136)

        hh_fm = place("hh_fm", [128, 8 * S], F16, A0)
        hl_fm = place("hl_fm", [128, 8 * S], F16, A0 + 32768)
        kh_t = place("kh_t", [128, 8 * S], F16, A0)
        kl_t = place("kl_t", [128, 8 * S], F16, A0 + 32768)
        x1_t = place("x1_t", [128, 8 * S], F32, A0)

        wbuf = place("wbuf", [128, 2 * D], F32, B0)
        wbh = place("wbh", [128, 2 * D], F16, B0 + 8192)
        wbl = place("wbl", [128, 2 * D], F16, B0 + 12288)
        xtm = place("xtm", [128, 2 * D], F32, B0 + 16384)
        hjunk = place("hjunk", [128, D], F32, B0 + 24576)
        hf32 = place("hf32", [128, D], F32, B0 + 28672)
        hfm32 = place("hfm32", [128, D], F32, B0 + 32768)
        wd32 = place("wd32", [128, 2 * CL], F32, B0 + 36864)
        wdh = place("wdh", [128, 8 * CL], F16, B0 + 40960)
        wdl = place("wdl", [128, 8 * CL], F16, B0 + 49152)
        wdoh = place("wdoh", [128, 8 * CL], F16, B0 + 16384)
        wdol = place("wdol", [128, 8 * CL], F16, B0 + 24576)
        cbuf = place("cbuf", [128, 512], F32, B0 + 57344)
        cbh = place("cbh", [128, 2 * 512], F16, B0 + 59392)
        cbl = place("cbl", [128, 2 * 512], F16, B0 + 61440)
        vh_t = place("vh_t", [128, 16640], F16, B0)
        vl_t = place("vl_t", [128, 16640], F16, B0 + 33280)
        r1_t = place("r1_t", [128, 8 * 512], F32, B0)
        aw_t = place("aw_t", [128, 512], F32, B0 + 16384)
        sqm_t = place("sqm_t", [128, 512], F32, B0 + 18432)
        xtm4 = place("xtm4", [128, 2 * 512], F32, B0 + 20480)
        rowA = place("rowA", [1, 512], F32, B0 + 24576)
        rowB = place("rowB", [1, 512], F32, B0 + 26624)
        rowC = place("rowC", [1, 512], F32, B0 + 28672)
        rowD = place("rowD", [1, 512], F32, B0 + 30720)
        xmh_t = place("xmh_t", [128, 8 * 1024], F16, B0)
        xml_t = place("xml_t", [128, 8 * 1024], F16, B0 + 16384)
        kah_t = place("kah_t", [128, 8 * 1024], F16, B0 + 32768)
        kal_t = place("kal_t", [128, 8 * 1024], F16, B0 + 49152)

        ckv_th = place("ckv_th", [128, 4 * 512], F16, C0)
        ckv_tl = place("ckv_tl", [128, 4 * 512], F16, C0 + 4096)
        wukh = place("wukh", [128, 4 * D], F16, C0 + 8192)
        wukl = place("wukl", [128, 4 * D], F16, C0 + 16384)
        cvsh = place("cvsh", [128, 4 * 128], F16, C0)
        cvsl = place("cvsl", [128, 4 * 128], F16, C0 + 1024)
        wuvh = place("wuvh", [128, 4 * D], F16, C0 + 8192)
        wuvl = place("wuvl", [128, 4 * D], F16, C0 + 16384)
        cqh_t = place("cqh_t", [128, 4 * 512], F16, C0)
        cql_t = place("cql_t", [128, 4 * 512], F16, C0 + 4096)
        wuqh_t = place("wuqh_t", [128, 2 * 512], F16, C0 + 8192)
        wuql_t = place("wuql_t", [128, 2 * 512], F16, C0 + 10240)
        expf = place("expf", [128, 2 * 512], F32, C0 + 12288)
        exph = place("exph", [128, 6 * 512], F16, C0 + 16384)
        expl = place("expl", [128, 6 * 512], F16, C0 + 22528)
        anf_t = place("anf_t", [128, 2 * 512], F32, C0 + 28672)
        an16h = place("an16h", [128, 2 * 512], F16, C0 + 32768)
        an16l = place("an16l", [128, 2 * 512], F16, C0 + 34816)
        qf_t = place("qf_t", [128, 512], F32, C0 + 36864)
        qt1_t = place("qt1_t", [128, 512], F32, C0 + 38912)
        zsc_t = place("zsc_t", [128, 1024], F32, C0 + 12288)
        wu32 = place("wu32", [128, 1024], F32, C0 + 40960)
        qhi_t = place("qhi_t", [128, 512], F16, C0 + 40960)
        qlo_t = place("qlo_t", [128, 512], F16, C0 + 41984)
        wo_th = place("wo_th", [128, 2 * 1024], F16, C0)
        wo_tl = place("wo_tl", [128, 2 * 1024], F16, C0 + 4096)
        anrh = place("anrh", [128, 8 * 512], F16, C0 + 8192)
        anrl = place("anrl", [128, 8 * 512], F16, C0 + 16384)
        wf_h = [place("wf_h0", [128, 2048], F16, C0),
                place("wf_h1", [128, 2048], F16, C0 + 4096)]
        wf_l = [place("wf_l0", [128, 2048], F16, C0 + 8192),
                place("wf_l1", [128, 2048], F16, C0 + 12288)]
        dbuf = place("dbuf", [128, 1024], F32, C0 + 16384)
        xfbuf = place("xfbuf", [128, 1024], F32, C0 + 20480)
        relu_t = place("relu_t", [128, 512], F32, C0 + 24576)
        k2_t = place("k2_t", [128, 512], F32, C0 + 26624)
        rs_t = place("rs_t", [128, 512], F32, C0 + 28672)
        ff_t = place("ff_t", [128, 512], F32, C0 + 30720)
        ytm = place("ytm", [128, 2 * D], F32, C0 + 32768)
        yjunk = place("yjunk", [128, D], F32, C0 + 16384)
        st2junk = place("st2junk", [128, D], F32, C0 + 16384)

        blk = ST.enter_context(nc.Block())
        P = Prog()
        op = P.op

        def f_consts(q):
            r = []
            for nm, t in (("tc", tc_t), ("tsn", ts_t), ("id32", id32_t),
                          ("perm32", perm_t), ("e1", e1_t)):
                r.append(q.dma_start(out=t[:], in_=d_in[nm][:, :]))
            r.append(q.dma_start(out=onesr_t[0:1, :], in_=d_in["onesr"][:, :]))
            r.append(q.dma_start(out=eps_t[:, :], in_=d_in["epsv"][:, :]))
            for nm, t in (("gamma", gam_t), ("beta", bet_t), ("mur", mur_t),
                          ("muk", muk_t)):
                r.append(q.dma_start(
                    out=t[:, :],
                    in_=d_in[nm].rearrange("(c p) o -> p (c o)", p=128),
                    allow_slow_non_contiguous=True))
            return r
        c_ld = op("sync", f_consts, dma=True, n=11, desc="consts")

        def f_bsplit(v):
            v.tensor_copy(beth_t[:], bet_t[:])
            return v.scalar_tensor_tensor(betl_t[:], bet_t[:], 1.0, beth_t[:],
                                          OP.mult, OP.subtract)
        bsplit = op("vector", f_bsplit, deps=[c_ld], desc="beta split")

        wprep = [("wuqT", "wuqh_s", "wuql_s", 4), ("woutT", "wouth_s", "woutl_s", 8),
                 ("wrT", "wrh_s", "wrl_s", 8), ("wkT", "wkh_s", "wkl_s", 8),
                 ("wvT", "wvh_s", "wvl_s", 8)]
        pp_store = [None, None]
        i_pp = 0
        wprep_last = None
        for wname, hs, ls, nch in wprep:
            for c in range(nch):
                half = i_pp % 2
                sl = slice(half * D, half * D + D)

                def f_ld(q, _w=wname, _c=c, _sl=sl):
                    return q.dma_start(out=wbuf[:, _sl],
                                       in_=d_in[_w][_c * 128:(_c + 1) * 128, :])
                ldo = op("sync", f_ld, deps=[pp_store[half]], dma=True,
                         desc=f"wp {wname}{c} ld")

                def f_sp(v, _sl=sl):
                    v.tensor_copy(wbh[:, _sl], wbuf[:, _sl])
                    return v.scalar_tensor_tensor(wbl[:, _sl], wbuf[:, _sl],
                                                  1.0, wbh[:, _sl],
                                                  OP.mult, OP.subtract)
                spo = op("vector", f_sp, deps=[ldo, pp_store[half]],
                         desc=f"wp {wname}{c} split")

                def f_st(g, _h=hs, _l=ls, _c=c, _sl=sl):
                    return [g.dma_start(out=scr[_h][_c * 128:(_c + 1) * 128, :],
                                        in_=wbh[:, _sl]),
                            g.dma_start(out=scr[_l][_c * 128:(_c + 1) * 128, :],
                                        in_=wbl[:, _sl])]
                sto = op("gpsimd", f_st, deps=[spo], dma=True, n=2,
                         desc=f"wp {wname}{c} st")
                pp_store[half] = sto
                wprep_last = sto
                i_pp += 1

        h_ev2 = []
        hn_ops = []
        st1_ops = []
        st3_ops = []
        tp_ops1 = []
        ev1_ops = []

        def emit_stats(i):
            half = i % 2
            xsl = slice(half * D, half * D + D)
            c0 = (i % 2) * 8

            def f_xld(q, _i=i, _sl=xsl):
                return q.dma_start(out=xtm[:, _sl],
                                   in_=d_in["x"][_i * 128:(_i + 1) * 128, :])
            xld = op("sync", f_xld,
                     deps=[hn_ops[i - 2] if i >= 2 else None], dma=True,
                     desc=f"x ld {i}")

            def f_st1a(v, _sl=xsl, _c=c0):
                return v.reduce_sum(hstat[:, _c:_c + 1], xtm[:, _sl],
                                    axis=AX.X)
            st1a = op("vector", f_st1a,
                      deps=[xld, st1_ops[i - 2] if i >= 2 else None],
                      desc=f"ln1 sum {i}")

            def f_st1b(s, _c=c0):
                return s.activation(hstat[:, _c + 1:_c + 2],
                                    hstat[:, _c:_c + 1], AF.Identity,
                                    scale=INV_D)
            st1 = op("scalar", f_st1b, deps=[st1a], desc=f"ln1 mu {i}")
            st1_ops.append(st1)
            return xld, st1

        def emit_st2(i):
            half = i % 2
            xsl = slice(half * D, half * D + D)
            c0 = (i % 2) * 8

            def f_st2a(v, _sl=xsl, _c=c0):
                return v.scalar_tensor_tensor(
                    st2junk[:], xtm[:, _sl], hstat[:, _c + 1:_c + 2],
                    xtm[:, _sl], OP.subtract, OP.mult,
                    accum_out=hstat[:, _c + 2:_c + 3])
            st2a = op("vector", f_st2a, deps=[st1_ops[i]],
                      desc=f"ln1 ssq {i}")

            def f_st3(s, _c=c0):
                s.activation(hstat[:, _c + 3:_c + 4],
                             hstat[:, _c + 2:_c + 3], AF.Identity,
                             scale=INV_D, bias=eps_t[:, 0:1])
                s.activation(hstat[:, _c + 4:_c + 5],
                             hstat[:, _c + 3:_c + 4], AF.Ln)
                return s.activation(hstat[:, _c + 5:_c + 6],
                                    hstat[:, _c + 4:_c + 5], AF.Exp,
                                    scale=-0.5)
            st3 = op("scalar", f_st3, deps=[st2a], desc=f"ln1 rstd {i}")
            st3_ops.append(st3)

        def emit_hn(i):
            half = i % 2
            xsl = slice(half * D, half * D + D)
            c0 = (i % 2) * 8

            def f_hn(v, _sl=xsl, _c=c0):
                return v.tensor_scalar(hf32[:], xtm[:, _sl],
                                       hstat[:, _c + 1:_c + 2],
                                       hstat[:, _c + 5:_c + 6],
                                       OP.subtract, OP.mult)
            hn = op("vector", f_hn,
                    deps=[st3_ops[i], tp_ops1[i - 1] if i >= 1 else None],
                    desc=f"ln1 h {i}")
            hn_ops.append(hn)

            bk0 = (i % 2) * 2

            def f_tp(t, _b=bk0):
                r = []
                for c in range(8):
                    r.append(t.transpose(
                        PS[_b + c // 4][:, (c % 4) * 128:(c % 4 + 1) * 128],
                        hf32[:, c * 128:(c + 1) * 128], id32_t[:]))
                return r
            tp = op("tensor", f_tp,
                    deps=[hn, c_ld, ev1_ops[i - 2] if i >= 2 else None],
                    desc=f"h tp {i}")
            tp_ops1.append(tp)
            return hn, tp

        def emit_ev(i):
            bk0 = (i % 2) * 2

            def f_ev1(v, _b=bk0):
                v.tensor_copy(hfm32[:, 0:512], PS[_b][:, :])
                return v.tensor_copy(hfm32[:, 512:1024], PS[_b + 1][:, :])
            ev1 = op("vector", f_ev1, deps=[tp_ops1[i]], desc=f"h evac {i}")
            ev1_ops.append(ev1)

            def f_ev2(v, _i=i):
                r = []
                for c in range(8):
                    dsl = slice(c * S + _i * 128, c * S + (_i + 1) * 128)
                    ssl = slice(c * 128, (c + 1) * 128)
                    v.tensor_scalar(hjunk[:, ssl], hfm32[:, ssl],
                                    gam_t[:, c:c + 1], bet_t[:, c:c + 1],
                                    OP.mult, OP.add)
                    v.tensor_copy(hh_fm[:, dsl], hjunk[:, ssl])
                    r.append(v.scalar_tensor_tensor(
                        hl_fm[:, dsl], hjunk[:, ssl], 1.0, hh_fm[:, dsl],
                        OP.mult, OP.subtract))
                return r
            ev2 = op("vector", f_ev2, deps=[c_ld], desc=f"h split {i}")
            h_ev2.append(ev2)

        for i in range(18):
            if i < 16:
                emit_stats(i)
            if 1 <= i <= 16:
                emit_hn(i - 1)
            if i < 16:
                emit_st2(i)
            if i >= 2:
                emit_ev(i - 2)

        tapped = []

        def do_tap(deps, srcs):
            outs = []
            for i, (ap_, shape, dt_) in enumerate(srcs):
                outs.append(nc.dram_tensor(f"dbg{i}", list(shape), dt_,
                                           kind="ExternalOutput").ap())

            def f_tap(q, _o=outs, _s=srcs):
                return [q.dma_start(out=o[:, :], in_=s[0][:, :] if s[0].ndim == 2
                                    else s[0][:])
                        for o, s in zip(_o, _s)]
            t_ = op("sync", f_tap, deps=deps, dma=True, n=len(srcs),
                    desc="tap")
            op("sync", lambda q: None, deps=[t_], desc="tapfin")
            tapped.append(True)

        if tap == "h":
            do_tap([h_ev2[15]], [(hh_fm, [128, 8 * S], F16),
                                 (hl_fm, [128, 8 * S], F16)])
            P.emit(blk, sems)
            return nc

        c_stores = []
        bias_ev = {}
        wd_splits = {}
        for wi, (wname, hs_c, ls_c) in enumerate(
                [("wdqT", "cqh_s", "cql_s"), ("wdkvT", "ckvh_s", "ckvl_s")]):
            wsp_list = []
            for c in range(8):
                half = c % 2
                sl32 = slice(half * CL, half * CL + CL)
                slc = slice(c * CL, (c + 1) * CL)

                def f_wld(q, _w=wname, _c=c, _sl=sl32):
                    return q.dma_start(out=wd32[:, _sl],
                                       in_=d_in[_w][_c * 128:(_c + 1) * 128, :])
                dep_reuse = wsp_list[c - 2] if c >= 2 else \
                    (wd_splits.get(wi - 1, [None] * 8)[6 + half] if wi > 0 else None)
                extra = []
                if wi == 0 and c < 2:
                    extra = [hn_ops[15], h_ev2[15]]
                wld = op("sync", f_wld, deps=[dep_reuse] + extra, dma=True,
                         desc=f"{wname} ld {c}")

                def f_wsp(v, _sl=sl32, _c=slc):
                    v.tensor_copy(wdh[:, _c], wd32[:, _sl])
                    return v.scalar_tensor_tensor(wdl[:, _c], wd32[:, _sl],
                                                  1.0, wdh[:, _c],
                                                  OP.mult, OP.subtract)
                dep_sp_extra = []
                if wi == 0 and c < 2:
                    dep_sp_extra = [hn_ops[15], h_ev2[15]]
                wsp = op("vector", f_wsp,
                         deps=[wld, c_ld] + dep_sp_extra, desc=f"{wname} sp {c}")
                wsp_list.append(wsp)
            wd_splits[wi] = wsp_list

            for co in range(4):
                for nt in range(4):
                    ti = co * 4 + nt
                    bank = ti % 2

                    def f_cmm(t, _co=co, _nt=nt, _b=bank):
                        r = []
                        for kc in range(8):
                            for p in range(3):
                                lhs = [wdh, wdh, wdl][p][
                                    :, kc * CL + _co * 128:
                                    kc * CL + _co * 128 + 128]
                                rhs = [hh_fm, hl_fm, hh_fm][p][
                                    :, kc * S + _nt * 512:
                                    kc * S + _nt * 512 + 512]
                                r.append(t.matmul(
                                    PS[_b][:, :], lhs, rhs,
                                    start=(kc == 0 and p == 0),
                                    stop=(kc == 7 and p == 2)))
                        return r
                    prev_ev = c_stores[-2][0] if len(c_stores) >= 2 else None
                    dep_p1 = [h_ev2[15]] if ti == 0 and wi == 0 else []
                    dep_ps = [ev1_ops[-1]] if ti < 2 and wi == 0 else []
                    cmm = op("tensor", f_cmm,
                             deps=[wsp_list[-1], prev_ev] + dep_p1 + dep_ps,
                             desc=f"c mm {wi}.{co}.{nt}")

                    half2 = ti % 2

                    def f_cev(v, _b=bank, _h2=half2):
                        sl = slice(_h2 * 512, _h2 * 512 + 512)
                        v.tensor_copy(cbuf[:], PS[_b][:, :])
                        v.tensor_copy(cbh[:, sl], cbuf[:])
                        return v.scalar_tensor_tensor(cbl[:, sl], cbuf[:], 1.0,
                                                      cbh[:, sl], OP.mult,
                                                      OP.subtract)
                    prev_st = c_stores[-2][1] if len(c_stores) >= 2 else None
                    cev = op("vector", f_cev,
                             deps=[cmm, prev_st],
                             desc=f"c ev {wi}.{co}.{nt}")

                    def f_cst(g, _h=hs_c, _l=ls_c, _co=co, _nt=nt, _h2=half2):
                        sl = slice(_h2 * 512, _h2 * 512 + 512)
                        return [
                            g.dma_start(out=scr[_h][_co * 128:(_co + 1) * 128,
                                                    _nt * 512:(_nt + 1) * 512],
                                        in_=cbh[:, sl]),
                            g.dma_start(out=scr[_l][_co * 128:(_co + 1) * 128,
                                                    _nt * 512:(_nt + 1) * 512],
                                        in_=cbl[:, sl])]
                    cst = op("gpsimd", f_cst, deps=[cev], dma=True, n=2,
                             desc=f"c st {wi}.{co}.{nt}")
                    c_stores.append((cev, cst))
        p2_last_cmm_dep = c_stores[-1][0]
        cq_all_stored = c_stores[15][1]
        ckv_all_stored = c_stores[-1][1]

        if tap == "c":
            do_tap([cq_all_stored, ckv_all_stored],
                   [(scr["cqh_s"], [CL, S], F16), (scr["cql_s"], [CL, S], F16),
                    (scr["ckvh_s"], [CL, S], F16),
                    (scr["ckvl_s"], [CL, S], F16),
                    (hh_fm, [128, 8 * S], F16), (hl_fm, [128, 8 * S], F16),
                    (wdh, [128, 8 * CL], F16), (wdl, [128, 8 * CL], F16)])
            P.emit(blk, sems)
            return nc

        wuk_sp = []
        for c in range(4):
            slc = slice(c * D, (c + 1) * D)

            def f_ld(q, _c=c):
                return q.dma_start(out=wu32[:, :],
                                   in_=d_in["wukT"][_c * 128:(_c + 1) * 128, :])
            ldo = op("sync", f_ld,
                     deps=[wuk_sp[c - 1] if c >= 1 else None,
                           p2_last_cmm_dep if c == 0 else None],
                     dma=True, desc=f"wuk ld {c}")

            def f_sp(v, _c=slc):
                v.tensor_copy(wukh[:, _c], wu32[:, :])
                return v.scalar_tensor_tensor(wukl[:, _c], wu32[:, :], 1.0,
                                              wukh[:, _c], OP.mult,
                                              OP.subtract)
            spo = op("vector", f_sp, deps=[ldo], desc=f"wuk sp {c}")
            wuk_sp.append(spo)

        k_written = []
        prev_kev = None
        prev_t2 = None
        prev_t2s = [None, None]
        prev_kevs = [None, None]
        for nt in range(4):
            def f_ckvld(q, _nt=nt):
                return [q.dma_start(
                            out=ckv_th[:, :].rearrange("p (c n) -> p c n", c=4),
                            in_=scr["ckvh_s"][:, _nt * 512:(_nt + 1) * 512]
                            .rearrange("(c p) n -> p c n", p=128)),
                        q.dma_start(
                            out=ckv_tl[:, :].rearrange("p (c n) -> p c n", c=4),
                            in_=scr["ckvl_s"][:, _nt * 512:(_nt + 1) * 512]
                            .rearrange("(c p) n -> p c n", p=128))]
            prev_mm_dep = k_written[-1] if k_written else None
            ckld = op("sync", f_ckvld,
                      deps=[ckv_all_stored, prev_mm_dep], dma=True, n=2,
                      desc=f"ckv ld {nt}")
            for pr in range(8):
                kbank = [2, 4][pr % 2]

                def f_kmm(t, _pr=pr, _kb=kbank):
                    r = []
                    for kc in range(4):
                        for p in range(3):
                            lhs = [wukh, wukh, wukl][p][
                                :, kc * D + _pr * 128: kc * D + _pr * 128 + 128]
                            rhs = [ckv_th, ckv_tl, ckv_th][p][
                                :, kc * 512:(kc + 1) * 512]
                            r.append(t.matmul(PS[_kb][:, :], lhs, rhs,
                                              start=(kc == 0 and p == 0),
                                              stop=(kc == 3 and p == 2)))
                    return r
                kmm = op("tensor", f_kmm,
                         deps=[ckld, wuk_sp[-1], prev_kevs[pr % 2],
                               p2_last_cmm_dep
                               if nt == 0 and pr == 0 else None],
                         desc=f"k mm {nt}.{pr}")

                def f_kev(s, _kb=kbank):
                    return s.copy(qf_t[:], PS[_kb][:, :])
                kev = op("scalar", f_kev, deps=[kmm, prev_t2],
                         desc=f"k ev {nt}.{pr}")
                prev_kev = kev
                prev_kevs[pr % 2] = kev

                pbank = [3, 5][pr % 2]

                def f_kpm(t, _pb=pbank):
                    return t.matmul(PS[_pb][:, :], perm_t[:], qf_t[:],
                                    start=True, stop=True)
                kpm = op("tensor", f_kpm, deps=[kev, prev_t2s[pr % 2]],
                         desc=f"k pm {nt}.{pr}")

                def f_krope(v, _nt=nt, _pb=pbank):
                    tsl = slice(_nt * 512, (_nt + 1) * 512)
                    v.tensor_mul(qt1_t[:], qf_t[:], tc_t[:, tsl])
                    v.tensor_mul(qf_t[:], PS[_pb][:, :], ts_t[:, tsl])
                    return v.tensor_add(qt1_t[:], qt1_t[:], qf_t[:])
                krope = op("vector", f_krope, deps=[kpm, kev],
                           desc=f"k rope {nt}.{pr}")
                prev_t2 = krope
                prev_t2s[pr % 2] = krope

                def f_ksp(v, _nt=nt, _pr=pr):
                    dsl = slice(_pr * S + _nt * 512, _pr * S + (_nt + 1) * 512)
                    v.tensor_copy(kh_t[:, dsl], qt1_t[:])
                    return v.scalar_tensor_tensor(kl_t[:, dsl], qt1_t[:], 1.0,
                                                  kh_t[:, dsl], OP.mult,
                                                  OP.subtract)
                ksp = op("vector", f_ksp, deps=[], desc=f"k sp {nt}.{pr}")
                k_written.append(ksp)
        k_done = k_written[-1]

        wuv_sp = []
        for c in range(4):
            slc = slice(c * D, (c + 1) * D)

            def f_ld(q, _c=c):
                return q.dma_start(out=wu32[:, :],
                                   in_=d_in["wuvT"][_c * 128:(_c + 1) * 128, :])
            ldo = op("sync", f_ld,
                     deps=[wuk_sp[-1] if c == 0 else wuv_sp[c - 1],
                           k_done if c == 0 else None],
                     dma=True, desc=f"wuv ld {c}")

            def f_sp(v, _c=slc):
                v.tensor_copy(wuvh[:, _c], wu32[:, :])
                return v.scalar_tensor_tensor(wuvl[:, _c], wu32[:, :], 1.0,
                                              wuvh[:, _c], OP.mult,
                                              OP.subtract)
            spo = op("vector", f_sp, deps=[ldo, k_done], desc=f"wuv sp {c}")
            wuv_sp.append(spo)

        def f_vones(g):
            g.memset(vh_t[:, 0:16640].rearrange("p (k h c) -> p k h c",
                                                k=16, h=16)[:, :, :, 64:65], 1.0)
            return g.memset(vl_t[:, 0:16640].rearrange("p (k h c) -> p k h c",
                                                       k=16, h=16)
                            [:, :, :, 64:65], 0.0)
        vones = op("gpsimd", f_vones,
                   deps=[ckv_all_stored, wprep_last, h_ev2[15]],
                   desc="v ones")

        v_written = []
        prev_vev = [None, None]
        prev_vsp = [None, None]
        for sk in range(16):
            def f_cvld(q, _sk=sk):
                return [q.dma_start(
                            out=cvsh[:, :].rearrange("p (c n) -> p c n", c=4),
                            in_=scr["ckvh_s"][:, _sk * 128:(_sk + 1) * 128]
                            .rearrange("(c p) n -> p c n", p=128)),
                        q.dma_start(
                            out=cvsl[:, :].rearrange("p (c n) -> p c n", c=4),
                            in_=scr["ckvl_s"][:, _sk * 128:(_sk + 1) * 128]
                            .rearrange("(c p) n -> p c n", p=128))]
            prev_dep = prev_vev[1] if sk > 0 else k_written[-1]
            cvld = op("sync", f_cvld, deps=[ckv_all_stored, prev_dep],
                      dma=True, n=2, desc=f"cv ld {sk}")
            for ntv in range(2):
                bank = 4 + ntv

                def f_vmm(t, _ntv=ntv, _b=bank):
                    r = []
                    for kc in range(4):
                        for p in range(3):
                            lhs = [cvsh, cvsh, cvsl][p][:, kc * 128:(kc + 1) * 128]
                            rhs = [wuvh, wuvl, wuvh][p][
                                :, kc * D + _ntv * 512: kc * D + _ntv * 512 + 512]
                            r.append(t.matmul(PS[_b][:, :], lhs, rhs,
                                              start=(kc == 0 and p == 0),
                                              stop=(kc == 3 and p == 2)))
                    return r
                vmm = op("tensor", f_vmm,
                         deps=[cvld, wuv_sp[-1], prev_vev[ntv]],
                         desc=f"v mm {sk}.{ntv}")

                def f_vev(s, _b=bank, _ntv=ntv):
                    dst = anf_t[:, 0:512] if _ntv == 0 else anf_t[:, 512:1024]
                    return s.copy(dst[:], PS[_b][:, :])
                vev = op("scalar", f_vev,
                         deps=[vmm, k_written[-1] if sk == 0 else None,
                               prev_vsp[ntv]],
                         desc=f"v ev {sk}.{ntv}")
                prev_vev[ntv] = vev

                def f_vsp(v, _sk=sk, _ntv=ntv):
                    src = anf_t[:, 0:512] if _ntv == 0 else anf_t[:, 512:1024]
                    r = []
                    for hh in range(8):
                        hd = _ntv * 8 + hh
                        dsl = slice(_sk * 1040 + hd * 65,
                                    _sk * 1040 + hd * 65 + 64)
                        ssl = slice(hh * 64, (hh + 1) * 64)
                        v.tensor_copy(vh_t[:, dsl], src[:, ssl])
                    for hh in range(8):
                        hd = _ntv * 8 + hh
                        dsl = slice(_sk * 1040 + hd * 65,
                                    _sk * 1040 + hd * 65 + 64)
                        ssl = slice(hh * 64, (hh + 1) * 64)
                        r.append(v.scalar_tensor_tensor(
                            vl_t[:, dsl], src[:, ssl], 1.0, vh_t[:, dsl],
                            OP.mult, OP.subtract))
                    return r
                vsp = op("vector", f_vsp, deps=[vev, vones],
                         desc=f"v sp {sk}.{ntv}")
                prev_vsp[ntv] = vsp
                v_written.append(vsp)
        v_done = v_written[-1]

        if tap == "kv":
            do_tap([k_done, v_done],
                   [(kh_t, [128, 8 * S], F16), (kl_t, [128, 8 * S], F16),
                    (vh_t, [128, 16640], F16), (vl_t, [128, 16640], F16)])
            P.emit(blk, sems)
            return nc

        an_spills = []
        blk_tail = {"qev": None, "zread": None, "bcast": None, "anorm": None,
                    "sc": {}, "ex": {}, "sp": {}, "av": {}}
        prev_block = {}
        for sq in range(NSQ):
            def f_cqld(q, _sq=sq):
                return [q.dma_start(
                            out=cqh_t[:, :].rearrange("p (c n) -> p c n", c=4),
                            in_=scr["cqh_s"][:, _sq * 512:(_sq + 1) * 512]
                            .rearrange("(c p) n -> p c n", p=128)),
                        q.dma_start(
                            out=cql_t[:, :].rearrange("p (c n) -> p c n", c=4),
                            in_=scr["cql_s"][:, _sq * 512:(_sq + 1) * 512]
                            .rearrange("(c p) n -> p c n", p=128))]
            dep0 = [cq_all_stored]
            if sq == 0:
                dep0 += [v_written[-1], k_written[-1]]
            else:
                dep0 += [prev_block["last_sc"]]
            cqld = op("sync", f_cqld, deps=dep0, dma=True, n=2,
                      desc=f"cq ld {sq}")

            for pr in range(NPAIR):
                b = sq * NPAIR + pr
                ring = b % 2

                def f_wuqld(g, _pr=pr, _r=ring):
                    sl = slice(_r * 512, _r * 512 + 512)
                    return [g.dma_start(
                                out=wuqh_t[:, sl].rearrange("p (c n) -> p c n", c=4),
                                in_=scr["wuqh_s"][:, _pr * 128:(_pr + 1) * 128]
                                .rearrange("(c p) n -> p c n", p=128)),
                            g.dma_start(
                                out=wuql_t[:, sl].rearrange("p (c n) -> p c n", c=4),
                                in_=scr["wuql_s"][:, _pr * 128:(_pr + 1) * 128]
                                .rearrange("(c p) n -> p c n", p=128))]
                wuqld = op("sync", f_wuqld,
                           deps=[wprep_last if b == 0 else None,
                                 prev_block.get("qmm2")],
                           dma=True, n=2, desc=f"wuq ld {b}")

                def f_qmm(t, _r=ring):
                    r = []
                    for kc in range(4):
                        for p in range(3):
                            lhs = [wuqh_t, wuqh_t, wuql_t][p][
                                :, _r * 512 + kc * 128: _r * 512 + kc * 128 + 128]
                            rhs = [cqh_t, cql_t, cqh_t][p][
                                :, kc * 512:(kc + 1) * 512]
                            r.append(t.matmul(PS[0][:, :], lhs, rhs,
                                              start=(kc == 0 and p == 0),
                                              stop=(kc == 3 and p == 2)))
                    return r
                qmm = op("tensor", f_qmm,
                         deps=[cqld, wuqld, prev_block.get("qev"),
                               prev_block.get(("ex", 14)),
                               v_done if b == 0 else None],
                         desc=f"q mm {b}")
                prev_block["qmm2"] = qmm

                def f_qev(s):
                    return s.copy(qf_t[:], PS[0][:, :])
                qev = op("scalar", f_qev,
                         deps=[qmm, prev_block.get("rope"),
                               prev_block.get("zbc")],
                         desc=f"q ev {b}")
                prev_block["qev"] = qev

                def f_qpm(t):
                    return t.matmul(PS[1][:, :], perm_t[:], qf_t[:],
                                    start=True, stop=True)
                qpm = op("tensor", f_qpm,
                         deps=[qev, prev_block.get("rope"),
                               prev_block.get(("ex", 14)),
                               wuv_sp[-1] if b == 0 else None],
                         desc=f"q pm {b}")

                def f_qrope(v, _sq=sq):
                    tsl = slice(_sq * 512, (_sq + 1) * 512)
                    v.tensor_mul(qt1_t[:], qf_t[:], tc_t[:, tsl])
                    v.tensor_mul(qf_t[:], PS[1][:, :], ts_t[:, tsl])
                    v.tensor_add(qt1_t[:], qt1_t[:], qf_t[:])
                    v.tensor_copy(qhi_t[:], qt1_t[:])
                    return v.scalar_tensor_tensor(qlo_t[:], qt1_t[:], 1.0,
                                                  qhi_t[:], OP.mult,
                                                  OP.subtract)
                qrope = op("vector", f_qrope,
                           deps=[qpm, qev, prev_block.get("anorm")],
                           desc=f"q rope {b}")
                prev_block["rope"] = qrope

                if tap == "q0" and b == 0:
                    do_tap([qrope, qev, wuqld],
                           [(qt1_t, [128, 512], F32),
                            (qf_t, [128, 512], F32),
                            (cqh_t, [128, 4 * 512], F16),
                            (cql_t, [128, 4 * 512], F16),
                            (wuqh_t, [128, 2 * 512], F16),
                            (wuql_t, [128, 2 * 512], F16),
                            (scr["wuqh_s"], [CL, D], F16),
                            (scr["wuql_s"], [CL, D], F16)])
                    P.emit(blk, sems)
                    return nc

                sc_ops = {}
                ex_ops = {}
                sp_ops = {}
                av_ops = {}

                SCBK = [(2, 3), (4, 5), (0, 1)]

                def mk_sc(j):
                    def f_sc(t, _pr=pr, _j=j):
                        bk = SCBK[_j % 3]
                        r = []
                        for hh in range(2):
                            rr = slice(hh * 64, hh * 64 + 64)
                            tp = (hh * 64, 0)
                            ksl = slice(_pr * S + _j * 128,
                                        _pr * S + _j * 128 + 128)
                            for p in range(3):
                                lhs = [kh_t, kh_t, kl_t][p][rr, ksl]
                                rhs = [qhi_t, qlo_t, qhi_t][p][rr, :]
                                r.append(t.matmul(PS[bk[hh]][:, :], lhs, rhs,
                                                  start=(p == 0), stop=(p == 2),
                                                  tile_position=tp))
                        return r
                    return f_sc

                def mk_ex2(j):
                    def f_ex(s, _j=j):
                        bk0, bk1 = SCBK[_j % 3]
                        rg = _j % 2
                        s.activation(expf[:, rg * 512:rg * 512 + 512],
                                     PS[bk0][:, :], AF.Exp, scale=SCALE)
                        return s.activation(anf_t[:, rg * 512:rg * 512 + 512],
                                            PS[bk1][:, :], AF.Exp, scale=SCALE)
                    return f_ex

                def mk_sp(j):
                    def f_sp(v, _j=j):
                        rg = _j % 2
                        r3 = _j % 3
                        r = []
                        for hh, srct in ((0, expf), (1, anf_t)):
                            ssl = slice(rg * 512, rg * 512 + 512)
                            dsl = slice((hh * 3 + r3) * 512,
                                        (hh * 3 + r3) * 512 + 512)
                            v.tensor_copy(exph[:, dsl], srct[:, ssl])
                            r.append(v.scalar_tensor_tensor(
                                expl[:, dsl], srct[:, ssl], 1.0, exph[:, dsl],
                                OP.mult, OP.subtract))
                        return r
                    return f_sp

                def mk_av(j):
                    def f_av(t, _pr=pr, _j=j):
                        r3 = _j % 3
                        r = []
                        for hh in range(2):
                            hd = _pr * 2 + hh
                            vsl = slice(_j * 1040 + hd * 65,
                                        _j * 1040 + hd * 65 + 65)
                            esl = slice((hh * 3 + r3) * 512,
                                        (hh * 3 + r3) * 512 + 512)
                            for p in range(3):
                                lhs = [vh_t, vh_t, vl_t][p][:, vsl]
                                rhs = [exph, expl, exph][p][:, esl]
                                r.append(t.matmul(
                                    PS[6 + hh][0:65, :], lhs, rhs,
                                    start=(_j == 0 and p == 0),
                                    stop=(_j == 15 and p == 2)))
                        return r
                    return f_av

                def mk_sc_deps(j):
                    deps_sc = [qrope]
                    if j >= 3:
                        deps_sc.append(ex_ops[j - 3])
                    else:
                        deps_sc.append(prev_block.get(("ex", [15, 13, 14][j])))
                    if j == 0:
                        deps_sc.append(prev_block.get("anorm"))
                    if b == 0 and j == 0:
                        deps_sc += [prev_kev, prev_t2]
                    if b == 0 and j == 1:
                        deps_sc += [prev_vev[0], prev_vev[1]]
                    return deps_sc

                def mk_av_deps(j):
                    deps_av = [sp_ops[j]]
                    if j == 0:
                        deps_av.append(prev_block.get("ansc"))
                        deps_av.append(prev_block.get("anorm"))
                        if b == 0:
                            deps_av.append(v_done)
                    return deps_av

                def emit_exsp(j):
                    deps_ex = [sc_ops[j]]
                    if j >= 2:
                        deps_ex.append(sp_ops[j - 2])
                    else:
                        deps_ex.append(prev_block.get(("sp", 14 + j)))
                    if j < 2:
                        deps_ex.append(prev_block.get("anorm"))
                    ex_ops[j] = op("scalar", mk_ex2(j), deps=deps_ex,
                                   desc=f"ex {b}.{j}")
                    deps_sp = [ex_ops[j]]
                    if j >= 3:
                        deps_sp.append(av_ops[j - 3])
                    else:
                        deps_sp.append(prev_block.get(("av", 13 + j)))
                    sp_ops[j] = op("vector", mk_sp(j), deps=deps_sp,
                                   desc=f"sp {b}.{j}")

                for j in range(NCH):
                    sc_ops[j] = op("tensor", mk_sc(j), deps=mk_sc_deps(j),
                                   desc=f"sc {b}.{j}")
                    emit_exsp(j)
                    if j >= 2:
                        jj = j - 2
                        av_ops[jj] = op("tensor", mk_av(jj),
                                        deps=mk_av_deps(jj),
                                        desc=f"av {b}.{jj}")
                for jj in (14, 15):
                    av_ops[jj] = op("tensor", mk_av(jj), deps=mk_av_deps(jj),
                                    desc=f"av {b}.{jj}")

                def f_zcp(s):
                    s.copy(zsc_t[64:65, 0:512], PS[6][64:65, :])
                    return s.copy(zsc_t[64:65, 512:1024], PS[7][64:65, :])
                zcp = op("scalar", f_zcp,
                         deps=[av_ops[15], qrope], desc=f"z cp {b}")

                def f_zrc(v):
                    return v.reciprocal(zsc_t[0:1, 0:1024], zsc_t[64:65, 0:1024])
                zrc = op("vector", f_zrc, deps=[zcp], desc=f"z rc {b}")

                def f_zbc(t):
                    t.matmul(PS[2][:, :], onesr_t[0:1, :], zsc_t[0:1, 0:512],
                             start=True, stop=True)
                    return t.matmul(PS[3][:, :], onesr_t[0:1, :],
                                    zsc_t[0:1, 512:1024], start=True, stop=True)
                zbc = op("tensor", f_zbc,
                         deps=[zrc, ex_ops[15], av_ops[15]], desc=f"z bc {b}")
                prev_block["zbc"] = zbc

                def f_ansc(s):
                    s.copy(anf_t[0:64, 0:512], PS[6][0:64, :])
                    return s.copy(anf_t[0:64, 512:1024], PS[7][0:64, :])
                ansc = op("scalar", f_ansc,
                          deps=[av_ops[15], sp_ops[15], zrc], desc=f"an sc {b}")
                prev_block["ansc"] = ansc

                def f_anorm(v):
                    r = []
                    for hh in range(2):
                        asl = slice(hh * 512, hh * 512 + 512)
                        v.tensor_mul(anf_t[0:64, asl], anf_t[0:64, asl],
                                     PS[2 + hh][0:64, :])
                        v.tensor_copy(an16h[0:64, asl], anf_t[0:64, asl])
                        r.append(v.scalar_tensor_tensor(
                            an16l[0:64, asl], anf_t[0:64, asl], 1.0,
                            an16h[0:64, asl], OP.mult, OP.subtract))
                    return r
                anorm = op("vector", f_anorm,
                           deps=[ansc, zbc,
                                 an_spills[-1] if an_spills else None],
                           desc=f"an norm {b}")
                prev_block["anorm"] = anorm

                def f_anst(g, _sq=sq, _pr=pr):
                    r = []
                    for hh in range(2):
                        asl = slice(hh * 512, hh * 512 + 512)
                        rr = slice(_pr * 128 + hh * 64, _pr * 128 + hh * 64 + 64)
                        cc = slice(_sq * 512, (_sq + 1) * 512)
                        r.append(g.dma_start(out=scr["anh_s"][rr, cc],
                                             in_=an16h[0:64, asl]))
                        r.append(g.dma_start(out=scr["anl_s"][rr, cc],
                                             in_=an16l[0:64, asl]))
                    return r
                anst = op("gpsimd", f_anst, deps=[anorm], dma=True, n=4,
                          desc=f"an st {b}")
                an_spills.append(anst)
                prev_block["anst"] = anst

                prev_block["last_sc"] = sc_ops[15]
                for j in range(NCH):
                    prev_block[("ex", j)] = ex_ops[j]
                    prev_block[("sp", j)] = sp_ops[j]
                    prev_block[("av", j)] = av_ops[j]
        attn_done = an_spills[-1]
        attn_done_compute = prev_block["anorm"]

        if tap == "an":
            do_tap([attn_done], [(scr["anh_s"], [D, S], F16),
                                 (scr["anl_s"], [D, S], F16)])
            P.emit(blk, sems)
            return nc

        x1_written = []
        prev_sq_norm = None
        prev_wout_mms = None
        for sq in range(NSQ):
            def f_anld(q, _sq=sq):
                cc = slice(_sq * 512, (_sq + 1) * 512)
                return [q.dma_start(out=anrh[:, :].rearrange("p (c n) -> p c n", c=8),
                                    in_=scr["anh_s"][:, cc]
                                    .rearrange("(c p) n -> p c n", p=128)),
                        q.dma_start(out=anrl[:, :].rearrange("p (c n) -> p c n", c=8),
                                    in_=scr["anl_s"][:, cc]
                                    .rearrange("(c p) n -> p c n", p=128))]
            anld = op("sync", f_anld,
                      deps=[an_spills[(sq + 1) * NPAIR - 1],
                            an_spills[-1] if sq == 0 else None,
                            prev_sq_norm],
                      dma=True, n=2, desc=f"an ld {sq}")

            wout_mms = []
            aw_ops = []
            r1_ops = []
            tp_ops = []
            for m in range(8):
                ring = m % 2

                def f_wold(g, _m=m, _r=ring):
                    sl = slice(_r * 1024, _r * 1024 + 1024)
                    return [g.dma_start(
                                out=wo_th[:, sl].rearrange("p (c n) -> p c n", c=8),
                                in_=scr["wouth_s"][:, _m * 128:(_m + 1) * 128]
                                .rearrange("(c p) n -> p c n", p=128)),
                            g.dma_start(
                                out=wo_tl[:, sl].rearrange("p (c n) -> p c n", c=8),
                                in_=scr["woutl_s"][:, _m * 128:(_m + 1) * 128]
                                .rearrange("(c p) n -> p c n", p=128))]
                if m >= 2:
                    dep_wo = [wout_mms[m - 2]]
                elif sq == 0:
                    dep_wo = [attn_done, attn_done_compute]
                else:
                    dep_wo = [prev_wout_mms[6 + m]]
                wold = op("sync", f_wold, deps=dep_wo,
                          dma=True, n=2, desc=f"wo ld {sq}.{m}")

                def f_womm(t, _m=m, _r=ring):
                    r = []
                    for kc in range(8):
                        for p in range(3):
                            lhs = [wo_th, wo_th, wo_tl][p][
                                :, _r * 1024 + kc * 128:
                                _r * 1024 + kc * 128 + 128]
                            rhs = [anrh, anrl, anrh][p][
                                :, kc * 512:(kc + 1) * 512]
                            r.append(t.matmul(PS[_m][:, :], lhs, rhs,
                                              start=(kc == 0 and p == 0),
                                              stop=(kc == 7 and p == 2)))
                    return r
                dep_ps = [attn_done_compute] if sq == 0 else [prev_sq_norm]
                womm = op("tensor", f_womm,
                          deps=[anld, wold] + dep_ps, desc=f"wo mm {sq}.{m}")
                wout_mms.append(womm)

                def f_aw(s, _m=m):
                    return s.copy(aw_t[:], PS[_m][:, :])
                awo = op("scalar", f_aw,
                         deps=[womm, r1_ops[m - 1] if m >= 1 else None],
                         desc=f"aw {sq}.{m}")
                aw_ops.append(awo)

                def f_xld(q, _sq=sq, _m=m, _r=ring):
                    sl = slice(_r * 512, _r * 512 + 512)
                    r = []
                    for i in range(4):
                        r.append(q.dma_start(
                            out=xtm4[:, _r * 512 + i * 128:
                                     _r * 512 + (i + 1) * 128],
                            in_=d_in["x"][_sq * 512 + i * 128:
                                          _sq * 512 + (i + 1) * 128,
                                          _m * 128:(_m + 1) * 128]))
                    return r
                xld4 = op("sync", f_xld,
                          deps=[anld, tp_ops[m - 2] if m >= 2 else
                                (prev_sq_norm if sq > 0 else None)],
                          dma=True, n=4, desc=f"x4 ld {sq}.{m}")

                def f_xtp(t, _m=m, _r=ring):
                    r = []
                    for i in range(4):
                        r.append(t.transpose(
                            PS[_m][:, i * 128:(i + 1) * 128],
                            xtm4[:, _r * 512 + i * 128:_r * 512 + (i + 1) * 128],
                            id32_t[:]))
                    return r
                xtp = op("tensor", f_xtp, deps=[xld4, awo],
                         desc=f"x tp {sq}.{m}")
                tp_ops.append(xtp)

                def f_r1(v, _m=m):
                    return v.tensor_add(r1_t[:, _m * 512:(_m + 1) * 512],
                                        aw_t[:], PS[_m][:, :])
                dep_r1 = [xtp, awo]
                if sq == 0 and m == 0:
                    dep_r1.append(attn_done_compute)
                r1o = op("vector", f_r1, deps=dep_r1, desc=f"r1 {sq}.{m}")
                r1_ops.append(r1o)

            st_mms = []
            sq_ops = []
            for m in range(8):
                def f_sqm(v, _m=m):
                    return v.tensor_mul(sqm_t[:], r1_t[:, _m * 512:(_m + 1) * 512],
                                        r1_t[:, _m * 512:(_m + 1) * 512])
                sqo = op("vector", f_sqm,
                         deps=[r1_ops[m],
                               st_mms[m - 1] if m >= 1 else None],
                         desc=f"sqm {sq}.{m}")
                sq_ops.append(sqo)

                def f_stm(t, _m=m):
                    a = t.matmul(PS[0][:, :], e1_t[:],
                                 r1_t[:, _m * 512:(_m + 1) * 512],
                                 start=(_m == 0), stop=(_m == 7))
                    bmm_ = t.matmul(PS[1][:, :], e1_t[:], sqm_t[:],
                                    start=(_m == 0), stop=(_m == 7))
                    return bmm_
                stm = op("tensor", f_stm,
                         deps=[sqo, r1_ops[m],
                               r1_ops[0] if m == 0 else None,
                               r1_ops[1] if m == 0 else None],
                         desc=f"st mm {sq}.{m}")
                st_mms.append(stm)

            def f_rows1(s):
                s.activation(rowA[0:1, :], PS[0][0:1, :], AF.Identity,
                             scale=INV_D)
                return s.activation(rowB[0:1, :], PS[1][0:1, :], AF.Identity,
                                    scale=INV_D)
            rows1 = op("scalar", f_rows1, deps=[st_mms[-1]],
                       desc=f"rows1 {sq}")

            def f_rows2(v):
                v.tensor_mul(rowC[0:1, :], rowA[0:1, :], rowA[0:1, :])
                v.tensor_sub(rowD[0:1, :], rowB[0:1, :], rowC[0:1, :])
                return v.tensor_scalar(rowD[0:1, :], rowD[0:1, :], EPS, None,
                                       OP.add)
            rows2 = op("vector", f_rows2, deps=[rows1], desc=f"rows2 {sq}")

            def f_rows3(s):
                s.activation(rowC[0:1, :], rowD[0:1, :], AF.Ln)
                return s.activation(rowC[0:1, :], rowC[0:1, :], AF.Exp,
                                    scale=-0.5)
            rows3 = op("scalar", f_rows3, deps=[rows2], desc=f"rows3 {sq}")

            def f_bc2(t):
                t.matmul(PS[2][:, :], onesr_t[0:1, :], rowA[0:1, :],
                         start=True, stop=True)
                return t.matmul(PS[3][:, :], onesr_t[0:1, :], rowC[0:1, :],
                                start=True, stop=True)
            bc2 = op("tensor", f_bc2,
                     deps=[rows3, rows2, r1_ops[2], r1_ops[3]],
                     desc=f"bc2 {sq}")

            norm_ops = []
            for m in range(8):
                def f_norm(v, _sq=sq, _m=m):
                    v.tensor_sub(aw_t[:], r1_t[:, _m * 512:(_m + 1) * 512],
                                 PS[2][:, :])
                    return v.tensor_mul(
                        x1_t[:, _m * S + _sq * 512: _m * S + _sq * 512 + 512],
                        aw_t[:], PS[3][:, :])
                depn = [bc2, aw_ops[-1]]
                if sq == 0 and m == 0:
                    depn.append(k_done)
                    depn.append(prev_block["last_sc"])
                nrm = op("vector", f_norm, deps=depn, desc=f"x1 {sq}.{m}")
                norm_ops.append(nrm)
                x1_written.append(nrm)
            prev_sq_norm = norm_ops[-1]
            prev_wout_mms = wout_mms
        x1_done = x1_written[-1]

        if tap == "x1":
            do_tap([x1_done], [(x1_t, [128, 8 * S], F32)])
            P.emit(blk, sems)
            return nc

        def f_stash(v):
            r = []
            for c in range(8):
                r.append(v.tensor_copy(stash_t[:, c:c + 1],
                                       x1_t[:, c * S + 1023:c * S + 1024]))
            return r
        stash = op("vector", f_stash, deps=[x1_done], desc="stash")

        y_stores = []
        yev_ops = []
        ynorm_ops = []

        def emit_p6(i, extra_deps=()):
            ring = i % 2
            b6 = ring * 4

            def f_ytp(t, _i=i, _b=b6):
                r = []
                for c in range(8):
                    r.append(t.transpose(
                        PS[_b + c // 4][:, (c % 4) * 128:(c % 4 + 1) * 128],
                        x1_t[:, c * S + _i * 128:c * S + (_i + 1) * 128],
                        id32_t[:]))
                return r
            dep_tp = list(extra_deps)
            if i >= 2:
                dep_tp.append(yev_ops[i - 2])
            ytp = op("tensor", f_ytp, deps=dep_tp, desc=f"y tp {i}")

            def f_yev(v, _r=ring, _b=b6):
                sl = slice(_r * D, _r * D + D)
                v.tensor_copy(ytm[:, sl][:, 0:512], PS[_b][:, :])
                return v.tensor_copy(ytm[:, sl][:, 512:1024], PS[_b + 1][:, :])
            dep_ev = [ytp]
            if i >= 2:
                dep_ev.append(y_stores[i - 2])
            yev = op("vector", f_yev, deps=dep_ev, desc=f"y ev {i}")
            yev_ops.append(yev)

            def f_yst1a(v, _r=ring):
                sl = slice(_r * D, _r * D + D)
                return v.reduce_sum(yst_t[:, 0:1], ytm[:, sl], axis=AX.X)
            yst1a = op("vector", f_yst1a, deps=[], desc=f"y sum {i}")

            def f_yst1b(s):
                return s.activation(yst_t[:, 1:2], yst_t[:, 0:1], AF.Identity,
                                    scale=INV_D)
            yst1 = op("scalar", f_yst1b, deps=[yst1a], desc=f"y mu {i}")

            def f_yst2a(v, _r=ring):
                sl = slice(_r * D, _r * D + D)
                return v.scalar_tensor_tensor(
                    yjunk[:], ytm[:, sl], yst_t[:, 1:2], ytm[:, sl],
                    OP.subtract, OP.mult, accum_out=yst_t[:, 2:3])
            yst2a = op("vector", f_yst2a,
                       deps=[yst1, ynorm_ops[i - 1] if i >= 1 else None],
                       desc=f"y ssq {i}")

            def f_yst3(s):
                s.activation(yst_t[:, 3:4], yst_t[:, 2:3], AF.Identity,
                             scale=INV_D, bias=eps_t[:, 0:1])
                s.activation(yst_t[:, 4:5], yst_t[:, 3:4], AF.Ln)
                return s.activation(yst_t[:, 5:6], yst_t[:, 4:5], AF.Exp,
                                    scale=-0.5)
            yst3 = op("scalar", f_yst3,
                      deps=[yst2a, ynorm_ops[i - 1] if i >= 1 else None],
                      desc=f"y st3 {i}")

            def f_ynorm(v, _r=ring):
                sl = slice(_r * D, _r * D + D)
                return v.tensor_scalar(ytm[:, sl], ytm[:, sl], yst_t[:, 1:2],
                                       yst_t[:, 5:6], OP.subtract, OP.mult)
            ynorm = op("vector", f_ynorm, deps=[yst3], desc=f"y norm {i}")
            ynorm_ops.append(ynorm)

            def f_yst(q, _i=i, _r=ring):
                sl = slice(_r * D, _r * D + D)
                return q.dma_start(out=y_d[_i * 128:(_i + 1) * 128, :],
                                   in_=ytm[:, sl])
            ysto = op("sync", f_yst, deps=[ynorm], dma=True, desc=f"y st {i}")
            y_stores.append(ysto)

        x2_written = []
        prev_half = {}
        for hf in range(2):
            hw = slice(hf * 1024, hf * 1024 + 1024)
            mix_ops = {}
            for tag, mu_t, dsth, dstl in (("k", muk_t, xmh_t, xml_t),):
                pass
            for phase, (mu_t,) in enumerate([(muk_t,), (mur_t,)]):
                tag = "xk" if phase == 0 else "xr"
                mx_list = []
                for c in range(8):
                    def f_mix(v, _c=c, _hf=hf, _mu=mu_t, _ph=phase):
                        xsl = slice(_c * S + _hf * 1024,
                                    _c * S + _hf * 1024 + 1024)
                        xprev = slice(_c * S + _hf * 1024 - 1,
                                      _c * S + _hf * 1024 + 1023)
                        r = []
                        if _hf == 0:
                            v.tensor_copy(dbuf[:, 0:1],
                                          x1_t[:, _c * S:_c * S + 1])
                            v.tensor_sub(dbuf[:, 1:1024],
                                         x1_t[:, _c * S + 1:_c * S + 1024],
                                         x1_t[:, _c * S:_c * S + 1023])
                            v.tensor_scalar(xfbuf[:, 0:1], dbuf[:, 0:1],
                                            _mu[:, _c:_c + 1], None, OP.mult)
                            v.scalar_tensor_tensor(
                                xfbuf[:, 1:1024], dbuf[:, 1:1024],
                                _mu[:, _c:_c + 1],
                                x1_t[:, _c * S:_c * S + 1023],
                                OP.mult, OP.add)
                        else:
                            v.scalar_tensor_tensor(
                                dbuf[:, 0:1],
                                x1_t[:, _c * S + 1024:_c * S + 1025],
                                1.0, stash_t[:, _c:_c + 1],
                                OP.mult, OP.subtract)
                            v.tensor_sub(dbuf[:, 1:1024],
                                         x1_t[:, _c * S + 1025:_c * S + 2048],
                                         x1_t[:, _c * S + 1024:_c * S + 2047])
                            v.scalar_tensor_tensor(
                                xfbuf[:, 0:1], dbuf[:, 0:1],
                                _mu[:, _c:_c + 1], stash_t[:, _c:_c + 1],
                                OP.mult, OP.add)
                            v.scalar_tensor_tensor(
                                xfbuf[:, 1:1024], dbuf[:, 1:1024],
                                _mu[:, _c:_c + 1],
                                x1_t[:, _c * S + 1024:_c * S + 2047],
                                OP.mult, OP.add)
                        dsl = slice(_c * 1024, (_c + 1) * 1024)
                        v.tensor_copy(xmh_t[:, dsl], xfbuf[:])
                        r.append(v.scalar_tensor_tensor(
                            xml_t[:, dsl], xfbuf[:], 1.0, xmh_t[:, dsl],
                            OP.mult, OP.subtract))
                        return r
                    depm = [stash, c_ld]
                    if phase == 0 and c == 0:
                        if hf == 0:
                            depm.append(x1_done)
                        else:
                            depm.append(prev_half["x2"])
                        depm.append(prev_half.get("last_rmm"))
                    if phase == 1 and c == 0:
                        depm.append(kk_mms[-1])
                    mx = op("vector", f_mix, deps=depm,
                            desc=f"{tag} mix {hf}.{c}")
                    mx_list.append(mx)
                if phase == 0:
                    xk_mix = mx_list
                else:
                    xr_mix = mx_list

                if phase == 0:
                    kk_mms = []
                    ka_ops = []
                    kar_ops = []
                    for m in range(8):
                        ring = m % 2

                        def f_wld(g, _m=m, _r=ring):
                            return [g.dma_start(
                                        out=wf_h[_r][:, 0:1024].rearrange("p (c n) -> p c n", c=8),
                                        in_=scr["wkh_s"][:, _m * 128:(_m + 1) * 128]
                                        .rearrange("(c p) n -> p c n", p=128)),
                                    g.dma_start(
                                        out=wf_l[_r][:, 0:1024].rearrange("p (c n) -> p c n", c=8),
                                        in_=scr["wkl_s"][:, _m * 128:(_m + 1) * 128]
                                        .rearrange("(c p) n -> p c n", p=128))]
                        dep_w = []
                        if hf == 0 and m < 2:
                            dep_w = [attn_done, wprep_last,
                                     prev_sq_norm]
                        wldk = op("sync", f_wld,
                                  deps=dep_w + [kk_mms[-3] if m >= 2 else
                                                prev_half.get("last_rmm")],
                                  dma=True, n=2, desc=f"wk ld {hf}.{m}")

                        for ntv in range(2):
                            bank = ntv

                            def f_kkmm(t, _m=m, _r=ring, _n=ntv, _b=bank):
                                r = []
                                for kc in range(8):
                                    for p in range(3):
                                        lhs = [wf_h[_r], wf_h[_r], wf_l[_r]][p][
                                            :, kc * 128:kc * 128 + 128]
                                        rhs = [xmh_t, xml_t, xmh_t][p][
                                            :, kc * 1024 + _n * 512:
                                            kc * 1024 + _n * 512 + 512]
                                        r.append(t.matmul(
                                            PS[_b][:, :], lhs, rhs,
                                            start=(kc == 0 and p == 0),
                                            stop=(kc == 7 and p == 2)))
                                return r
                            dep_kk = [wldk, xk_mix[-1]]
                            if m == 0 and hf == 0:
                                dep_kk.append(x1_done)
                            if m >= 1:
                                dep_kk.append(kar_ops[-2])
                            kkm = op("tensor", f_kkmm, deps=dep_kk,
                                     desc=f"kk mm {hf}.{m}.{ntv}")
                            kk_mms.append(kkm)

                            def f_karelu(s, _b=bank):
                                return s.activation(relu_t[:], PS[_b][:, :],
                                                    AF.Relu)
                            kar = op("scalar", f_karelu,
                                     deps=[kkm,
                                           ka_ops[-1] if ka_ops else None],
                                     desc=f"ka relu {hf}.{m}.{ntv}")
                            kar_ops.append(kar)

                            def f_kasp(v, _m=m, _n=ntv):
                                dsl = slice(_m * 1024 + _n * 512,
                                            _m * 1024 + _n * 512 + 512)
                                v.tensor_mul(k2_t[:], relu_t[:], relu_t[:])
                                v.tensor_copy(kah_t[:, dsl], k2_t[:])
                                return v.scalar_tensor_tensor(
                                    kal_t[:, dsl], k2_t[:], 1.0,
                                    kah_t[:, dsl], OP.mult, OP.subtract)
                            dep_ka = [kar]
                            if hf == 0 and m == 0:
                                dep_ka.append(attn_done_compute)
                            kasp = op("vector", f_kasp, deps=dep_ka,
                                      desc=f"ka sp {hf}.{m}.{ntv}")
                            ka_ops.append(kasp)
                    kk_last = kk_mms[-1]
                    if hf == 1:
                        kar_last_hf1 = kar_ops[-1]
                    if tap == "ffn1" and hf == 0:
                        do_tap([ka_ops[-1], xk_mix[-1]],
                               [(xmh_t, [128, 8 * 1024], F16),
                                (xml_t, [128, 8 * 1024], F16),
                                (kah_t, [128, 8 * 1024], F16),
                                (kal_t, [128, 8 * 1024], F16)])
                        P.emit(blk, sems)
                        return nc

            rv_ops = []
            rmms = []
            sig_ops = []
            last_rmm = None
            for m in range(8):
                ring = m % 2

                def f_wldrv(g, _m=m, _r=ring):
                    return [g.dma_start(
                                out=wf_h[_r][:, 0:1024].rearrange("p (c n) -> p c n", c=8),
                                in_=scr["wrh_s"][:, _m * 128:(_m + 1) * 128]
                                .rearrange("(c p) n -> p c n", p=128)),
                            g.dma_start(
                                out=wf_l[_r][:, 0:1024].rearrange("p (c n) -> p c n", c=8),
                                in_=scr["wrl_s"][:, _m * 128:(_m + 1) * 128]
                                .rearrange("(c p) n -> p c n", p=128)),
                            g.dma_start(
                                out=wf_h[_r][:, 1024:2048].rearrange("p (c n) -> p c n", c=8),
                                in_=scr["wvh_s"][:, _m * 128:(_m + 1) * 128]
                                .rearrange("(c p) n -> p c n", p=128)),
                            g.dma_start(
                                out=wf_l[_r][:, 1024:2048].rearrange("p (c n) -> p c n", c=8),
                                in_=scr["wvl_s"][:, _m * 128:(_m + 1) * 128]
                                .rearrange("(c p) n -> p c n", p=128))]
                dep_w = [rmms[-3]] if m >= 2 else [kk_last]
                wldrv = op("sync", f_wldrv, deps=dep_w, dma=True, n=4,
                           desc=f"wrv ld {hf}.{m}")

                for ntv in range(2):
                    rbank = 2
                    vbank = 3

                    def f_rmm_r(t, _m=m, _r=ring, _n=ntv, _rb=rbank):
                        r = []
                        for kc in range(8):
                            for p in range(3):
                                lhs = [wf_h[_r], wf_h[_r], wf_l[_r]][p][
                                    :, kc * 128:kc * 128 + 128]
                                rhs = [xmh_t, xml_t, xmh_t][p][
                                    :, kc * 1024 + _n * 512:
                                    kc * 1024 + _n * 512 + 512]
                                r.append(t.matmul(PS[_rb][:, :], lhs, rhs,
                                                  start=(kc == 0 and p == 0),
                                                  stop=(kc == 7 and p == 2)))
                        return r
                    dep_rr = [wldrv, xr_mix[-1], ka_ops[-1]]
                    if sig_ops:
                        dep_rr.append(sig_ops[-1])
                    elif hf == 0:
                        dep_rr.append(x1_done)
                    rmm_r = op("tensor", f_rmm_r, deps=dep_rr,
                               desc=f"r mm {hf}.{m}.{ntv}")

                    def f_rmm_v(t, _m=m, _r=ring, _n=ntv, _vb=vbank):
                        r = []
                        for kc in range(8):
                            for p in range(3):
                                lhs = [wf_h[_r], wf_h[_r], wf_l[_r]][p][
                                    :, 1024 + kc * 128:1024 + kc * 128 + 128]
                                rhs = [kah_t, kal_t, kah_t][p][
                                    :, kc * 1024 + _n * 512:
                                    kc * 1024 + _n * 512 + 512]
                                r.append(t.matmul(PS[_vb][:, :], lhs, rhs,
                                                  start=(kc == 0 and p == 0),
                                                  stop=(kc == 7 and p == 2)))
                        return r
                    dep_rv = [wldrv, ka_ops[-1]]
                    if rv_ops:
                        dep_rv.append(rv_ops[-1])
                    elif hf == 0:
                        dep_rv.append(x1_done)
                    rmm = op("tensor", f_rmm_v, deps=dep_rv,
                             desc=f"v mm {hf}.{m}.{ntv}")
                    rmms.append(rmm)
                    last_rmm = rmm

                    def f_sig(s, _rb=rbank):
                        return s.activation(rs_t[:], PS[_rb][:, :], AF.Sigmoid)
                    sig = op("scalar", f_sig,
                             deps=[rmm_r, rv_ops[-1] if rv_ops else None],
                             desc=f"sig {hf}.{m}.{ntv}")
                    sig_ops.append(sig)

                    def f_x2(v, _m=m, _hf=hf, _n=ntv, _vb=vbank):
                        xsl = slice(_m * S + _hf * 1024 + _n * 512,
                                    _m * S + _hf * 1024 + _n * 512 + 512)
                        v.tensor_mul(ff_t[:], rs_t[:], PS[_vb][:, :])
                        return v.tensor_add(x1_t[:, xsl], ff_t[:],
                                            x1_t[:, xsl])
                    x2o = op("vector", f_x2, deps=[sig, rmm, stash],
                             desc=f"x2 {hf}.{m}.{ntv}")
                    rv_ops.append(x2o)
                    x2_written.append(x2o)
                    if hf == 1 and len(rv_ops) % 2 == 0:
                        i6 = len(rv_ops) // 2 - 1
                        emit_p6(i6, extra_deps=(
                            [x2_hf0_last, kar_last_hf1] if i6 == 0 else []))
            prev_half["x2"] = rv_ops[-1]
            prev_half["last_rmm"] = last_rmm
            if hf == 0:
                x2_hf0_last = rv_ops[-1]
        x2_done = x2_written[-1]

        if tap == "x2":
            do_tap([x2_done], [(x1_t, [128, 8 * S], F32)])
            P.emit(blk, sems)
            return nc

        for i in range(8, 16):
            emit_p6(i, extra_deps=([x2_done] if i == 8 else []))

        def f_fin(q):
            return None
        op("sync", f_fin, deps=[y_stores[-1], y_stores[-2]], desc="fin")

        P.emit(blk, sems)
    return nc


def _host_constants():
    ROPE_BASE = 10000.0
    DR = 64
    inv_freq = 1.0 / ROPE_BASE ** (np.arange(0, DR, 2, dtype=np.float32) / DR)
    t = np.arange(S, dtype=np.float32)
    freqs = np.outer(t, inv_freq)
    cos = np.concatenate([np.cos(freqs), np.cos(freqs)], -1)[:, ::2]
    sin = np.concatenate([np.sin(freqs), np.sin(freqs)], -1)[:, ::2]
    p = np.arange(128)
    tc = cos[:, p % 32].T.astype(np.float32)
    sgn = np.where((p % 64) < 32, -1.0, 1.0).astype(np.float32)
    ts = (sin[:, p % 32] * sgn[None, :]).T.astype(np.float32)
    id32 = np.eye(128, dtype=np.float32)
    perm = np.zeros((128, 128), dtype=np.float32)
    for a in range(128):
        perm[a ^ 32, a] = 1.0
    e1 = np.zeros((128, 128), dtype=np.float32)
    e1[:, 0] = 1.0
    onesr = np.ones((1, 128), dtype=np.float32)
    return np.ascontiguousarray(tc), np.ascontiguousarray(ts), id32, perm, \
        e1, onesr


_DEINT = None


def _deint_cols():
    global _DEINT
    if _DEINT is None:
        idx = []
        for h in range(H):
            idx += [h * 64 + 2 * i for i in range(32)]
            idx += [h * 64 + 2 * i + 1 for i in range(32)]
        _DEINT = np.array(idx)
    return _DEINT


def kernel(L0_attn_x, L0_attn_w_down_q, L0_attn_w_down_kv, L0_attn_w_up_q,
           L0_attn_w_up_k, L0_attn_w_up_v, L0_attn_w_out, L0_attn_ln_scale,
           L0_attn_ln_bias, L0_ffn_W_r, L0_ffn_W_k, L0_ffn_W_v, L0_ffn_mu_r,
           L0_ffn_mu_k):
    x = np.ascontiguousarray(np.asarray(L0_attn_x, dtype=np.float32))
    tc, ts, id32, perm, e1, onesr = _host_constants()
    di = _deint_cols()
    c = np.ascontiguousarray
    f32 = np.float32
    wdqT = c(np.asarray(L0_attn_w_down_q, f32).T)
    wdkvT = c(np.asarray(L0_attn_w_down_kv, f32).T)
    wuqT = c(np.asarray(L0_attn_w_up_q, f32).T[:, di])
    wukT = c(np.asarray(L0_attn_w_up_k, f32).T[:, di])
    wuvT = c(np.asarray(L0_attn_w_up_v, f32).T)
    woutT = c(np.asarray(L0_attn_w_out, f32).T)
    wrT = c(np.asarray(L0_ffn_W_r, f32).T)
    wkT = c(np.asarray(L0_ffn_W_k, f32).T)
    wvT = c(np.asarray(L0_ffn_W_v, f32).T)
    gam = c(np.asarray(L0_attn_ln_scale, f32).reshape(D, 1))
    bet = c(np.asarray(L0_attn_ln_bias, f32).reshape(D, 1))
    mur = c(np.asarray(L0_ffn_mu_r, f32).reshape(D, 1))
    muk = c(np.asarray(L0_ffn_mu_k, f32).reshape(D, 1))

    nc = build_program()
    shared = dict(wdqT=wdqT, wdkvT=wdkvT, wuqT=wuqT, wukT=wukT, wuvT=wuvT,
                  woutT=woutT, wrT=wrT, wkT=wkT, wvT=wvT, gamma=gam, beta=bet,
                  mur=mur, muk=muk, tc=tc, tsn=ts, id32=id32, perm32=perm,
                  e1=e1, onesr=onesr,
                  epsv=np.full((128, 1), EPS, np.float32))
    in_maps = [dict(shared, x=c(x[b])) for b in range(8)]
    res = run_bass_kernel_spmd(nc, in_maps, core_ids=list(range(8)))
    out = np.stack([np.asarray(res.results[b]["y"], dtype=np.float32)
                    for b in range(8)], axis=0)
    return out


# version 12
# speedup vs baseline: 1.2695x; 1.0426x over previous
import numpy as np
from contextlib import ExitStack

import concourse.bass as bass
import concourse.mybir as mybir
from concourse.bass_utils import run_bass_kernel_spmd

F32 = mybir.dt.float32
F16 = mybir.dt.float16
U8 = mybir.dt.uint8
AF = mybir.ActivationFunctionType
OP = mybir.AluOpType
AX = mybir.AxisListType

S, D, H, DH, CL = 2048, 1024, 16, 64, 512
NPAIR, NSQ, SQW, NCH = 8, 4, 512, 16
EPS = 1e-5
INV_D = 1.0 / D
SCALE = 0.125


class _Op:
    __slots__ = ("eng", "fn", "deps", "idx", "dma", "need", "val", "desc")


class Prog:
    ENGS = ("sync", "tensor", "vector", "scalar", "gpsimd")

    def __init__(self):
        self.ops = []
        self.ndma = {}

    def op(self, eng, fn, deps=(), dma=False, n=1, desc=""):
        o = _Op()
        o.eng, o.fn, o.dma, o.desc = eng, fn, dma, desc
        o.idx = len(self.ops)
        o.need, o.val = False, None
        o.deps = []
        for d in deps:
            if d is None:
                continue
            if not isinstance(d, tuple):
                d = (d, False)
            assert d[0].idx < o.idx, f"forward dep at {desc}"
            o.deps.append(d)
        if dma:
            self.ndma[o.idx] = n
        self.ops.append(o)
        return o

    def emit(self, blk, sems):
        for o in self.ops:
            if o.dma:
                o.need = True
            for d, hard in o.deps:
                if d.eng != o.eng or d.dma or o.dma or hard:
                    d.need = True
        cnt = {e: 0 for e in self.ENGS}
        for o in self.ops:
            if o.need:
                cnt[o.eng] += 16 * self.ndma[o.idx] if o.dma else 1
                o.val = cnt[o.eng]
        by_eng = {e: [o for o in self.ops if o.eng == e] for e in self.ENGS}

        def emit_engine(ename, h):
            guar = {e: 0 for e in self.ENGS}
            for o in by_eng[ename]:
                for d, hard in o.deps:
                    if not d.need:
                        continue
                    if d.eng == ename and not (d.dma or o.dma or hard):
                        continue
                    if d.val > guar[d.eng]:
                        h.wait_ge(sems[d.eng].sem, d.val)
                        guar[d.eng] = d.val
                r = o.fn(h)
                if o.need:
                    ct = sems[o.eng]
                    if o.dma:
                        instrs = r if isinstance(r, (list, tuple)) else [r]
                        assert len(instrs) == self.ndma[o.idx], \
                            f"{o.desc}: {len(instrs)} != {self.ndma[o.idx]}"
                        for ins in instrs:
                            ct.inc(ins, 16)
                    else:
                        ins = r[-1] if isinstance(r, (list, tuple)) else r
                        ct.inc(ins, 1)
                    assert ct.n == o.val, f"{o.desc}: sem val mismatch"

        for ename in self.ENGS:
            getattr(blk, ename)(lambda h, _e=ename: emit_engine(_e, h))




def _dma_gather(q, out_ap, ncols, dram_ap, **kw):
    c = dram_ap.shape[0] // 128
    return q.dma_start(
        out=out_ap.rearrange("p (c n) -> p c n", c=c),
        in_=dram_ap.rearrange("(c p) n -> p c n", p=128), **kw)

class Ct:
    __slots__ = ("sem", "n")

    def __init__(self, sem):
        self.sem = sem
        self.n = 0

    def inc(self, instr, k):
        instr.then_inc(self.sem, k)
        self.n += k


def build_program(tap=None):
    nc = bass.Bass("TRN2", target_bir_lowering=False)

    d_in = {}
    for name, shape in [
            ("x", [S, D]), ("wdqT", [D, CL]), ("wdkvT", [D, CL]),
            ("wuqT", [CL, D]), ("wukT", [CL, D]), ("wuvT", [CL, D]),
            ("woutT", [D, D]), ("wrT", [D, D]), ("wkT", [D, D]),
            ("wvT", [D, D]), ("gamma", [D, 1]), ("beta", [D, 1]),
            ("mur", [D, 1]), ("muk", [D, 1]), ("tc", [128, S]),
            ("tsn", [128, S]), ("id32", [128, 128]), ("perm32", [128, 128]),
            ("e1", [128, 128]), ("onesr", [1, 128]), ("epsv", [128, 1])]:
        d_in[name] = nc.dram_tensor(name, shape, F32, kind="ExternalInput").ap()
    y_d = nc.dram_tensor("y", [S, D], F32, kind="ExternalOutput").ap()

    scr = {}
    for name, shape in [
            ("cqh_s", [CL, S]), ("cql_s", [CL, S]),
            ("ckvh_s", [CL, S]), ("ckvl_s", [CL, S]),
            ("wuqh_s", [CL, D]), ("wuql_s", [CL, D]),
            ("wouth_s", [D, D]), ("woutl_s", [D, D]),
            ("wrh_s", [D, D]), ("wrl_s", [D, D]),
            ("wkh_s", [D, D]), ("wkl_s", [D, D]),
            ("wvh_s", [D, D]), ("wvl_s", [D, D]),
            ("anh_s", [D, S]), ("anl_s", [D, S])]:
        scr[name] = nc.dram_tensor(name, shape, F16).ap()

    with ExitStack() as ST:
        sems = {e: Ct(ST.enter_context(nc.semaphore(f"sm_{e}")))
                for e in Prog.ENGS}
        PS = [ST.enter_context(nc.psum_tensor(f"p{i}", [128, 512], F32))
              for i in range(8)]

        TOT = 196352
        backing = ST.enter_context(nc.sbuf_tensor("backing", [128, TOT], U8))
        base_addr = nc.lookup_mloc(backing).addr

        def place(name, shape, dtype, off):
            assert off % 64 == 0, name
            nbytes = int(np.prod(shape[1:])) * mybir.dt.size(dtype)
            assert off + nbytes <= TOT, f"{name} overflow"
            h = nc.alloc_sbuf_tensor_at(name, list(shape), dtype,
                                        offset=base_addr + off)
            return h.ap()

        A0 = 19200
        B0 = A0 + 65536
        C0 = B0 + 66560
        assert C0 + 45056 <= TOT

        tc_t = place("tc_t", [128, S], F32, 0)
        ts_t = place("ts_t", [128, S], F32, 8192)
        id32_t = place("id32_t", [128, 128], F32, 16384)
        perm_t = place("perm_t", [128, 128], F32, 16896)
        e1_t = place("e1_t", [128, 128], F32, 17408)
        onesr_t = place("onesr_t", [1, 128], F32, 17920)
        gam_t = place("gam_t", [128, 8], F32, 18432)
        bet_t = place("bet_t", [128, 8], F32, 18496)
        beth_t = place("beth_t", [128, 8], F16, 18560)
        betl_t = place("betl_t", [128, 8], F16, 18624)
        mur_t = place("mur_t", [128, 8], F32, 18688)
        muk_t = place("muk_t", [128, 8], F32, 18752)
        bq_t = place("bq_t", [128, 4], F32, 18816)
        bkv_t = place("bkv_t", [128, 4], F32, 18880)
        hstat = place("hstat", [128, 16], F32, 18944)
        stash_t = place("stash_t", [128, 8], F32, 19008)
        yst_t = place("yst_t", [128, 8], F32, 19072)
        eps_t = place("eps_t", [128, 1], F32, 19136)

        hh_fm = place("hh_fm", [128, 8 * S], F16, A0)
        hl_fm = place("hl_fm", [128, 8 * S], F16, A0 + 32768)
        kh_t = place("kh_t", [128, 8 * S], F16, A0)
        kl_t = place("kl_t", [128, 8 * S], F16, A0 + 32768)
        x1_t = place("x1_t", [128, 8 * S], F32, A0)

        wbuf = place("wbuf", [128, 2 * D], F32, B0)
        wbh = place("wbh", [128, 2 * D], F16, B0 + 8192)
        wbl = place("wbl", [128, 2 * D], F16, B0 + 12288)
        xtm = place("xtm", [128, 2 * D], F32, B0 + 16384)
        hjunk = place("hjunk", [128, D], F32, B0 + 24576)
        hf32 = place("hf32", [128, D], F32, B0 + 28672)
        hfm32 = place("hfm32", [128, D], F32, B0 + 32768)
        wd32 = place("wd32", [128, 2 * CL], F32, B0 + 36864)
        wdh = place("wdh", [128, 8 * CL], F16, B0 + 40960)
        wdl = place("wdl", [128, 8 * CL], F16, B0 + 49152)
        wdoh = place("wdoh", [128, 8 * CL], F16, B0 + 16384)
        wdol = place("wdol", [128, 8 * CL], F16, B0 + 24576)
        cbuf = place("cbuf", [128, 512], F32, B0 + 57344)
        cbh = place("cbh", [128, 2 * 512], F16, B0 + 59392)
        cbl = place("cbl", [128, 2 * 512], F16, B0 + 61440)
        vh_t = place("vh_t", [128, 16640], F16, B0)
        vl_t = place("vl_t", [128, 16640], F16, B0 + 33280)
        r1_t = place("r1_t", [128, 8 * 512], F32, B0)
        aw_t = place("aw_t", [128, 512], F32, B0 + 16384)
        sqm_t = place("sqm_t", [128, 512], F32, B0 + 18432)
        xtm4 = place("xtm4", [128, 2 * 512], F32, B0 + 20480)
        rowA = place("rowA", [1, 512], F32, B0 + 24576)
        rowB = place("rowB", [1, 512], F32, B0 + 26624)
        rowC = place("rowC", [1, 512], F32, B0 + 28672)
        rowD = place("rowD", [1, 512], F32, B0 + 30720)
        xmh_t = place("xmh_t", [128, 8 * 1024], F16, B0)
        xml_t = place("xml_t", [128, 8 * 1024], F16, B0 + 16384)
        kah_t = place("kah_t", [128, 8 * 1024], F16, B0 + 32768)
        kal_t = place("kal_t", [128, 8 * 1024], F16, B0 + 49152)

        ckv_th = place("ckv_th", [128, 4 * 512], F16, C0)
        ckv_tl = place("ckv_tl", [128, 4 * 512], F16, C0 + 4096)
        wukh = place("wukh", [128, 4 * D], F16, C0 + 8192)
        wukl = place("wukl", [128, 4 * D], F16, C0 + 16384)
        cvsh = place("cvsh", [128, 4 * 128], F16, C0)
        cvsl = place("cvsl", [128, 4 * 128], F16, C0 + 1024)
        wuvh = place("wuvh", [128, 4 * D], F16, C0 + 8192)
        wuvl = place("wuvl", [128, 4 * D], F16, C0 + 16384)
        cqh_t = place("cqh_t", [128, 4 * 512], F16, C0)
        cql_t = place("cql_t", [128, 4 * 512], F16, C0 + 4096)
        wuqh_t = place("wuqh_t", [128, 2 * 512], F16, C0 + 8192)
        wuql_t = place("wuql_t", [128, 2 * 512], F16, C0 + 10240)
        expf = place("expf", [128, 2 * 512], F32, C0 + 12288)
        exph = place("exph", [128, 6 * 512], F16, C0 + 16384)
        expl = place("expl", [128, 6 * 512], F16, C0 + 22528)
        anf_t = place("anf_t", [128, 2 * 512], F32, C0 + 28672)
        an16h = place("an16h", [128, 2 * 512], F16, C0 + 32768)
        an16l = place("an16l", [128, 2 * 512], F16, C0 + 34816)
        qf_t = place("qf_t", [128, 512], F32, C0 + 36864)
        qt1_t = place("qt1_t", [128, 512], F32, C0 + 38912)
        zsc_t = place("zsc_t", [128, 1024], F32, C0 + 12288)
        zinv_t = [place("zinv0", [128, 512], F32, C0 + 12288),
                  place("zinv1", [128, 512], F32, C0 + 14336)]
        wu32 = place("wu32", [128, 1024], F32, C0 + 40960)
        qhi_t = place("qhi_t", [128, 512], F16, C0 + 40960)
        qlo_t = place("qlo_t", [128, 512], F16, C0 + 41984)
        wo_th = place("wo_th", [128, 2 * 1024], F16, C0)
        wo_tl = place("wo_tl", [128, 2 * 1024], F16, C0 + 4096)
        anrh = place("anrh", [128, 8 * 512], F16, C0 + 8192)
        anrl = place("anrl", [128, 8 * 512], F16, C0 + 16384)
        wf_h = [place("wf_h0", [128, 2048], F16, C0),
                place("wf_h1", [128, 2048], F16, C0 + 4096)]
        wf_l = [place("wf_l0", [128, 2048], F16, C0 + 8192),
                place("wf_l1", [128, 2048], F16, C0 + 12288)]
        dbuf = place("dbuf", [128, 1024], F32, C0 + 16384)
        xfbuf = place("xfbuf", [128, 1024], F32, C0 + 20480)
        relu_t = place("relu_t", [128, 512], F32, C0 + 24576)
        k2_t = place("k2_t", [128, 512], F32, C0 + 26624)
        rs_t = place("rs_t", [128, 512], F32, C0 + 28672)
        ff_t = place("ff_t", [128, 512], F32, C0 + 30720)
        ytm = place("ytm", [128, 2 * D], F32, C0 + 32768)
        yjunk = place("yjunk", [128, D], F32, C0 + 16384)
        st2junk = place("st2junk", [128, D], F32, C0 + 16384)

        blk = ST.enter_context(nc.Block())
        P = Prog()
        op = P.op

        def f_consts(q):
            r = []
            for nm, t in (("tc", tc_t), ("tsn", ts_t), ("id32", id32_t),
                          ("perm32", perm_t), ("e1", e1_t)):
                r.append(q.dma_start(out=t[:], in_=d_in[nm][:, :]))
            r.append(q.dma_start(out=onesr_t[0:1, :], in_=d_in["onesr"][:, :]))
            r.append(q.dma_start(out=eps_t[:, :], in_=d_in["epsv"][:, :]))
            for nm, t in (("gamma", gam_t), ("beta", bet_t), ("mur", mur_t),
                          ("muk", muk_t)):
                r.append(q.dma_start(
                    out=t[:, :],
                    in_=d_in[nm].rearrange("(c p) o -> p (c o)", p=128),
                    allow_slow_non_contiguous=True))
            return r
        c_ld = op("sync", f_consts, dma=True, n=11, desc="consts")

        def f_bsplit(v):
            v.tensor_copy(beth_t[:], bet_t[:])
            return v.scalar_tensor_tensor(betl_t[:], bet_t[:], 1.0, beth_t[:],
                                          OP.mult, OP.subtract)
        bsplit = op("vector", f_bsplit, deps=[c_ld], desc="beta split")

        wprep = [("wuqT", "wuqh_s", "wuql_s", 4), ("woutT", "wouth_s", "woutl_s", 8),
                 ("wrT", "wrh_s", "wrl_s", 8), ("wkT", "wkh_s", "wkl_s", 8),
                 ("wvT", "wvh_s", "wvl_s", 8)]
        pp_store = [None, None]
        i_pp = 0
        wprep_last = None
        for wname, hs, ls, nch in wprep:
            for c in range(nch):
                half = i_pp % 2
                sl = slice(half * D, half * D + D)

                def f_ld(q, _w=wname, _c=c, _sl=sl):
                    return q.dma_start(out=wbuf[:, _sl],
                                       in_=d_in[_w][_c * 128:(_c + 1) * 128, :])
                ldo = op("sync", f_ld, deps=[pp_store[half]], dma=True,
                         desc=f"wp {wname}{c} ld")

                def f_sp(v, _sl=sl):
                    v.tensor_copy(wbh[:, _sl], wbuf[:, _sl])
                    return v.scalar_tensor_tensor(wbl[:, _sl], wbuf[:, _sl],
                                                  1.0, wbh[:, _sl],
                                                  OP.mult, OP.subtract)
                spo = op("vector", f_sp, deps=[ldo, pp_store[half]],
                         desc=f"wp {wname}{c} split")

                def f_st(g, _h=hs, _l=ls, _c=c, _sl=sl):
                    return [g.dma_start(out=scr[_h][_c * 128:(_c + 1) * 128, :],
                                        in_=wbh[:, _sl]),
                            g.dma_start(out=scr[_l][_c * 128:(_c + 1) * 128, :],
                                        in_=wbl[:, _sl])]
                sto = op("gpsimd", f_st, deps=[spo], dma=True, n=2,
                         desc=f"wp {wname}{c} st")
                pp_store[half] = sto
                wprep_last = sto
                i_pp += 1

        h_ev2 = []
        hn_ops = []
        st1_ops = []
        st3_ops = []
        tp_ops1 = []
        ev1_ops = []

        def emit_stats(i):
            half = i % 2
            xsl = slice(half * D, half * D + D)
            c0 = (i % 2) * 8

            def f_xld(q, _i=i, _sl=xsl):
                return q.dma_start(out=xtm[:, _sl],
                                   in_=d_in["x"][_i * 128:(_i + 1) * 128, :])
            xld = op("sync", f_xld,
                     deps=[hn_ops[i - 2] if i >= 2 else None], dma=True,
                     desc=f"x ld {i}")

            def f_st1a(v, _sl=xsl, _c=c0):
                return v.reduce_sum(hstat[:, _c:_c + 1], xtm[:, _sl],
                                    axis=AX.X)
            st1a = op("vector", f_st1a,
                      deps=[xld, st1_ops[i - 2] if i >= 2 else None],
                      desc=f"ln1 sum {i}")

            def f_st1b(s, _c=c0):
                return s.activation(hstat[:, _c + 1:_c + 2],
                                    hstat[:, _c:_c + 1], AF.Identity,
                                    scale=INV_D)
            st1 = op("scalar", f_st1b, deps=[st1a], desc=f"ln1 mu {i}")
            st1_ops.append(st1)
            return xld, st1

        def emit_st2(i):
            half = i % 2
            xsl = slice(half * D, half * D + D)
            c0 = (i % 2) * 8

            def f_st2a(v, _sl=xsl, _c=c0):
                return v.scalar_tensor_tensor(
                    st2junk[:], xtm[:, _sl], hstat[:, _c + 1:_c + 2],
                    xtm[:, _sl], OP.subtract, OP.mult,
                    accum_out=hstat[:, _c + 2:_c + 3])
            st2a = op("vector", f_st2a, deps=[st1_ops[i]],
                      desc=f"ln1 ssq {i}")

            def f_st3(s, _c=c0):
                s.activation(hstat[:, _c + 3:_c + 4],
                             hstat[:, _c + 2:_c + 3], AF.Identity,
                             scale=INV_D, bias=eps_t[:, 0:1])
                s.activation(hstat[:, _c + 4:_c + 5],
                             hstat[:, _c + 3:_c + 4], AF.Ln)
                return s.activation(hstat[:, _c + 5:_c + 6],
                                    hstat[:, _c + 4:_c + 5], AF.Exp,
                                    scale=-0.5)
            st3 = op("scalar", f_st3, deps=[st2a], desc=f"ln1 rstd {i}")
            st3_ops.append(st3)

        def emit_hn(i):
            half = i % 2
            xsl = slice(half * D, half * D + D)
            c0 = (i % 2) * 8

            def f_hn(v, _sl=xsl, _c=c0):
                return v.tensor_scalar(hf32[:], xtm[:, _sl],
                                       hstat[:, _c + 1:_c + 2],
                                       hstat[:, _c + 5:_c + 6],
                                       OP.subtract, OP.mult)
            hn = op("vector", f_hn,
                    deps=[st3_ops[i], tp_ops1[i - 1] if i >= 1 else None],
                    desc=f"ln1 h {i}")
            hn_ops.append(hn)

            bk0 = (i % 2) * 2

            def f_tp(t, _b=bk0):
                r = []
                for c in range(8):
                    r.append(t.transpose(
                        PS[_b + c // 4][:, (c % 4) * 128:(c % 4 + 1) * 128],
                        hf32[:, c * 128:(c + 1) * 128], id32_t[:]))
                return r
            tp = op("tensor", f_tp,
                    deps=[hn, c_ld, ev1_ops[i - 2] if i >= 2 else None],
                    desc=f"h tp {i}")
            tp_ops1.append(tp)
            return hn, tp

        def emit_ev(i):
            bk0 = (i % 2) * 2

            def f_ev1(v, _b=bk0):
                v.tensor_copy(hfm32[:, 0:512], PS[_b][:, :])
                return v.tensor_copy(hfm32[:, 512:1024], PS[_b + 1][:, :])
            ev1 = op("vector", f_ev1, deps=[tp_ops1[i]], desc=f"h evac {i}")
            ev1_ops.append(ev1)

            def f_ev2(v, _i=i):
                r = []
                for c in range(8):
                    dsl = slice(c * S + _i * 128, c * S + (_i + 1) * 128)
                    ssl = slice(c * 128, (c + 1) * 128)
                    v.tensor_scalar(hjunk[:, ssl], hfm32[:, ssl],
                                    gam_t[:, c:c + 1], bet_t[:, c:c + 1],
                                    OP.mult, OP.add)
                    v.tensor_copy(hh_fm[:, dsl], hjunk[:, ssl])
                    r.append(v.scalar_tensor_tensor(
                        hl_fm[:, dsl], hjunk[:, ssl], 1.0, hh_fm[:, dsl],
                        OP.mult, OP.subtract))
                return r
            ev2 = op("vector", f_ev2, deps=[c_ld], desc=f"h split {i}")
            h_ev2.append(ev2)

        for i in range(18):
            if i < 16:
                emit_stats(i)
            if 1 <= i <= 16:
                emit_hn(i - 1)
            if i < 16:
                emit_st2(i)
            if i >= 2:
                emit_ev(i - 2)

        tapped = []

        def do_tap(deps, srcs):
            outs = []
            for i, (ap_, shape, dt_) in enumerate(srcs):
                outs.append(nc.dram_tensor(f"dbg{i}", list(shape), dt_,
                                           kind="ExternalOutput").ap())

            def f_tap(q, _o=outs, _s=srcs):
                return [q.dma_start(out=o[:, :], in_=s[0][:, :] if s[0].ndim == 2
                                    else s[0][:])
                        for o, s in zip(_o, _s)]
            t_ = op("sync", f_tap, deps=deps, dma=True, n=len(srcs),
                    desc="tap")
            op("sync", lambda q: None, deps=[t_], desc="tapfin")
            tapped.append(True)

        if tap == "h":
            do_tap([h_ev2[15]], [(hh_fm, [128, 8 * S], F16),
                                 (hl_fm, [128, 8 * S], F16)])
            P.emit(blk, sems)
            return nc

        c_stores = []
        bias_ev = {}
        wd_splits = {}
        for wi, (wname, hs_c, ls_c) in enumerate(
                [("wdqT", "cqh_s", "cql_s"), ("wdkvT", "ckvh_s", "ckvl_s")]):
            wsp_list = []
            for c in range(8):
                half = c % 2
                sl32 = slice(half * CL, half * CL + CL)
                slc = slice(c * CL, (c + 1) * CL)

                def f_wld(q, _w=wname, _c=c, _sl=sl32):
                    return q.dma_start(out=wd32[:, _sl],
                                       in_=d_in[_w][_c * 128:(_c + 1) * 128, :])
                dep_reuse = wsp_list[c - 2] if c >= 2 else \
                    (wd_splits.get(wi - 1, [None] * 8)[6 + half] if wi > 0 else None)
                extra = []
                if wi == 0 and c < 2:
                    extra = [hn_ops[15], h_ev2[15]]
                wld = op("sync", f_wld, deps=[dep_reuse] + extra, dma=True,
                         desc=f"{wname} ld {c}")

                def f_wsp(v, _sl=sl32, _c=slc):
                    v.tensor_copy(wdh[:, _c], wd32[:, _sl])
                    return v.scalar_tensor_tensor(wdl[:, _c], wd32[:, _sl],
                                                  1.0, wdh[:, _c],
                                                  OP.mult, OP.subtract)
                dep_sp_extra = []
                if wi == 0 and c < 2:
                    dep_sp_extra = [hn_ops[15], h_ev2[15]]
                wsp = op("vector", f_wsp,
                         deps=[wld, c_ld] + dep_sp_extra, desc=f"{wname} sp {c}")
                wsp_list.append(wsp)
            wd_splits[wi] = wsp_list

            for co in range(4):
                for nt in range(4):
                    ti = co * 4 + nt
                    bank = ti % 2

                    def f_cmm(t, _co=co, _nt=nt, _b=bank):
                        r = []
                        for kc in range(8):
                            for p in range(3):
                                lhs = [wdh, wdh, wdl][p][
                                    :, kc * CL + _co * 128:
                                    kc * CL + _co * 128 + 128]
                                rhs = [hh_fm, hl_fm, hh_fm][p][
                                    :, kc * S + _nt * 512:
                                    kc * S + _nt * 512 + 512]
                                r.append(t.matmul(
                                    PS[_b][:, :], lhs, rhs,
                                    start=(kc == 0 and p == 0),
                                    stop=(kc == 7 and p == 2)))
                        return r
                    prev_ev = c_stores[-2][0] if len(c_stores) >= 2 else None
                    dep_p1 = [h_ev2[15]] if ti == 0 and wi == 0 else []
                    dep_ps = [ev1_ops[-1]] if ti < 2 and wi == 0 else []
                    cmm = op("tensor", f_cmm,
                             deps=[wsp_list[-1], prev_ev] + dep_p1 + dep_ps,
                             desc=f"c mm {wi}.{co}.{nt}")

                    half2 = ti % 2

                    def f_cev(v, _b=bank, _h2=half2):
                        sl = slice(_h2 * 512, _h2 * 512 + 512)
                        v.tensor_copy(cbuf[:], PS[_b][:, :])
                        v.tensor_copy(cbh[:, sl], cbuf[:])
                        return v.scalar_tensor_tensor(cbl[:, sl], cbuf[:], 1.0,
                                                      cbh[:, sl], OP.mult,
                                                      OP.subtract)
                    prev_st = c_stores[-2][1] if len(c_stores) >= 2 else None
                    cev = op("vector", f_cev,
                             deps=[cmm, prev_st],
                             desc=f"c ev {wi}.{co}.{nt}")

                    def f_cst(g, _h=hs_c, _l=ls_c, _co=co, _nt=nt, _h2=half2):
                        sl = slice(_h2 * 512, _h2 * 512 + 512)
                        return [
                            g.dma_start(out=scr[_h][_co * 128:(_co + 1) * 128,
                                                    _nt * 512:(_nt + 1) * 512],
                                        in_=cbh[:, sl]),
                            g.dma_start(out=scr[_l][_co * 128:(_co + 1) * 128,
                                                    _nt * 512:(_nt + 1) * 512],
                                        in_=cbl[:, sl])]
                    cst = op("gpsimd", f_cst, deps=[cev], dma=True, n=2,
                             desc=f"c st {wi}.{co}.{nt}")
                    c_stores.append((cev, cst))
        p2_last_cmm_dep = c_stores[-1][0]
        cq_all_stored = c_stores[15][1]
        ckv_all_stored = c_stores[-1][1]

        if tap == "c":
            do_tap([cq_all_stored, ckv_all_stored],
                   [(scr["cqh_s"], [CL, S], F16), (scr["cql_s"], [CL, S], F16),
                    (scr["ckvh_s"], [CL, S], F16),
                    (scr["ckvl_s"], [CL, S], F16),
                    (hh_fm, [128, 8 * S], F16), (hl_fm, [128, 8 * S], F16),
                    (wdh, [128, 8 * CL], F16), (wdl, [128, 8 * CL], F16)])
            P.emit(blk, sems)
            return nc

        wuk_sp = []
        for c in range(4):
            slc = slice(c * D, (c + 1) * D)

            def f_ld(q, _c=c):
                return q.dma_start(out=wu32[:, :],
                                   in_=d_in["wukT"][_c * 128:(_c + 1) * 128, :])
            ldo = op("sync", f_ld,
                     deps=[wuk_sp[c - 1] if c >= 1 else None,
                           p2_last_cmm_dep if c == 0 else None],
                     dma=True, desc=f"wuk ld {c}")

            def f_sp(v, _c=slc):
                v.tensor_copy(wukh[:, _c], wu32[:, :])
                return v.scalar_tensor_tensor(wukl[:, _c], wu32[:, :], 1.0,
                                              wukh[:, _c], OP.mult,
                                              OP.subtract)
            spo = op("vector", f_sp, deps=[ldo], desc=f"wuk sp {c}")
            wuk_sp.append(spo)

        k_written = []
        prev_kev = None
        prev_t2 = None
        prev_t2s = [None, None]
        prev_kevs = [None, None]
        for nt in range(4):
            def f_ckvld(q, _nt=nt):
                return [q.dma_start(
                            out=ckv_th[:, :].rearrange("p (c n) -> p c n", c=4),
                            in_=scr["ckvh_s"][:, _nt * 512:(_nt + 1) * 512]
                            .rearrange("(c p) n -> p c n", p=128)),
                        q.dma_start(
                            out=ckv_tl[:, :].rearrange("p (c n) -> p c n", c=4),
                            in_=scr["ckvl_s"][:, _nt * 512:(_nt + 1) * 512]
                            .rearrange("(c p) n -> p c n", p=128))]
            prev_mm_dep = k_written[-1] if k_written else None
            ckld = op("sync", f_ckvld,
                      deps=[ckv_all_stored, prev_mm_dep], dma=True, n=2,
                      desc=f"ckv ld {nt}")
            for pr in range(8):
                kbank = [2, 4][pr % 2]

                def f_kmm(t, _pr=pr, _kb=kbank):
                    r = []
                    for kc in range(4):
                        for p in range(3):
                            lhs = [wukh, wukh, wukl][p][
                                :, kc * D + _pr * 128: kc * D + _pr * 128 + 128]
                            rhs = [ckv_th, ckv_tl, ckv_th][p][
                                :, kc * 512:(kc + 1) * 512]
                            r.append(t.matmul(PS[_kb][:, :], lhs, rhs,
                                              start=(kc == 0 and p == 0),
                                              stop=(kc == 3 and p == 2)))
                    return r
                kmm = op("tensor", f_kmm,
                         deps=[ckld, wuk_sp[-1], prev_kevs[pr % 2],
                               p2_last_cmm_dep
                               if nt == 0 and pr == 0 else None],
                         desc=f"k mm {nt}.{pr}")

                def f_kev(s, _kb=kbank):
                    return s.copy(qf_t[:], PS[_kb][:, :])
                kev = op("scalar", f_kev, deps=[kmm, prev_t2],
                         desc=f"k ev {nt}.{pr}")
                prev_kev = kev
                prev_kevs[pr % 2] = kev

                pbank = [3, 5][pr % 2]

                def f_kpm(t, _pb=pbank):
                    return t.matmul(PS[_pb][:, :], perm_t[:], qf_t[:],
                                    start=True, stop=True)
                kpm = op("tensor", f_kpm, deps=[kev, prev_t2s[pr % 2]],
                         desc=f"k pm {nt}.{pr}")

                def f_krope(v, _nt=nt, _pb=pbank):
                    tsl = slice(_nt * 512, (_nt + 1) * 512)
                    v.tensor_mul(qt1_t[:], qf_t[:], tc_t[:, tsl])
                    v.tensor_mul(qf_t[:], PS[_pb][:, :], ts_t[:, tsl])
                    return v.tensor_add(qt1_t[:], qt1_t[:], qf_t[:])
                krope = op("vector", f_krope, deps=[kpm, kev],
                           desc=f"k rope {nt}.{pr}")
                prev_t2 = krope
                prev_t2s[pr % 2] = krope

                def f_ksp(v, _nt=nt, _pr=pr):
                    dsl = slice(_pr * S + _nt * 512, _pr * S + (_nt + 1) * 512)
                    v.tensor_copy(kh_t[:, dsl], qt1_t[:])
                    return v.scalar_tensor_tensor(kl_t[:, dsl], qt1_t[:], 1.0,
                                                  kh_t[:, dsl], OP.mult,
                                                  OP.subtract)
                ksp = op("vector", f_ksp, deps=[], desc=f"k sp {nt}.{pr}")
                k_written.append(ksp)
        k_done = k_written[-1]

        wuv_sp = []
        for c in range(4):
            slc = slice(c * D, (c + 1) * D)

            def f_ld(q, _c=c):
                return q.dma_start(out=wu32[:, :],
                                   in_=d_in["wuvT"][_c * 128:(_c + 1) * 128, :])
            ldo = op("sync", f_ld,
                     deps=[wuk_sp[-1] if c == 0 else wuv_sp[c - 1],
                           k_done if c == 0 else None],
                     dma=True, desc=f"wuv ld {c}")

            def f_sp(v, _c=slc):
                v.tensor_copy(wuvh[:, _c], wu32[:, :])
                return v.scalar_tensor_tensor(wuvl[:, _c], wu32[:, :], 1.0,
                                              wuvh[:, _c], OP.mult,
                                              OP.subtract)
            spo = op("vector", f_sp, deps=[ldo, k_done], desc=f"wuv sp {c}")
            wuv_sp.append(spo)

        def f_vones(g):
            g.memset(vh_t[:, 0:16640].rearrange("p (k h c) -> p k h c",
                                                k=16, h=16)[:, :, :, 64:65], 1.0)
            return g.memset(vl_t[:, 0:16640].rearrange("p (k h c) -> p k h c",
                                                       k=16, h=16)
                            [:, :, :, 64:65], 0.0)
        vones = op("gpsimd", f_vones,
                   deps=[ckv_all_stored, wprep_last, h_ev2[15]],
                   desc="v ones")

        v_written = []
        prev_vev = [None, None]
        prev_vsp = [None, None]
        for sk in range(16):
            def f_cvld(q, _sk=sk):
                return [q.dma_start(
                            out=cvsh[:, :].rearrange("p (c n) -> p c n", c=4),
                            in_=scr["ckvh_s"][:, _sk * 128:(_sk + 1) * 128]
                            .rearrange("(c p) n -> p c n", p=128)),
                        q.dma_start(
                            out=cvsl[:, :].rearrange("p (c n) -> p c n", c=4),
                            in_=scr["ckvl_s"][:, _sk * 128:(_sk + 1) * 128]
                            .rearrange("(c p) n -> p c n", p=128))]
            prev_dep = prev_vev[1] if sk > 0 else k_written[-1]
            cvld = op("sync", f_cvld, deps=[ckv_all_stored, prev_dep],
                      dma=True, n=2, desc=f"cv ld {sk}")
            for ntv in range(2):
                bank = 4 + ntv

                def f_vmm(t, _ntv=ntv, _b=bank):
                    r = []
                    for kc in range(4):
                        for p in range(3):
                            lhs = [cvsh, cvsh, cvsl][p][:, kc * 128:(kc + 1) * 128]
                            rhs = [wuvh, wuvl, wuvh][p][
                                :, kc * D + _ntv * 512: kc * D + _ntv * 512 + 512]
                            r.append(t.matmul(PS[_b][:, :], lhs, rhs,
                                              start=(kc == 0 and p == 0),
                                              stop=(kc == 3 and p == 2)))
                    return r
                vmm = op("tensor", f_vmm,
                         deps=[cvld, wuv_sp[-1], prev_vev[ntv]],
                         desc=f"v mm {sk}.{ntv}")

                def f_vev(s, _b=bank, _ntv=ntv):
                    dst = anf_t[:, 0:512] if _ntv == 0 else anf_t[:, 512:1024]
                    return s.copy(dst[:], PS[_b][:, :])
                vev = op("scalar", f_vev,
                         deps=[vmm, k_written[-1] if sk == 0 else None,
                               prev_vsp[ntv]],
                         desc=f"v ev {sk}.{ntv}")
                prev_vev[ntv] = vev

                def f_vsp(v, _sk=sk, _ntv=ntv):
                    src = anf_t[:, 0:512] if _ntv == 0 else anf_t[:, 512:1024]
                    r = []
                    for hh in range(8):
                        hd = _ntv * 8 + hh
                        dsl = slice(_sk * 1040 + hd * 65,
                                    _sk * 1040 + hd * 65 + 64)
                        ssl = slice(hh * 64, (hh + 1) * 64)
                        v.tensor_copy(vh_t[:, dsl], src[:, ssl])
                    for hh in range(8):
                        hd = _ntv * 8 + hh
                        dsl = slice(_sk * 1040 + hd * 65,
                                    _sk * 1040 + hd * 65 + 64)
                        ssl = slice(hh * 64, (hh + 1) * 64)
                        r.append(v.scalar_tensor_tensor(
                            vl_t[:, dsl], src[:, ssl], 1.0, vh_t[:, dsl],
                            OP.mult, OP.subtract))
                    return r
                vsp = op("vector", f_vsp, deps=[vev, vones],
                         desc=f"v sp {sk}.{ntv}")
                prev_vsp[ntv] = vsp
                v_written.append(vsp)
        v_done = v_written[-1]

        if tap == "kv":
            do_tap([k_done, v_done],
                   [(kh_t, [128, 8 * S], F16), (kl_t, [128, 8 * S], F16),
                    (vh_t, [128, 16640], F16), (vl_t, [128, 16640], F16)])
            P.emit(blk, sems)
            return nc

        an_spills = []
        blk_tail = {"qev": None, "zread": None, "bcast": None, "anorm": None,
                    "sc": {}, "ex": {}, "sp": {}, "av": {}}
        prev_block = {}
        for sq in range(NSQ):
            def f_cqld(q, _sq=sq):
                return [q.dma_start(
                            out=cqh_t[:, :].rearrange("p (c n) -> p c n", c=4),
                            in_=scr["cqh_s"][:, _sq * 512:(_sq + 1) * 512]
                            .rearrange("(c p) n -> p c n", p=128)),
                        q.dma_start(
                            out=cql_t[:, :].rearrange("p (c n) -> p c n", c=4),
                            in_=scr["cql_s"][:, _sq * 512:(_sq + 1) * 512]
                            .rearrange("(c p) n -> p c n", p=128))]
            dep0 = [cq_all_stored]
            if sq == 0:
                dep0 += [v_written[-1], k_written[-1]]
            else:
                dep0 += [prev_block["last_sc"]]
            cqld = op("sync", f_cqld, deps=dep0, dma=True, n=2,
                      desc=f"cq ld {sq}")

            for pr in range(NPAIR):
                b = sq * NPAIR + pr
                ring = b % 2

                def f_wuqld(g, _pr=pr, _r=ring):
                    sl = slice(_r * 512, _r * 512 + 512)
                    return [g.dma_start(
                                out=wuqh_t[:, sl].rearrange("p (c n) -> p c n", c=4),
                                in_=scr["wuqh_s"][:, _pr * 128:(_pr + 1) * 128]
                                .rearrange("(c p) n -> p c n", p=128)),
                            g.dma_start(
                                out=wuql_t[:, sl].rearrange("p (c n) -> p c n", c=4),
                                in_=scr["wuql_s"][:, _pr * 128:(_pr + 1) * 128]
                                .rearrange("(c p) n -> p c n", p=128))]
                wuqld = op("sync", f_wuqld,
                           deps=[wprep_last if b == 0 else None,
                                 prev_block.get("qmm2")],
                           dma=True, n=2, desc=f"wuq ld {b}")

                def f_qmm(t, _r=ring):
                    r = []
                    for kc in range(4):
                        for p in range(3):
                            lhs = [wuqh_t, wuqh_t, wuql_t][p][
                                :, _r * 512 + kc * 128: _r * 512 + kc * 128 + 128]
                            rhs = [cqh_t, cql_t, cqh_t][p][
                                :, kc * 512:(kc + 1) * 512]
                            r.append(t.matmul(PS[0][:, :], lhs, rhs,
                                              start=(kc == 0 and p == 0),
                                              stop=(kc == 3 and p == 2)))
                    return r
                qmm = op("tensor", f_qmm,
                         deps=[cqld, wuqld, prev_block.get("qev"),
                               prev_block.get(("ex", 14)),
                               v_done if b == 0 else None],
                         desc=f"q mm {b}")
                prev_block["qmm2"] = qmm

                def f_qev(s):
                    return s.copy(qf_t[:], PS[0][:, :])
                qev = op("scalar", f_qev,
                         deps=[qmm, prev_block.get("rope"),
                               prev_block.get("zbc")],
                         desc=f"q ev {b}")
                prev_block["qev"] = qev

                def f_qpm(t):
                    return t.matmul(PS[1][:, :], perm_t[:], qf_t[:],
                                    start=True, stop=True)
                qpm = op("tensor", f_qpm,
                         deps=[qev, prev_block.get("rope"),
                               prev_block.get(("ex", 14)),
                               wuv_sp[-1] if b == 0 else None],
                         desc=f"q pm {b}")

                def f_qrope(v, _sq=sq):
                    tsl = slice(_sq * 512, (_sq + 1) * 512)
                    v.tensor_mul(qt1_t[:], qf_t[:], tc_t[:, tsl])
                    v.tensor_mul(qf_t[:], PS[1][:, :], ts_t[:, tsl])
                    v.tensor_add(qt1_t[:], qt1_t[:], qf_t[:])
                    v.tensor_copy(qhi_t[:], qt1_t[:])
                    return v.scalar_tensor_tensor(qlo_t[:], qt1_t[:], 1.0,
                                                  qhi_t[:], OP.mult,
                                                  OP.subtract)
                qrope = op("vector", f_qrope,
                           deps=[qpm, qev, prev_block.get("anorm")],
                           desc=f"q rope {b}")
                prev_block["rope"] = qrope

                if tap == "q0" and b == 0:
                    do_tap([qrope, qev, wuqld],
                           [(qt1_t, [128, 512], F32),
                            (qf_t, [128, 512], F32),
                            (cqh_t, [128, 4 * 512], F16),
                            (cql_t, [128, 4 * 512], F16),
                            (wuqh_t, [128, 2 * 512], F16),
                            (wuql_t, [128, 2 * 512], F16),
                            (scr["wuqh_s"], [CL, D], F16),
                            (scr["wuql_s"], [CL, D], F16)])
                    P.emit(blk, sems)
                    return nc

                sc_ops = {}
                ex_ops = {}
                sp_ops = {}
                av_ops = {}

                SCBK = [(2, 3), (4, 5), (0, 1)]

                def mk_sc(j):
                    def f_sc(t, _pr=pr, _j=j):
                        bk = SCBK[_j % 3]
                        r = []
                        for hh in range(2):
                            rr = slice(hh * 64, hh * 64 + 64)
                            tp = (hh * 64, 0)
                            ksl = slice(_pr * S + _j * 128,
                                        _pr * S + _j * 128 + 128)
                            for p in range(3):
                                lhs = [kh_t, kh_t, kl_t][p][rr, ksl]
                                rhs = [qhi_t, qlo_t, qhi_t][p][rr, :]
                                r.append(t.matmul(PS[bk[hh]][:, :], lhs, rhs,
                                                  start=(p == 0), stop=(p == 2),
                                                  tile_position=tp))
                        return r
                    return f_sc

                def mk_ex2(j):
                    def f_ex(s, _j=j):
                        bk0, bk1 = SCBK[_j % 3]
                        rg = _j % 2
                        s.activation(expf[:, rg * 512:rg * 512 + 512],
                                     PS[bk0][:, :], AF.Exp, scale=SCALE)
                        return s.activation(anf_t[:, rg * 512:rg * 512 + 512],
                                            PS[bk1][:, :], AF.Exp, scale=SCALE)
                    return f_ex

                def mk_sp(j):
                    def f_sp(v, _j=j):
                        rg = _j % 2
                        r3 = _j % 3
                        r = []
                        for hh, srct in ((0, expf), (1, anf_t)):
                            ssl = slice(rg * 512, rg * 512 + 512)
                            dsl = slice((hh * 3 + r3) * 512,
                                        (hh * 3 + r3) * 512 + 512)
                            v.tensor_copy(exph[:, dsl], srct[:, ssl])
                            r.append(v.scalar_tensor_tensor(
                                expl[:, dsl], srct[:, ssl], 1.0, exph[:, dsl],
                                OP.mult, OP.subtract))
                        return r
                    return f_sp

                def mk_av(j):
                    def f_av(t, _pr=pr, _j=j):
                        r3 = _j % 3
                        r = []
                        for hh in range(2):
                            hd = _pr * 2 + hh
                            vsl = slice(_j * 1040 + hd * 65,
                                        _j * 1040 + hd * 65 + 65)
                            esl = slice((hh * 3 + r3) * 512,
                                        (hh * 3 + r3) * 512 + 512)
                            for p in range(3):
                                lhs = [vh_t, vh_t, vl_t][p][:, vsl]
                                rhs = [exph, expl, exph][p][:, esl]
                                r.append(t.matmul(
                                    PS[6 + hh][0:65, :], lhs, rhs,
                                    start=(_j == 0 and p == 0),
                                    stop=(_j == 15 and p == 2)))
                        return r
                    return f_av

                def mk_sc_deps(j):
                    deps_sc = [qrope]
                    if j >= 3:
                        deps_sc.append(ex_ops[j - 3])
                    else:
                        deps_sc.append(prev_block.get(("ex", [15, 13, 14][j])))
                    if j == 0:
                        deps_sc.append(prev_block.get("anorm"))
                    if b == 0 and j == 0:
                        deps_sc += [prev_kev, prev_t2]
                    if b == 0 and j == 1:
                        deps_sc += [prev_vev[0], prev_vev[1]]
                    return deps_sc

                def mk_av_deps(j):
                    deps_av = [sp_ops[j]]
                    if j == 0:
                        deps_av.append(prev_block.get("ansc"))
                        deps_av.append(prev_block.get("anorm"))
                        if b == 0:
                            deps_av.append(v_done)
                    return deps_av

                def emit_exsp(j):
                    deps_ex = [sc_ops[j]]
                    if j >= 2:
                        deps_ex.append(sp_ops[j - 2])
                    else:
                        deps_ex.append(prev_block.get(("sp", 14 + j)))
                    if j < 2:
                        deps_ex.append(prev_block.get("anorm"))
                    ex_ops[j] = op("scalar", mk_ex2(j), deps=deps_ex,
                                   desc=f"ex {b}.{j}")
                    deps_sp = [ex_ops[j]]
                    if j >= 3:
                        deps_sp.append(av_ops[j - 3])
                    else:
                        deps_sp.append(prev_block.get(("av", 13 + j)))
                    sp_ops[j] = op("vector", mk_sp(j), deps=deps_sp,
                                   desc=f"sp {b}.{j}")

                for j in range(NCH):
                    sc_ops[j] = op("tensor", mk_sc(j), deps=mk_sc_deps(j),
                                   desc=f"sc {b}.{j}")
                    emit_exsp(j)
                    if j >= 2:
                        jj = j - 2
                        av_ops[jj] = op("tensor", mk_av(jj),
                                        deps=mk_av_deps(jj),
                                        desc=f"av {b}.{jj}")
                for jj in (14, 15):
                    av_ops[jj] = op("tensor", mk_av(jj), deps=mk_av_deps(jj),
                                    desc=f"av {b}.{jj}")

                def f_zcp(s):
                    s.copy(zsc_t[0:1, 0:512], PS[6][64:65, :])
                    return s.copy(zsc_t[0:1, 512:1024], PS[7][64:65, :])
                zcp = op("scalar", f_zcp,
                         deps=[av_ops[15], qrope], desc=f"z cp {b}")

                def f_zbc(t):
                    t.matmul(PS[2][:, :], onesr_t[0:1, :], zsc_t[0:1, 0:512],
                             start=True, stop=True)
                    return t.matmul(PS[3][:, :], onesr_t[0:1, :],
                                    zsc_t[0:1, 512:1024], start=True, stop=True)
                zbc = op("tensor", f_zbc,
                         deps=[zcp, ex_ops[15], av_ops[15]], desc=f"z bc {b}")
                prev_block["zbc"] = zbc

                def f_ansc(s):
                    s.copy(anf_t[0:64, 0:512], PS[6][0:64, :])
                    return s.copy(anf_t[0:64, 512:1024], PS[7][0:64, :])
                ansc = op("scalar", f_ansc,
                          deps=[av_ops[15], sp_ops[15]], desc=f"an sc {b}")
                prev_block["ansc"] = ansc

                def f_anorm(v):
                    r = []
                    for hh in range(2):
                        asl = slice(hh * 512, hh * 512 + 512)
                        v.reciprocal(zinv_t[hh][0:64, :], PS[2 + hh][0:64, :])
                        v.tensor_mul(anf_t[0:64, asl], anf_t[0:64, asl],
                                     zinv_t[hh][0:64, :])
                        v.tensor_copy(an16h[0:64, asl], anf_t[0:64, asl])
                        r.append(v.scalar_tensor_tensor(
                            an16l[0:64, asl], anf_t[0:64, asl], 1.0,
                            an16h[0:64, asl], OP.mult, OP.subtract))
                    return r
                anorm = op("vector", f_anorm,
                           deps=[ansc, zbc,
                                 an_spills[-1] if an_spills else None],
                           desc=f"an norm {b}")
                prev_block["anorm"] = anorm

                def f_anst(g, _sq=sq, _pr=pr):
                    r = []
                    for hh in range(2):
                        asl = slice(hh * 512, hh * 512 + 512)
                        rr = slice(_pr * 128 + hh * 64, _pr * 128 + hh * 64 + 64)
                        cc = slice(_sq * 512, (_sq + 1) * 512)
                        r.append(g.dma_start(out=scr["anh_s"][rr, cc],
                                             in_=an16h[0:64, asl]))
                        r.append(g.dma_start(out=scr["anl_s"][rr, cc],
                                             in_=an16l[0:64, asl]))
                    return r
                anst = op("gpsimd", f_anst, deps=[anorm], dma=True, n=4,
                          desc=f"an st {b}")
                an_spills.append(anst)
                prev_block["anst"] = anst

                prev_block["last_sc"] = sc_ops[15]
                for j in range(NCH):
                    prev_block[("ex", j)] = ex_ops[j]
                    prev_block[("sp", j)] = sp_ops[j]
                    prev_block[("av", j)] = av_ops[j]
        attn_done = an_spills[-1]
        attn_done_compute = prev_block["anorm"]

        if tap == "an":
            do_tap([attn_done], [(scr["anh_s"], [D, S], F16),
                                 (scr["anl_s"], [D, S], F16)])
            P.emit(blk, sems)
            return nc

        x1_written = []
        prev_sq_norm = None
        prev_wout_mms = None
        for sq in range(NSQ):
            def f_anld(q, _sq=sq):
                cc = slice(_sq * 512, (_sq + 1) * 512)
                return [q.dma_start(out=anrh[:, :].rearrange("p (c n) -> p c n", c=8),
                                    in_=scr["anh_s"][:, cc]
                                    .rearrange("(c p) n -> p c n", p=128)),
                        q.dma_start(out=anrl[:, :].rearrange("p (c n) -> p c n", c=8),
                                    in_=scr["anl_s"][:, cc]
                                    .rearrange("(c p) n -> p c n", p=128))]
            anld = op("sync", f_anld,
                      deps=[an_spills[(sq + 1) * NPAIR - 1],
                            an_spills[-1] if sq == 0 else None,
                            prev_sq_norm],
                      dma=True, n=2, desc=f"an ld {sq}")

            wout_mms = []
            aw_ops = []
            r1_ops = []
            tp_ops = []
            for m in range(8):
                ring = m % 2

                def f_wold(g, _m=m, _r=ring):
                    sl = slice(_r * 1024, _r * 1024 + 1024)
                    return [g.dma_start(
                                out=wo_th[:, sl].rearrange("p (c n) -> p c n", c=8),
                                in_=scr["wouth_s"][:, _m * 128:(_m + 1) * 128]
                                .rearrange("(c p) n -> p c n", p=128)),
                            g.dma_start(
                                out=wo_tl[:, sl].rearrange("p (c n) -> p c n", c=8),
                                in_=scr["woutl_s"][:, _m * 128:(_m + 1) * 128]
                                .rearrange("(c p) n -> p c n", p=128))]
                if m >= 2:
                    dep_wo = [wout_mms[m - 2]]
                elif sq == 0:
                    dep_wo = [attn_done, attn_done_compute]
                else:
                    dep_wo = [prev_wout_mms[6 + m]]
                wold = op("sync", f_wold, deps=dep_wo,
                          dma=True, n=2, desc=f"wo ld {sq}.{m}")

                def f_womm(t, _m=m, _r=ring):
                    r = []
                    for kc in range(8):
                        for p in range(3):
                            lhs = [wo_th, wo_th, wo_tl][p][
                                :, _r * 1024 + kc * 128:
                                _r * 1024 + kc * 128 + 128]
                            rhs = [anrh, anrl, anrh][p][
                                :, kc * 512:(kc + 1) * 512]
                            r.append(t.matmul(PS[_m][:, :], lhs, rhs,
                                              start=(kc == 0 and p == 0),
                                              stop=(kc == 7 and p == 2)))
                    return r
                dep_ps = [attn_done_compute] if sq == 0 else [prev_sq_norm]
                womm = op("tensor", f_womm,
                          deps=[anld, wold] + dep_ps, desc=f"wo mm {sq}.{m}")
                wout_mms.append(womm)

                def f_aw(s, _m=m):
                    return s.copy(aw_t[:], PS[_m][:, :])
                awo = op("scalar", f_aw,
                         deps=[womm, r1_ops[m - 1] if m >= 1 else None],
                         desc=f"aw {sq}.{m}")
                aw_ops.append(awo)

                def f_xld(q, _sq=sq, _m=m, _r=ring):
                    sl = slice(_r * 512, _r * 512 + 512)
                    r = []
                    for i in range(4):
                        r.append(q.dma_start(
                            out=xtm4[:, _r * 512 + i * 128:
                                     _r * 512 + (i + 1) * 128],
                            in_=d_in["x"][_sq * 512 + i * 128:
                                          _sq * 512 + (i + 1) * 128,
                                          _m * 128:(_m + 1) * 128]))
                    return r
                xld4 = op("sync", f_xld,
                          deps=[anld, tp_ops[m - 2] if m >= 2 else
                                (prev_sq_norm if sq > 0 else None)],
                          dma=True, n=4, desc=f"x4 ld {sq}.{m}")

                def f_xtp(t, _m=m, _r=ring):
                    r = []
                    for i in range(4):
                        r.append(t.transpose(
                            PS[_m][:, i * 128:(i + 1) * 128],
                            xtm4[:, _r * 512 + i * 128:_r * 512 + (i + 1) * 128],
                            id32_t[:]))
                    return r
                xtp = op("tensor", f_xtp, deps=[xld4, awo],
                         desc=f"x tp {sq}.{m}")
                tp_ops.append(xtp)

                def f_r1(v, _m=m):
                    return v.tensor_add(r1_t[:, _m * 512:(_m + 1) * 512],
                                        aw_t[:], PS[_m][:, :])
                dep_r1 = [xtp, awo]
                if sq == 0 and m == 0:
                    dep_r1.append(attn_done_compute)
                r1o = op("vector", f_r1, deps=dep_r1, desc=f"r1 {sq}.{m}")
                r1_ops.append(r1o)

            st_mms = []
            sq_ops = []
            for m in range(8):
                def f_sqm(v, _m=m):
                    return v.tensor_mul(sqm_t[:], r1_t[:, _m * 512:(_m + 1) * 512],
                                        r1_t[:, _m * 512:(_m + 1) * 512])
                sqo = op("vector", f_sqm,
                         deps=[r1_ops[m],
                               st_mms[m - 1] if m >= 1 else None],
                         desc=f"sqm {sq}.{m}")
                sq_ops.append(sqo)

                def f_stm(t, _m=m):
                    a = t.matmul(PS[0][:, :], e1_t[:],
                                 r1_t[:, _m * 512:(_m + 1) * 512],
                                 start=(_m == 0), stop=(_m == 7))
                    bmm_ = t.matmul(PS[1][:, :], e1_t[:], sqm_t[:],
                                    start=(_m == 0), stop=(_m == 7))
                    return bmm_
                stm = op("tensor", f_stm,
                         deps=[sqo, r1_ops[m],
                               r1_ops[0] if m == 0 else None,
                               r1_ops[1] if m == 0 else None],
                         desc=f"st mm {sq}.{m}")
                st_mms.append(stm)

            def f_rows1(s):
                s.activation(rowA[0:1, :], PS[0][0:1, :], AF.Identity,
                             scale=INV_D)
                return s.activation(rowB[0:1, :], PS[1][0:1, :], AF.Identity,
                                    scale=INV_D)
            rows1 = op("scalar", f_rows1, deps=[st_mms[-1]],
                       desc=f"rows1 {sq}")

            def f_rows2(v):
                v.tensor_mul(rowC[0:1, :], rowA[0:1, :], rowA[0:1, :])
                v.tensor_sub(rowD[0:1, :], rowB[0:1, :], rowC[0:1, :])
                return v.tensor_scalar(rowD[0:1, :], rowD[0:1, :], EPS, None,
                                       OP.add)
            rows2 = op("vector", f_rows2, deps=[rows1], desc=f"rows2 {sq}")

            def f_rows3(s):
                s.activation(rowC[0:1, :], rowD[0:1, :], AF.Ln)
                return s.activation(rowC[0:1, :], rowC[0:1, :], AF.Exp,
                                    scale=-0.5)
            rows3 = op("scalar", f_rows3, deps=[rows2], desc=f"rows3 {sq}")

            def f_bc2(t):
                t.matmul(PS[2][:, :], onesr_t[0:1, :], rowA[0:1, :],
                         start=True, stop=True)
                return t.matmul(PS[3][:, :], onesr_t[0:1, :], rowC[0:1, :],
                                start=True, stop=True)
            bc2 = op("tensor", f_bc2,
                     deps=[rows3, rows2, r1_ops[2], r1_ops[3]],
                     desc=f"bc2 {sq}")

            norm_ops = []
            for m in range(8):
                def f_norm(v, _sq=sq, _m=m):
                    v.tensor_sub(aw_t[:], r1_t[:, _m * 512:(_m + 1) * 512],
                                 PS[2][:, :])
                    return v.tensor_mul(
                        x1_t[:, _m * S + _sq * 512: _m * S + _sq * 512 + 512],
                        aw_t[:], PS[3][:, :])
                depn = [bc2, aw_ops[-1]]
                if sq == 0 and m == 0:
                    depn.append(k_done)
                    depn.append(prev_block["last_sc"])
                nrm = op("vector", f_norm, deps=depn, desc=f"x1 {sq}.{m}")
                norm_ops.append(nrm)
                x1_written.append(nrm)
            prev_sq_norm = norm_ops[-1]
            prev_wout_mms = wout_mms
        x1_done = x1_written[-1]

        if tap == "x1":
            do_tap([x1_done], [(x1_t, [128, 8 * S], F32)])
            P.emit(blk, sems)
            return nc

        def f_stash(v):
            r = []
            for c in range(8):
                r.append(v.tensor_copy(stash_t[:, c:c + 1],
                                       x1_t[:, c * S + 1023:c * S + 1024]))
            return r
        stash = op("vector", f_stash, deps=[x1_done], desc="stash")

        y_stores = []
        yev_ops = []
        ynorm_ops = []

        def emit_p6(i, extra_deps=()):
            ring = i % 2
            b6 = ring * 4

            def f_ytp(t, _i=i, _b=b6):
                r = []
                for c in range(8):
                    r.append(t.transpose(
                        PS[_b + c // 4][:, (c % 4) * 128:(c % 4 + 1) * 128],
                        x1_t[:, c * S + _i * 128:c * S + (_i + 1) * 128],
                        id32_t[:]))
                return r
            dep_tp = list(extra_deps)
            if i >= 2:
                dep_tp.append(yev_ops[i - 2])
            ytp = op("tensor", f_ytp, deps=dep_tp, desc=f"y tp {i}")

            def f_yev(v, _r=ring, _b=b6):
                sl = slice(_r * D, _r * D + D)
                v.tensor_copy(ytm[:, sl][:, 0:512], PS[_b][:, :])
                return v.tensor_copy(ytm[:, sl][:, 512:1024], PS[_b + 1][:, :])
            dep_ev = [ytp]
            if i >= 2:
                dep_ev.append(y_stores[i - 2])
            yev = op("vector", f_yev, deps=dep_ev, desc=f"y ev {i}")
            yev_ops.append(yev)

            def f_yst1a(v, _r=ring):
                sl = slice(_r * D, _r * D + D)
                return v.reduce_sum(yst_t[:, 0:1], ytm[:, sl], axis=AX.X)
            yst1a = op("vector", f_yst1a, deps=[], desc=f"y sum {i}")

            def f_yst1b(s):
                return s.activation(yst_t[:, 1:2], yst_t[:, 0:1], AF.Identity,
                                    scale=INV_D)
            yst1 = op("scalar", f_yst1b, deps=[yst1a], desc=f"y mu {i}")

            def f_yst2a(v, _r=ring):
                sl = slice(_r * D, _r * D + D)
                return v.scalar_tensor_tensor(
                    yjunk[:], ytm[:, sl], yst_t[:, 1:2], ytm[:, sl],
                    OP.subtract, OP.mult, accum_out=yst_t[:, 2:3])
            yst2a = op("vector", f_yst2a,
                       deps=[yst1, ynorm_ops[i - 1] if i >= 1 else None],
                       desc=f"y ssq {i}")

            def f_yst3(s):
                s.activation(yst_t[:, 3:4], yst_t[:, 2:3], AF.Identity,
                             scale=INV_D, bias=eps_t[:, 0:1])
                s.activation(yst_t[:, 4:5], yst_t[:, 3:4], AF.Ln)
                return s.activation(yst_t[:, 5:6], yst_t[:, 4:5], AF.Exp,
                                    scale=-0.5)
            yst3 = op("scalar", f_yst3,
                      deps=[yst2a, ynorm_ops[i - 1] if i >= 1 else None],
                      desc=f"y st3 {i}")

            def f_ynorm(v, _r=ring):
                sl = slice(_r * D, _r * D + D)
                return v.tensor_scalar(ytm[:, sl], ytm[:, sl], yst_t[:, 1:2],
                                       yst_t[:, 5:6], OP.subtract, OP.mult)
            ynorm = op("vector", f_ynorm, deps=[yst3], desc=f"y norm {i}")
            ynorm_ops.append(ynorm)

            def f_yst(q, _i=i, _r=ring):
                sl = slice(_r * D, _r * D + D)
                return q.dma_start(out=y_d[_i * 128:(_i + 1) * 128, :],
                                   in_=ytm[:, sl])
            ysto = op("sync", f_yst, deps=[ynorm], dma=True, desc=f"y st {i}")
            y_stores.append(ysto)

        x2_written = []
        prev_half = {}
        for hf in range(2):
            hw = slice(hf * 1024, hf * 1024 + 1024)
            mix_ops = {}
            for tag, mu_t, dsth, dstl in (("k", muk_t, xmh_t, xml_t),):
                pass
            for phase, (mu_t,) in enumerate([(muk_t,), (mur_t,)]):
                tag = "xk" if phase == 0 else "xr"
                mx_list = []
                for c in range(8):
                    def f_mix(v, _c=c, _hf=hf, _mu=mu_t, _ph=phase):
                        xsl = slice(_c * S + _hf * 1024,
                                    _c * S + _hf * 1024 + 1024)
                        xprev = slice(_c * S + _hf * 1024 - 1,
                                      _c * S + _hf * 1024 + 1023)
                        r = []
                        if _hf == 0:
                            v.tensor_copy(dbuf[:, 0:1],
                                          x1_t[:, _c * S:_c * S + 1])
                            v.tensor_sub(dbuf[:, 1:1024],
                                         x1_t[:, _c * S + 1:_c * S + 1024],
                                         x1_t[:, _c * S:_c * S + 1023])
                            v.tensor_scalar(xfbuf[:, 0:1], dbuf[:, 0:1],
                                            _mu[:, _c:_c + 1], None, OP.mult)
                            v.scalar_tensor_tensor(
                                xfbuf[:, 1:1024], dbuf[:, 1:1024],
                                _mu[:, _c:_c + 1],
                                x1_t[:, _c * S:_c * S + 1023],
                                OP.mult, OP.add)
                        else:
                            v.scalar_tensor_tensor(
                                dbuf[:, 0:1],
                                x1_t[:, _c * S + 1024:_c * S + 1025],
                                1.0, stash_t[:, _c:_c + 1],
                                OP.mult, OP.subtract)
                            v.tensor_sub(dbuf[:, 1:1024],
                                         x1_t[:, _c * S + 1025:_c * S + 2048],
                                         x1_t[:, _c * S + 1024:_c * S + 2047])
                            v.scalar_tensor_tensor(
                                xfbuf[:, 0:1], dbuf[:, 0:1],
                                _mu[:, _c:_c + 1], stash_t[:, _c:_c + 1],
                                OP.mult, OP.add)
                            v.scalar_tensor_tensor(
                                xfbuf[:, 1:1024], dbuf[:, 1:1024],
                                _mu[:, _c:_c + 1],
                                x1_t[:, _c * S + 1024:_c * S + 2047],
                                OP.mult, OP.add)
                        dsl = slice(_c * 1024, (_c + 1) * 1024)
                        v.tensor_copy(xmh_t[:, dsl], xfbuf[:])
                        r.append(v.scalar_tensor_tensor(
                            xml_t[:, dsl], xfbuf[:], 1.0, xmh_t[:, dsl],
                            OP.mult, OP.subtract))
                        return r
                    depm = [stash, c_ld]
                    if phase == 0 and c == 0:
                        if hf == 0:
                            depm.append(x1_done)
                        else:
                            depm.append(prev_half["x2"])
                        depm.append(prev_half.get("last_rmm"))
                    if phase == 1 and c == 0:
                        depm.append(kk_mms[-1])
                    mx = op("vector", f_mix, deps=depm,
                            desc=f"{tag} mix {hf}.{c}")
                    mx_list.append(mx)
                if phase == 0:
                    xk_mix = mx_list
                else:
                    xr_mix = mx_list

                if phase == 0:
                    kk_mms = []
                    ka_ops = []
                    kar_ops = []
                    for m in range(8):
                        ring = m % 2

                        def f_wld(g, _m=m, _r=ring):
                            return [g.dma_start(
                                        out=wf_h[_r][:, 0:1024].rearrange("p (c n) -> p c n", c=8),
                                        in_=scr["wkh_s"][:, _m * 128:(_m + 1) * 128]
                                        .rearrange("(c p) n -> p c n", p=128)),
                                    g.dma_start(
                                        out=wf_l[_r][:, 0:1024].rearrange("p (c n) -> p c n", c=8),
                                        in_=scr["wkl_s"][:, _m * 128:(_m + 1) * 128]
                                        .rearrange("(c p) n -> p c n", p=128))]
                        dep_w = []
                        if hf == 0 and m < 2:
                            dep_w = [attn_done, wprep_last,
                                     prev_sq_norm]
                        wldk = op("sync", f_wld,
                                  deps=dep_w + [kk_mms[-3] if m >= 2 else
                                                prev_half.get("last_rmm")],
                                  dma=True, n=2, desc=f"wk ld {hf}.{m}")

                        for ntv in range(2):
                            bank = ntv

                            def f_kkmm(t, _m=m, _r=ring, _n=ntv, _b=bank):
                                r = []
                                for kc in range(8):
                                    for p in range(3):
                                        lhs = [wf_h[_r], wf_h[_r], wf_l[_r]][p][
                                            :, kc * 128:kc * 128 + 128]
                                        rhs = [xmh_t, xml_t, xmh_t][p][
                                            :, kc * 1024 + _n * 512:
                                            kc * 1024 + _n * 512 + 512]
                                        r.append(t.matmul(
                                            PS[_b][:, :], lhs, rhs,
                                            start=(kc == 0 and p == 0),
                                            stop=(kc == 7 and p == 2)))
                                return r
                            dep_kk = [wldk, xk_mix[-1]]
                            if m == 0 and hf == 0:
                                dep_kk.append(x1_done)
                            if m >= 1:
                                dep_kk.append(kar_ops[-2])
                            kkm = op("tensor", f_kkmm, deps=dep_kk,
                                     desc=f"kk mm {hf}.{m}.{ntv}")
                            kk_mms.append(kkm)

                            def f_karelu(s, _b=bank):
                                return s.activation(relu_t[:], PS[_b][:, :],
                                                    AF.Relu)
                            kar = op("scalar", f_karelu,
                                     deps=[kkm,
                                           ka_ops[-1] if ka_ops else None],
                                     desc=f"ka relu {hf}.{m}.{ntv}")
                            kar_ops.append(kar)

                            def f_kasp(v, _m=m, _n=ntv):
                                dsl = slice(_m * 1024 + _n * 512,
                                            _m * 1024 + _n * 512 + 512)
                                v.tensor_mul(k2_t[:], relu_t[:], relu_t[:])
                                v.tensor_copy(kah_t[:, dsl], k2_t[:])
                                return v.scalar_tensor_tensor(
                                    kal_t[:, dsl], k2_t[:], 1.0,
                                    kah_t[:, dsl], OP.mult, OP.subtract)
                            dep_ka = [kar]
                            if hf == 0 and m == 0:
                                dep_ka.append(attn_done_compute)
                            kasp = op("vector", f_kasp, deps=dep_ka,
                                      desc=f"ka sp {hf}.{m}.{ntv}")
                            ka_ops.append(kasp)
                    kk_last = kk_mms[-1]
                    if hf == 1:
                        kar_last_hf1 = kar_ops[-1]
                    if tap == "ffn1" and hf == 0:
                        do_tap([ka_ops[-1], xk_mix[-1]],
                               [(xmh_t, [128, 8 * 1024], F16),
                                (xml_t, [128, 8 * 1024], F16),
                                (kah_t, [128, 8 * 1024], F16),
                                (kal_t, [128, 8 * 1024], F16)])
                        P.emit(blk, sems)
                        return nc

            rv_ops = []
            rmms = []
            sig_ops = []
            last_rmm = None
            for m in range(8):
                ring = m % 2

                def f_wldrv(g, _m=m, _r=ring):
                    return [g.dma_start(
                                out=wf_h[_r][:, 0:1024].rearrange("p (c n) -> p c n", c=8),
                                in_=scr["wrh_s"][:, _m * 128:(_m + 1) * 128]
                                .rearrange("(c p) n -> p c n", p=128)),
                            g.dma_start(
                                out=wf_l[_r][:, 0:1024].rearrange("p (c n) -> p c n", c=8),
                                in_=scr["wrl_s"][:, _m * 128:(_m + 1) * 128]
                                .rearrange("(c p) n -> p c n", p=128)),
                            g.dma_start(
                                out=wf_h[_r][:, 1024:2048].rearrange("p (c n) -> p c n", c=8),
                                in_=scr["wvh_s"][:, _m * 128:(_m + 1) * 128]
                                .rearrange("(c p) n -> p c n", p=128)),
                            g.dma_start(
                                out=wf_l[_r][:, 1024:2048].rearrange("p (c n) -> p c n", c=8),
                                in_=scr["wvl_s"][:, _m * 128:(_m + 1) * 128]
                                .rearrange("(c p) n -> p c n", p=128))]
                dep_w = [rmms[-3]] if m >= 2 else [kk_last]
                wldrv = op("sync", f_wldrv, deps=dep_w, dma=True, n=4,
                           desc=f"wrv ld {hf}.{m}")

                for ntv in range(2):
                    rbank = 2
                    vbank = 3

                    def f_rmm_r(t, _m=m, _r=ring, _n=ntv, _rb=rbank):
                        r = []
                        for kc in range(8):
                            for p in range(3):
                                lhs = [wf_h[_r], wf_h[_r], wf_l[_r]][p][
                                    :, kc * 128:kc * 128 + 128]
                                rhs = [xmh_t, xml_t, xmh_t][p][
                                    :, kc * 1024 + _n * 512:
                                    kc * 1024 + _n * 512 + 512]
                                r.append(t.matmul(PS[_rb][:, :], lhs, rhs,
                                                  start=(kc == 0 and p == 0),
                                                  stop=(kc == 7 and p == 2)))
                        return r
                    dep_rr = [wldrv, xr_mix[-1], ka_ops[-1]]
                    if sig_ops:
                        dep_rr.append(sig_ops[-1])
                    elif hf == 0:
                        dep_rr.append(x1_done)
                    rmm_r = op("tensor", f_rmm_r, deps=dep_rr,
                               desc=f"r mm {hf}.{m}.{ntv}")

                    def f_rmm_v(t, _m=m, _r=ring, _n=ntv, _vb=vbank):
                        r = []
                        for kc in range(8):
                            for p in range(3):
                                lhs = [wf_h[_r], wf_h[_r], wf_l[_r]][p][
                                    :, 1024 + kc * 128:1024 + kc * 128 + 128]
                                rhs = [kah_t, kal_t, kah_t][p][
                                    :, kc * 1024 + _n * 512:
                                    kc * 1024 + _n * 512 + 512]
                                r.append(t.matmul(PS[_vb][:, :], lhs, rhs,
                                                  start=(kc == 0 and p == 0),
                                                  stop=(kc == 7 and p == 2)))
                        return r
                    dep_rv = [wldrv, ka_ops[-1]]
                    if rv_ops:
                        dep_rv.append(rv_ops[-1])
                    elif hf == 0:
                        dep_rv.append(x1_done)
                    rmm = op("tensor", f_rmm_v, deps=dep_rv,
                             desc=f"v mm {hf}.{m}.{ntv}")
                    rmms.append(rmm)
                    last_rmm = rmm

                    def f_sig(s, _rb=rbank):
                        return s.activation(rs_t[:], PS[_rb][:, :], AF.Sigmoid)
                    sig = op("scalar", f_sig,
                             deps=[rmm_r, rv_ops[-1] if rv_ops else None],
                             desc=f"sig {hf}.{m}.{ntv}")
                    sig_ops.append(sig)

                    def f_x2(v, _m=m, _hf=hf, _n=ntv, _vb=vbank):
                        xsl = slice(_m * S + _hf * 1024 + _n * 512,
                                    _m * S + _hf * 1024 + _n * 512 + 512)
                        v.tensor_mul(ff_t[:], rs_t[:], PS[_vb][:, :])
                        return v.tensor_add(x1_t[:, xsl], ff_t[:],
                                            x1_t[:, xsl])
                    x2o = op("vector", f_x2, deps=[sig, rmm, stash],
                             desc=f"x2 {hf}.{m}.{ntv}")
                    rv_ops.append(x2o)
                    x2_written.append(x2o)
                    if hf == 1 and len(rv_ops) % 2 == 0:
                        i6 = len(rv_ops) // 2 - 1
                        emit_p6(i6, extra_deps=(
                            [x2_hf0_last, kar_last_hf1] if i6 == 0 else []))
            prev_half["x2"] = rv_ops[-1]
            prev_half["last_rmm"] = last_rmm
            if hf == 0:
                x2_hf0_last = rv_ops[-1]
        x2_done = x2_written[-1]

        if tap == "x2":
            do_tap([x2_done], [(x1_t, [128, 8 * S], F32)])
            P.emit(blk, sems)
            return nc

        for i in range(8, 16):
            emit_p6(i, extra_deps=([x2_done] if i == 8 else []))

        def f_fin(q):
            return None
        op("sync", f_fin, deps=[y_stores[-1], y_stores[-2]], desc="fin")

        P.emit(blk, sems)
    return nc


def _host_constants():
    ROPE_BASE = 10000.0
    DR = 64
    inv_freq = 1.0 / ROPE_BASE ** (np.arange(0, DR, 2, dtype=np.float32) / DR)
    t = np.arange(S, dtype=np.float32)
    freqs = np.outer(t, inv_freq)
    cos = np.concatenate([np.cos(freqs), np.cos(freqs)], -1)[:, ::2]
    sin = np.concatenate([np.sin(freqs), np.sin(freqs)], -1)[:, ::2]
    p = np.arange(128)
    tc = cos[:, p % 32].T.astype(np.float32)
    sgn = np.where((p % 64) < 32, -1.0, 1.0).astype(np.float32)
    ts = (sin[:, p % 32] * sgn[None, :]).T.astype(np.float32)
    id32 = np.eye(128, dtype=np.float32)
    perm = np.zeros((128, 128), dtype=np.float32)
    for a in range(128):
        perm[a ^ 32, a] = 1.0
    e1 = np.zeros((128, 128), dtype=np.float32)
    e1[:, 0] = 1.0
    onesr = np.ones((1, 128), dtype=np.float32)
    return np.ascontiguousarray(tc), np.ascontiguousarray(ts), id32, perm, \
        e1, onesr


_DEINT = None


def _deint_cols():
    global _DEINT
    if _DEINT is None:
        idx = []
        for h in range(H):
            idx += [h * 64 + 2 * i for i in range(32)]
            idx += [h * 64 + 2 * i + 1 for i in range(32)]
        _DEINT = np.array(idx)
    return _DEINT


def kernel(L0_attn_x, L0_attn_w_down_q, L0_attn_w_down_kv, L0_attn_w_up_q,
           L0_attn_w_up_k, L0_attn_w_up_v, L0_attn_w_out, L0_attn_ln_scale,
           L0_attn_ln_bias, L0_ffn_W_r, L0_ffn_W_k, L0_ffn_W_v, L0_ffn_mu_r,
           L0_ffn_mu_k):
    x = np.ascontiguousarray(np.asarray(L0_attn_x, dtype=np.float32))
    tc, ts, id32, perm, e1, onesr = _host_constants()
    di = _deint_cols()
    c = np.ascontiguousarray
    f32 = np.float32
    wdqT = c(np.asarray(L0_attn_w_down_q, f32).T)
    wdkvT = c(np.asarray(L0_attn_w_down_kv, f32).T)
    wuqT = c(np.asarray(L0_attn_w_up_q, f32).T[:, di])
    wukT = c(np.asarray(L0_attn_w_up_k, f32).T[:, di])
    wuvT = c(np.asarray(L0_attn_w_up_v, f32).T)
    woutT = c(np.asarray(L0_attn_w_out, f32).T)
    wrT = c(np.asarray(L0_ffn_W_r, f32).T)
    wkT = c(np.asarray(L0_ffn_W_k, f32).T)
    wvT = c(np.asarray(L0_ffn_W_v, f32).T)
    gam = c(np.asarray(L0_attn_ln_scale, f32).reshape(D, 1))
    bet = c(np.asarray(L0_attn_ln_bias, f32).reshape(D, 1))
    mur = c(np.asarray(L0_ffn_mu_r, f32).reshape(D, 1))
    muk = c(np.asarray(L0_ffn_mu_k, f32).reshape(D, 1))

    nc = build_program()
    shared = dict(wdqT=wdqT, wdkvT=wdkvT, wuqT=wuqT, wukT=wukT, wuvT=wuvT,
                  woutT=woutT, wrT=wrT, wkT=wkT, wvT=wvT, gamma=gam, beta=bet,
                  mur=mur, muk=muk, tc=tc, tsn=ts, id32=id32, perm32=perm,
                  e1=e1, onesr=onesr,
                  epsv=np.full((128, 1), EPS, np.float32))
    in_maps = [dict(shared, x=c(x[b])) for b in range(8)]
    res = run_bass_kernel_spmd(nc, in_maps, core_ids=list(range(8)))
    out = np.stack([np.asarray(res.results[b]["y"], dtype=np.float32)
                    for b in range(8)], axis=0)
    return out
